# version 10
# baseline (speedup 1.0000x reference)
import numpy as np
import ml_dtypes
from contextlib import ExitStack

import concourse.bass as bass
import concourse.mybir as mybir
from concourse.bass_utils import run_bass_kernel_spmd

F32 = mybir.dt.float32
BF16 = mybir.dt.bfloat16
AF = mybir.ActivationFunctionType
ALU = mybir.AluOpType
AX = mybir.AxisListType
BF = ml_dtypes.bfloat16

D = 1024
KT = 8
T = 4096
NCORES = 8


class _H:
    def __init__(self, rec):
        self._rec = rec

    def then_inc(self, sem, v=1):
        self._rec[3].append((sem, v))
        return self


class EQ:
    def __init__(self, name):
        self._name = name
        self._ops = []

    def __getattr__(self, attr):
        if attr.startswith("_"):
            raise AttributeError(attr)
        def call(*a, **kw):
            rec = [attr, a, kw, []]
            self._ops.append(rec)
            return _H(rec)
        return call

    def _replay(self, eng):
        for name, a, kw, incs in self._ops:
            ins = getattr(eng, name)(*a, **kw)
            for sem, v in incs:
                ins = ins.then_inc(sem, v)


class Sem:
    def __init__(self, handle):
        self.h = handle
        self.n = 0

    def bump(self, k=1):
        self.n += k
        return self.n


def tinc(handle, sem, k=1):
    handle.then_inc(sem.h, k)
    return sem.bump(k)


def chain(q, sem, handle):
    n = tinc(handle, sem)
    q.wait_ge(sem.h, n)
    return n


class Prog:
    def __init__(self, nc, ctx):
        self.nc = nc
        self.ctx = ctx
        self._root_ctx = ctx
        self._semmap = {}
        self._names = {}
        self.prefix = ""
        self._base_depth = 0
        self.sync = EQ("sync")
        self.tensor = EQ("tensor")
        self.scalar = EQ("scalar")
        self.vector = EQ("vector")
        self.gpsimd = EQ("gpsimd")

    def _uname(self, name):
        name = self.prefix + name
        cnt = self._names.get(name, 0)
        self._names[name] = cnt + 1
        return name if cnt == 0 else f"{name}__{cnt}"

    def sb(self, name, shape, dt=F32):
        if getattr(self, "_sb_left", False):
            side = "left"
        else:
            side = "right" if getattr(self, "_depth", 0) > 0 else "left"
        return self.ctx.enter_context(
            self.nc.sbuf_tensor(self._uname(name), shape, dt, side=side))

    def ps(self, name, shape, dt=F32):
        side = "right" if getattr(self, "_depth", 0) > 0 else "left"
        return self.ctx.enter_context(
            self.nc.psum_tensor(self._uname(name), shape, dt, side=side))

    def sem(self, name):
        name = self.prefix + name
        if name not in self._semmap:
            self._semmap[name] = Sem(
                self._root_ctx.enter_context(self.nc.semaphore(name)))
        return self._semmap[name]

    def push_scope(self):
        self._scope_stack = getattr(self, "_scope_stack", [])
        self._scope_stack.append(self.ctx)
        self._depth = getattr(self, "_depth", 0) + 1
        inner = ExitStack()
        self.ctx = inner
        return inner

    def pop_scope(self):
        self.ctx.close()
        self.ctx = self._scope_stack.pop()
        self._depth -= 1

    from contextlib import contextmanager as _cm

    @_cm
    def scope(self):
        old = self.ctx
        inner = ExitStack()
        self.ctx = inner
        self._depth = getattr(self, "_depth", 0) + 1
        try:
            with inner:
                yield
        finally:
            self.ctx = old
            self._depth -= 1

    def finalize(self):
        block = self.ctx.enter_context(self.nc.Block())
        for attr, q in [("sync", self.sync), ("tensor", self.tensor),
                        ("scalar", self.scalar), ("vector", self.vector),
                        ("gpsimd", self.gpsimd)]:
            if q._ops:
                getattr(block, attr)(lambda eng, q=q: q._replay(eng))


def emit_ln_stats(p, sems, x_k, sq_k, consts, psA, psB, psC, psD, n_cols, st):
    t, s, v = p.tensor, p.scalar, p.vector
    pe, act, dve = sems["pe"], sems["act"], sems["dve"]

    def _same_ap(a, b):
        return a.tensor.name == b.tensor.name and a.offset == b.offset
    interleave = all(_same_ap(sq_k(k), sq_k(0)) for k in range(1, KT))
    if not interleave:
        for k in range(KT):
            h = s.activation(sq_k(k), x_k(k), AF.Square)
        n_act = tinc(h, act)
    for k in range(KT):
        t.matmul(psA[0:1, 0:n_cols], consts["ones_col"][:], x_k(k),
                 start=(k == 0), stop=(k == KT - 1))
    if interleave:
        n_pe0 = pe.n
        for k in range(KT):
            if k > 0:
                s.wait_ge(pe.h, n_pe0)
            h = s.activation(sq_k(0), x_k(k), AF.Square)
            n_act = tinc(h, act)
            t.wait_ge(act.h, n_act)
            h = t.matmul(psB[0:1, 0:n_cols], consts["ones_col"][:], sq_k(0),
                         start=(k == 0), stop=(k == KT - 1))
            n_pe0 = tinc(h, pe)
        n_pe = n_pe0
    else:
        t.wait_ge(act.h, n_act)
        for k in range(KT):
            h = t.matmul(psB[0:1, 0:n_cols], consts["ones_col"][:], sq_k(k),
                         start=(k == 0), stop=(k == KT - 1))
        n_pe = tinc(h, pe)

    ex = st["ex"][0:1, 0:n_cols]
    ex2 = st["ex2"][0:1, 0:n_cols]
    rstd = st["rstd"][0:1, 0:n_cols]
    brow = st["brow"][0:1, 0:n_cols]
    v.wait_ge(pe.h, n_pe)
    v.tensor_scalar(ex, psA[0:1, 0:n_cols], 1.0 / D, None, ALU.mult)
    h = v.tensor_scalar(ex2, psB[0:1, 0:n_cols], 1.0 / D, None, ALU.mult)
    chain(v, dve, h)
    h = v.tensor_mul(brow, ex, ex)
    chain(v, dve, h)
    h = v.tensor_sub(ex2, ex2, brow)
    n_dve = tinc(h, dve)
    s.wait_ge(dve.h, n_dve)
    h = s.activation(rstd, ex2, AF.Ln, bias=consts["eps1"][0:1, 0:1])
    chain(s, act, h)
    h = s.activation(rstd, rstd, AF.Exp, scale=-0.5)
    n_act = tinc(h, act)
    v.wait_ge(act.h, n_act)
    h = v.scalar_tensor_tensor(brow, ex, -1.0, rstd, ALU.mult, ALU.mult)
    n_dve = tinc(h, dve)

    t.wait_ge(dve.h, n_dve)
    t.matmul(psC[:, 0:n_cols], consts["ones_row"][:], rstd, start=True, stop=True)
    h = t.matmul(psD[:, 0:n_cols], consts["ones_row"][:], brow, start=True, stop=True)
    return tinc(h, pe)


def emit_ln(p, sems, x_k, sq_k, h_k, consts, psA, psB, psC, psD, n_cols, st):
    t, s, v = p.tensor, p.scalar, p.vector
    pe, act, dve = sems["pe"], sems["act"], sems["dve"]
    n_pe = emit_ln_stats(p, sems, x_k, sq_k, consts, psA, psB, psC, psD,
                         n_cols, st)
    v.wait_ge(pe.h, n_pe)
    for k in range(KT):
        h = v.tensor_mul(sq_k(k), x_k(k), psC[:, 0:n_cols])
    chain(v, dve, h)
    for k in range(KT):
        h = v.tensor_add(h_k(k), sq_k(k), psD[:, 0:n_cols])
    return tinc(h, dve)


def build_ffn0(Tfull=T, Tc=1024, fuse=None):
    E = 8
    NBLK = 16
    NDM = 8
    HK = 16
    n_chunks = Tfull // Tc
    NS = Tc // 512

    nc = fuse["nc"] if fuse else bass.Bass("TRN2", target_bir_lowering=False)
    xin = _fdt(nc, fuse, "xin", [D, Tfull], F32, "ExternalInput")
    w12p = _fdt(nc, fuse, "w12p", [E, NBLK, D, 256], F32, "ExternalInput")
    w3p = _fdt(nc, fuse, "w3p", [E, NDM, 2 * D, 128], F32, "ExternalInput")
    gwT = _fdt(nc, fuse, "gwT", [D, E], F32, "ExternalInput")
    idm = _fdt(nc, fuse, "idm", [128, 128], F32, "ExternalInput")
    ind8 = _fdt(nc, fuse, "ind8", [E, E * 128], F32, "ExternalInput")
    xout = _fdt(nc, fuse, "xout", [D, Tfull], F32, "ExternalOutput")

    with _prog_ctx(nc, fuse) as p:
        t, s, v, g, sy = p.tensor, p.scalar, p.vector, p.gpsimd, p.sync

        xc = p.sb("xc", [128, KT * Tc], F32)
        sq = p.sb("sq", [128, KT * Tc], F32)
        hc = p.sb("hc", [128, KT * Tc], BF16)
        hid = p.sb("hid", [128, NBLK * NS * 512], BF16)
        silu_t = p.sb("silu_t", [128, 2 * 512], BF16)
        t2_t = p.sb("t2_t", [128, 2 * 512], BF16)
        pbc = p.sb("pbc", [128, E * Tc], BF16)
        w12sb = p.sb("w12sb", [128, 2 * KT * 256], BF16)
        w3sb = p.sb("w3sb", [128, 2 * HK * 128], BF16)
        gwsb = p.sb("gwsb", [128, KT * E], BF16)
        id32 = p.sb("id32", [128, 128], F32)
        ind8sb = p.sb("ind8sb", [E, E * 128], F32)
        ntt = Tc // 128
        sc_tok = p.sb("sc_tok", [128, ntt * E], F32)
        sorted8 = p.sb("sorted8", [128, E], F32)
        negs1 = p.sb("negs1", [128, 1], F32)
        mask8 = p.sb("mask8", [128, E], F32)
        eall = p.sb("eall", [128, E], F32)
        zz = p.sb("zz", [128, E], F32)
        den = p.sb("den", [128, 1], F32)
        rec = p.sb("rec", [128, 1], F32)
        ptok = p.sb("ptok", [128, ntt * E], F32)
        pT = p.sb("pT", [E, Tc], F32)
        st = {n: p.sb("st_" + n, [1, 512], F32)
              for n in ("ex", "ex2", "rstd", "brow")}
        ones_col = p.sb("ones_col", [128, 1], F32)
        ones_row = p.sb("ones_row", [1, 128], F32)
        eps1 = p.sb("eps1", [1, 1], F32)
        consts = {"ones_col": ones_col, "ones_row": ones_row, "eps1": eps1}

        PS = [p.ps(f"P{i}", [128, 512], F32) for i in range(8)]

        ld = p.sem("ld")
        wldp = {("w12", 0): p.sem("wld120"), ("w12", 1): p.sem("wld121"),
                ("w3", 0): p.sem("wld30"), ("w3", 1): p.sem("wld31")}
        wld_base = {kk: ss.n for kk, ss in wldp.items()}
        pe = p.sem("pe")
        act = p.sem("act")
        dve = p.sem("dve")
        stx = p.sem("stx")
        sems = {"pe": pe, "act": act, "dve": dve}

        v.memset(ones_col[:], 1.0)
        v.memset(ones_row[:], 1.0)
        h = v.memset(eps1[:], 1e-5)
        n_const = tinc(h, dve)

        sy.dma_start(out=id32[:], in_=idm[:]).then_inc(ld.h, 16)
        ld.bump(16)
        h = sy.dma_start(out=ind8sb[:], in_=ind8[:])
        n_ld0 = tinc(h, ld, 16)
        gld = p.sem("gld")
        h = g.dma_start(out=gwsb[:].rearrange("p (k e) -> p k e", k=KT),
                        in_=gwT[:].rearrange("(k p) e -> p k e", p=128))
        n_gld = tinc(h, gld, 16)

        w12v = lambda par: w12sb[:, par * KT * 256:(par + 1) * KT * 256]
        w3v = lambda par: w3sb[:, par * HK * 128:(par + 1) * HK * 128]
        dma_seq = []
        for e in range(E):
            dma_seq += [("w12", e, m) for m in range(NBLK)]
            dma_seq += [("w3", e, dm) for dm in range(NDM)]
        gseq = [(kind, c, e, blk) for c in range(n_chunks)
                for (kind, e, blk) in dma_seq]
        kind_hist = {"w12": [], "w3": []}
        for i, (kind, c, e, blk) in enumerate(gseq):
            kind_hist[kind].append(i)
        block_pe = []

        xsl = lambda k, n: xc[:, k * Tc + n * 512: k * Tc + n * 512 + 512]
        sqsl = lambda k, n: sq[:, k * Tc + n * 512: k * Tc + n * 512 + 512]
        hsl = lambda k, n: hc[:, k * Tc + n * 512: k * Tc + n * 512 + 512]

        gblk = 0
        per_kind = {"w12": 0, "w3": 0}
        for eng in (t, s, v):
            eng.wait_ge(dve.h, n_const)
        t.wait_ge(gld.h, n_gld)

        for c in range(n_chunks):
            if c > 0:
                sy.wait_ge(stx.h, stx.n)
            h = sy.dma_start(out=xc[:].rearrange("p (k tc) -> p k tc", k=KT),
                             in_=xin[:, c * Tc:(c + 1) * Tc]
                             .rearrange("(k p) tc -> p k tc", p=128))
            n_x = tinc(h, ld, 16)
            for eng in (s, t, v):
                eng.wait_ge(ld.h, n_x)

            for n in range(NS):
                n_h = emit_ln(p, sems, lambda k, n=n: xsl(k, n),
                              lambda k, n=n: sqsl(k, n),
                              lambda k, n=n: hsl(k, n), consts,
                              PS[0], PS[1], PS[2], PS[3], 512, st)

            t.wait_ge(dve.h, n_h)
            for tt in range(ntt):
                for k in range(KT):
                    h = t.matmul(PS[4][:, 0:E],
                                 hc[:, k * Tc + tt * 128: k * Tc + tt * 128 + 128],
                                 gwsb[:, k * E:(k + 1) * E],
                                 start=(k == 0), stop=(k == KT - 1))
                n_pe = tinc(h, pe)
                v.wait_ge(pe.h, n_pe)
                h = v.tensor_copy(sc_tok[:, tt * E:(tt + 1) * E], PS[4][:, 0:E])
                n_dve = tinc(h, dve)
                t.wait_ge(dve.h, n_dve)
            for tt in range(ntt):
                sco = sc_tok[:, tt * E:(tt + 1) * E]
                h = v.max(sorted8[:], sco)
                chain(v, dve, h)
                h = v.tensor_scalar(negs1[:], sorted8[:, 0:1], -1.0, None, ALU.mult)
                n_dve = tinc(h, dve)
                s.wait_ge(dve.h, n_dve)
                h = s.activation(eall[:], sco, AF.Exp, bias=negs1[:])
                n_act = tinc(h, act)
                h = v.tensor_scalar(mask8[:], sco, sorted8[:, 1:2], None, ALU.is_ge)
                chain(v, dve, h)
                v.wait_ge(act.h, n_act)
                h = v.tensor_mul(zz[:], eall[:], mask8[:])
                chain(v, dve, h)
                h = v.tensor_reduce(den[:], zz[:], AX.X, ALU.add)
                chain(v, dve, h)
                h = v.reciprocal(rec[:], den[:])
                chain(v, dve, h)
                h = v.tensor_scalar(ptok[:, tt * E:(tt + 1) * E], zz[:], rec[:],
                                    None, ALU.mult)
                n_dve = tinc(h, dve)
            t.wait_ge(dve.h, n_dve)
            for tt in range(ntt):
                h = t.transpose(PS[5][0:E, 0:128],
                                ptok[:, tt * E:(tt + 1) * E], id32[:])
                n_pe = tinc(h, pe)
                v.wait_ge(pe.h, n_pe)
                h = v.tensor_copy(pT[:, tt * 128:(tt + 1) * 128], PS[5][0:E, 0:128])
                n_dve = tinc(h, dve)
                t.wait_ge(dve.h, n_dve)
            for e in range(E):
                for n in range(NS):
                    h = t.matmul(PS[6][:, 0:512],
                                 ind8sb[:, e * 128:(e + 1) * 128],
                                 pT[0:E, n * 512:(n + 1) * 512],
                                 start=True, stop=True)
                    n_pe = tinc(h, pe)
                    s.wait_ge(pe.h, n_pe)
                    h = s.activation(pbc[:, e * Tc + n * 512: e * Tc + n * 512 + 512],
                                     PS[6][:, 0:512], AF.Copy)
                    n_actb = tinc(h, act)
                    t.wait_ge(act.h, n_actb)

            hid_hist = {}
            py_hist = {}
            for e in range(E):
                for m in range(NBLK):
                    kcnt = per_kind["w12"]
                    par = kcnt % 2
                    t.wait_ge(wldp[("w12", par)].h,
                              wld_base[("w12", par)] + 16 * (kcnt // 2 + 1))
                    for n in range(NS):
                        mp = (m * NS + n) % 2
                        pha = PS[0 + mp * 2]
                        phb = PS[1 + mp * 2]
                        prev = hid_hist.get(mp)
                        if prev is not None:
                            t.wait_ge(act.h, prev[0])
                            t.wait_ge(dve.h, prev[1])
                        for k in range(KT):
                            t.matmul(pha[:, 0:512],
                                     w12v(par)[:, k * 256:k * 256 + 128],
                                     hsl(k, n), start=(k == 0), stop=(k == KT - 1))
                        for k in range(KT):
                            h = t.matmul(phb[:, 0:512],
                                         w12v(par)[:, k * 256 + 128:k * 256 + 256],
                                         hsl(k, n), start=(k == 0), stop=(k == KT - 1))
                        last_mm = h
                        n_pe = tinc(h, pe)
                        sl = silu_t[:, mp * 512:mp * 512 + 512]
                        t2 = t2_t[:, mp * 512:mp * 512 + 512]
                        s.wait_ge(pe.h, n_pe)
                        if prev is not None:
                            s.wait_ge(dve.h, prev[1])
                        h = s.activation(sl, pha[:, 0:512], AF.Silu)
                        n_act = tinc(h, act)
                        v.wait_ge(pe.h, n_pe)
                        h = v.tensor_mul(t2, phb[:, 0:512],
                                         pbc[:, e * Tc + n * 512:e * Tc + n * 512 + 512])
                        chain(v, dve, h)
                        v.wait_ge(act.h, n_act)
                        h = v.tensor_mul(hid[:, (m * NS + n) * 512:(m * NS + n) * 512 + 512],
                                         sl, t2)
                        n_dve = tinc(h, dve)
                        hid_hist[mp] = (n_act, n_dve)
                    gblk += 1
                    block_pe.append(pe.n)
                    per_kind["w12"] += 1
                t.wait_ge(dve.h, max(d for (_, d) in hid_hist.values()))
                for dm in range(NDM):
                    kcnt = per_kind["w3"]
                    par = kcnt % 2
                    t.wait_ge(wldp[("w3", par)].h,
                              wld_base[("w3", par)] + 16 * (kcnt // 2 + 1))
                    for n in range(NS):
                        dp = (dm * NS + n) % 2
                        py = PS[4 + dp]
                        prev = py_hist.get(dp)
                        if prev is not None:
                            t.wait_ge(dve.h, prev)
                        for hk in range(HK):
                            h = t.matmul(py[:, 0:512],
                                         w3v(par)[:, hk * 128:hk * 128 + 128],
                                         hid[:, (hk * NS + n) * 512:(hk * NS + n) * 512 + 512],
                                         start=(hk == 0), stop=(hk == HK - 1))
                        last_mm3 = h
                        n_pe = tinc(h, pe)
                        v.wait_ge(pe.h, n_pe)
                        h = v.scalar_tensor_tensor(xsl(dm, n), py[:, 0:512], 0.5,
                                                   xsl(dm, n), ALU.mult, ALU.add)
                        n_dve = tinc(h, dve)
                        py_hist[dp] = n_dve
                    gblk += 1
                    block_pe.append(pe.n)
                    per_kind["w3"] += 1

            t.wait_ge(dve.h, dve.n)
            t.wait_ge(act.h, act.n)
            sy.wait_ge(dve.h, dve.n)
            h = sy.dma_start(out=xout[:, c * Tc:(c + 1) * Tc]
                             .rearrange("(k p) tc -> p k tc", p=128),
                             in_=xc[:].rearrange("p (k tc) -> p k tc", k=KT))
            tinc(h, stx, 16)

        per_kind = {"w12": 0, "w3": 0}
        for i, (kind, c, e, blk) in enumerate(gseq):
            kcnt = per_kind[kind]
            par = kcnt % 2
            if kcnt >= 2:
                g.wait_ge(pe.h, block_pe[kind_hist[kind][kcnt - 2]])
            if kind == "w12":
                hh = g.dma_start(
                    out=w12v(par).rearrange("p (k c2) -> p k c2", k=KT),
                    in_=w12p[e, blk].rearrange("(k p) c2 -> p k c2", p=128))
            else:
                hh = g.dma_start(
                    out=w3v(par).rearrange("p (k c2) -> p k c2", k=HK),
                    in_=w3p[e, blk].rearrange("(k p) c2 -> p k c2", p=128))
            tinc(hh, wldp[(kind, par)], 16)
            per_kind[kind] += 1
        assert len(block_pe) == len(gseq)
        g.wait_ge(stx.h, stx.n)
    return nc


def _prep_ffn0_weights(w12, w3, gate_w):
    E = 8
    w12p = np.empty((E, 16, D, 256), dtype=np.float32)
    for e in range(E):
        w12T = np.ascontiguousarray(w12[e].T)
        for m in range(16):
            w12p[e, m, :, 0:128] = w12T[:, m * 128:(m + 1) * 128]
            w12p[e, m, :, 128:256] = w12T[:, (m + 16) * 128:(m + 17) * 128]
    w3p = np.empty((E, 8, 2 * D, 128), dtype=np.float32)
    for e in range(E):
        w3T = np.ascontiguousarray(w3[e].T)
        for dm in range(8):
            w3p[e, dm] = w3T[:, dm * 128:(dm + 1) * 128]
    gwT = np.ascontiguousarray(gate_w.T)
    ind8 = np.zeros((E, E * 128), dtype=np.float32)
    for e in range(E):
        ind8[e, e * 128:(e + 1) * 128] = 1.0
    return w12p, w3p, gwT, ind8




def emit_barrier(p, all_sems):
    for q in (p.tensor, p.scalar, p.vector, p.gpsimd, p.sync):
        for s_ in all_sems:
            if s_.n > 0:
                q.wait_ge(s_.h, s_.n)


def _fdt(nc, fuse, name, shape, dt_, kind):
    if fuse is None:
        return nc.dram_tensor(name, shape, dt_, kind=kind).ap()
    io = fuse.get("io", {})
    if name in io:
        return io[name]
    cache = fuse["wcache"]
    if name not in cache:
        cache[name] = nc.dram_tensor(fuse["pref"] + name, shape, dt_,
                                     kind=kind).ap()
    return cache[name]


from contextlib import contextmanager as _cm2


@_cm2
def _prog_ctx(nc, fuse):
    if fuse is None:
        with ExitStack() as ctx:
            p = Prog(nc, ctx)
            yield p
            p.finalize()
    else:
        p = fuse["p"]
        old_pref, old_base = p.prefix, p._base_depth
        p.prefix = fuse["pref"]
        with p.scope():
            p._base_depth = p._depth
            yield p
            emit_barrier(p, list(p._semmap.values()))
        p.prefix, p._base_depth = old_pref, old_base


def build_attn0(Tfull=T, Tc=512, fuse=None):
    NH = 16
    HD = 64
    n_chunks = Tfull // Tc
    NKT = Tfull // 128
    NQC = Tfull // 512
    ntt = Tc // 128

    nc = fuse["nc"] if fuse else bass.Bass("TRN2", target_bir_lowering=False)
    xin = _fdt(nc, fuse, "xin", [D, Tfull], F32, "ExternalInput")
    wqT = _fdt(nc, fuse, "wqT", [D, D], F32, "ExternalInput")
    wkT = _fdt(nc, fuse, "wkT", [D, D], F32, "ExternalInput")
    wvT = _fdt(nc, fuse, "wvT", [D, D], F32, "ExternalInput")
    woT = _fdt(nc, fuse, "woT", [D, D], F32, "ExternalInput")
    bq = _fdt(nc, fuse, "bq", [128, KT], F32, "ExternalInput")
    bk = _fdt(nc, fuse, "bk", [128, KT], F32, "ExternalInput")
    bv = _fdt(nc, fuse, "bv", [128, KT], F32, "ExternalInput")
    bo = _fdt(nc, fuse, "bo", [128, KT], F32, "ExternalInput")
    nqw = _fdt(nc, fuse, "nqw", [128, KT], F32, "ExternalInput")
    nkw = _fdt(nc, fuse, "nkw", [128, KT], F32, "ExternalInput")
    idm = _fdt(nc, fuse, "idm", [128, 128], F32, "ExternalInput")
    idm16 = _fdt(nc, fuse, "idm16", [128, 128], BF16, "ExternalInput")
    hind = _fdt(nc, fuse, "hind", [128, KT * NH], F32, "ExternalInput")
    qh = _fdt(nc, fuse, "qh_sc", [D, Tfull], BF16, "Internal")
    kh = _fdt(nc, fuse, "kh_sc", [D, Tfull], BF16, "Internal")
    vtok = _fdt(nc, fuse, "vtok_sc", [Tfull, D], BF16, "Internal")
    ctxh = _fdt(nc, fuse, "ctxh_sc", [D, Tfull], BF16, "Internal")
    xout = _fdt(nc, fuse, "xout", [D, Tfull], F32, "ExternalOutput")

    with _prog_ctx(nc, fuse) as p:
        t, s, v, g, sy = p.tensor, p.scalar, p.vector, p.gpsimd, p.sync

        xc2g = [p.sb("xcA", [128, KT * Tc], F32),
                p.sb("xcB", [128, KT * Tc], F32)]
        xc = xc2g[0]
        qhat = p.sb("qhat", [128, KT * Tc], BF16)
        bsb = {nm: p.sb("b_" + nm, [128, KT], F32)
               for nm in ("q", "k", "v", "o")}
        nwsb = {nm: p.sb("nw_" + nm, [128, KT], F32) for nm in ("q", "k")}
        id32 = p.sb("id32", [128, 128], F32)
        id16 = p.sb("id16", [128, 128], BF16)
        hindsb = p.sb("hindsb", [128, KT * NH], F32)
        vmax = {nm: p.sb("vmax_" + nm, [16, 1], F32) for nm in ("q", "k")}
        cmax = p.sb("cmax", [16, 1], F32)
        bprod = p.sb("bprod", [16, 1], F32)
        negb = p.sb("negb", [16, 1], F32)
        negbw = p.sb("negbw", [16, 128], F32)
        negbc = p.sb("negbc", [128, 16], F32)
        ln0125 = p.sb("ln0125", [16, 1], F32)
        st = {n: p.sb("st_" + n, [1, 512], F32)
              for n in ("ex", "ex2", "rstd", "brow")}
        rstdq = p.sb("rstdq", [1, 512], F32)
        ones_col = p.sb("ones_col", [128, 1], F32)
        ones_row = p.sb("ones_row", [1, 128], F32)
        onesf = p.sb("onesf", [128, 64], F32)
        eps1 = p.sb("eps1", [1, 1], F32)
        eps6 = p.sb("eps6", [1, 1], F32)
        consts = {"ones_col": ones_col, "ones_row": ones_row, "eps1": eps1}

        PS2 = p.ps("PS2", [128, 512], F32)

        ld = p.sem("ld")
        gld = p.sem("gld")
        pe = p.sem("pe")
        act = p.sem("act")
        dve = p.sem("dve")
        stq = p.sem("stq")
        stx = p.sem("stx")
        stc = [p.sem("stc0"), p.sem("stc1")]
        sems = {"pe": pe, "act": act, "dve": dve}
        allsems = [ld, gld, pe, act, dve, stq, stx, stc[0], stc[1]]

        v.memset(ones_col[:], 1.0)
        v.memset(ones_row[:], 1.0)
        v.memset(onesf[:], 1.0)
        v.memset(eps1[:], 1e-5)
        v.memset(eps6[:], 1e-6)
        v.memset(vmax["q"][:], -1e30)
        v.memset(vmax["k"][:], -1e30)
        h = v.memset(ln0125[:], -2.0794415416798357)
        n_const = tinc(h, dve)

        for ap_, sbuf in [(idm, id32), (idm16, id16), (hind, hindsb),
                          (bq, bsb["q"]), (bk, bsb["k"]), (bv, bsb["v"]),
                          (bo, bsb["o"]), (nqw, nwsb["q"]), (nkw, nwsb["k"])]:
            h = sy.dma_start(out=sbuf[:], in_=ap_[:])
            n_ld0 = tinc(h, ld, 16)

        xsl = lambda k: xc[:, k * Tc:(k + 1) * Tc]

        with p.scope():
            PSW = [p.ps(f"PW{i}", [128, 1024], F32) for i in range(2)]
            ROT4 = [PSW[0][:, 0:512], PSW[0][:, 512:1024],
                    PSW[1][:, 0:512], PSW[1][:, 512:1024]]
            PB = p.ps("PB", [128, 1024], BF16)
            PB2 = p.ps("PB2", [128, 1024], BF16)
            sq = p.sb("sq", [128, KT * Tc], F32)
            qraw = sq
            hc = p.sb("hc", [128, KT * Tc], BF16)
            vsb = p.sb("vsb", [128, KT * Tc], BF16)
            vtsb = p.sb("vtsb", [128, ntt * D], BF16)
            qsq = p.sb("qsq", [128, 2 * Tc], F32)
            wsb = {nm: p.sb("w_" + nm, [128, KT * D], BF16)
                   for nm in ("q", "k", "v")}
            for nm, wt in [("q", wqT), ("k", wkT), ("v", wvT)]:
                h = g.dma_start(out=wsb[nm][:].rearrange("p (k m) -> p k m", k=KT),
                                in_=wt[:].rearrange("(k p) m -> p k m", p=128))
                n_gld = tinc(h, gld, 16)
            for eng in (t, s, v):
                eng.wait_ge(gld.h, n_gld)
                eng.wait_ge(dve.h, n_const)

            sqsl = lambda k: sq[:, k * Tc:(k + 1) * Tc]
            hsl = lambda k: hc[:, k * Tc:(k + 1) * Tc]

            nh_hist = {}

            def issue_x0(c2):
                h2 = sy.dma_start(
                    out=xc2g[c2 % 2][:].rearrange("p (k tc) -> p k tc", k=KT),
                    in_=xin[:, c2 * Tc:(c2 + 1) * Tc]
                    .rearrange("(k p) tc -> p k tc", p=128))
                return tinc(h2, ld, 16)

            xn = {0: issue_x0(0)}
            for c in range(n_chunks):
                xc = xc2g[c % 2]
                xsl = lambda k, xc=xc: xc[:, k * Tc:(k + 1) * Tc]
                if c + 1 < n_chunks:
                    if c >= 1:
                        sy.wait_ge(dve.h, nh_hist[c - 1])
                    xn[c + 1] = issue_x0(c + 1)
                n_x = xn[c]
                for eng in (s, t, v):
                    eng.wait_ge(ld.h, n_x)
                if c > 0:
                    s.wait_ge(dve.h, dve.n)
                n_h = emit_ln(p, sems, xsl, sqsl, hsl, consts,
                              ROT4[0], ROT4[1], ROT4[2], ROT4[3], Tc, st)
                nh_hist[c] = n_h
                t.wait_ge(dve.h, n_h)

                for nm in ("q", "k", "v"):
                    qk = nm != "v"
                    ev_hist = []
                    for m in range(KT):
                        pp = ROT4[m % 4]
                        if m >= 4:
                            t.wait_ge(act.h, ev_hist[m - 4])
                        for k in range(KT):
                            h = t.matmul(pp[:, 0:Tc],
                                         wsb[nm][:, k * D + m * 128:k * D + m * 128 + 128],
                                         hsl(k), start=(k == 0), stop=(k == KT - 1))
                        n_pe = tinc(h, pe)
                        s.wait_ge(pe.h, n_pe)
                        dst = qraw if qk else vsb
                        h = s.activation(dst[:, m * Tc:(m + 1) * Tc], pp[:, 0:Tc],
                                         AF.Identity, bias=bsb[nm][:, m:m + 1])
                        n_act = tinc(h, act)
                        ev_hist.append(n_act)

                    if qk:
                        sq_pe = []
                        for m in range(KT):
                            qs = qsq[:, (m % 2) * Tc:(m % 2) * Tc + Tc]
                            v.wait_ge(act.h, ev_hist[m])
                            if m >= 2:
                                v.wait_ge(pe.h, sq_pe[m - 2])
                            h = v.tensor_mul(qs, qraw[:, m * Tc:(m + 1) * Tc],
                                             qraw[:, m * Tc:(m + 1) * Tc])
                            n_dve = tinc(h, dve)
                            t.wait_ge(dve.h, n_dve)
                            h = t.matmul(PS2[0:1, 0:Tc], ones_col[:], qs,
                                         start=(m == 0), stop=(m == KT - 1))
                            sq_pe.append(tinc(h, pe))
                        s.wait_ge(pe.h, sq_pe[-1])
                        h = s.activation(rstdq[0:1, 0:Tc], PS2[0:1, 0:Tc], AF.Ln,
                                         bias=eps6[0:1, 0:1], scale=1.0 / D)
                        chain(s, act, h)
                        h = s.activation(rstdq[0:1, 0:Tc], rstdq[0:1, 0:Tc],
                                         AF.Exp, scale=-0.5)
                        n_act = tinc(h, act)
                        t.wait_ge(act.h, n_act)
                        h = t.matmul(PS2[:, 0:Tc], ones_row[:], rstdq[0:1, 0:Tc],
                                     start=True, stop=True)
                        n_pe = tinc(h, pe)
                        v.wait_ge(pe.h, n_pe)
                        v.wait_ge(stq.h, stq.n)
                        for m in range(KT):
                            h = v.scalar_tensor_tensor(
                                qhat[:, m * Tc:(m + 1) * Tc],
                                qraw[:, m * Tc:(m + 1) * Tc],
                                nwsb[nm][:, m:m + 1], PS2[:, 0:Tc],
                                ALU.mult, ALU.mult)
                        n_dve = tinc(h, dve)
                        t.wait_ge(dve.h, n_dve)
                        v.wait_ge(dve.h, n_dve)
                        sq_pe = []
                        for m in range(KT):
                            qs = qsq[:, (m % 2) * Tc:(m % 2) * Tc + Tc]
                            if m >= 2:
                                v.wait_ge(pe.h, sq_pe[m - 2])
                            h = v.tensor_mul(qs, qhat[:, m * Tc:(m + 1) * Tc],
                                             qhat[:, m * Tc:(m + 1) * Tc])
                            n_dve = tinc(h, dve)
                            t.wait_ge(dve.h, n_dve)
                            h = t.matmul(PS2[0:NH, 0:Tc],
                                         hindsb[:, m * NH:(m + 1) * NH], qs,
                                         start=(m == 0), stop=(m == KT - 1))
                            sq_pe.append(tinc(h, pe))
                        v.wait_ge(pe.h, sq_pe[-1])
                        h = v.tensor_reduce(cmax[:], PS2[0:NH, 0:Tc], AX.X,
                                            ALU.max)
                        chain(v, dve, h)
                        h = v.tensor_max(vmax[nm][:], vmax[nm][:], cmax[:])
                        n_dve = tinc(h, dve)
                        t.wait_ge(dve.h, n_dve)
                        sy.wait_ge(dve.h, n_dve)
                        h = sy.dma_start(
                            out=(qh if nm == "q" else kh)[:, c * Tc:(c + 1) * Tc]
                            .rearrange("(k p) tc -> p k tc", p=128),
                            in_=qhat[:].rearrange("p (k tc) -> p k tc", k=KT))
                        tinc(h, stq, 16)
                    else:
                        tr_hist = []
                        for m in range(KT):
                            for tt in range(ntt):
                                i = m * ntt + tt
                                bank = PB if i % 2 == 0 else PB2
                                rot = (i // 2) % 4
                                if i >= 2:
                                    t.wait_ge(dve.h, tr_hist[i - 2])
                                t.wait_ge(act.h, ev_hist[m])
                                h = t.transpose(
                                    bank[:, rot * 128:(rot + 1) * 128],
                                    vsb[:, m * Tc + tt * 128:m * Tc + tt * 128 + 128],
                                    id16[:])
                                n_pe = tinc(h, pe)
                                v.wait_ge(pe.h, n_pe)
                                h = v.tensor_copy(
                                    vtsb[:, tt * D + m * 128:tt * D + (m + 1) * 128],
                                    bank[:, rot * 128:(rot + 1) * 128])
                                tr_hist.append(tinc(h, dve))
                        sy.wait_ge(dve.h, tr_hist[-1])
                        h = sy.dma_start(
                            out=vtok[c * Tc:(c + 1) * Tc, :]
                            .rearrange("(tt p) d -> p tt d", p=128),
                            in_=vtsb[:].rearrange("p (tt d) -> p tt d", d=D))
                        tinc(h, stq, 16)
            emit_barrier(p, allsems)

        h = v.tensor_mul(bprod[:], vmax["q"][:], vmax["k"][:])
        chain(v, dve, h)
        n_dve = dve.n
        s.wait_ge(dve.h, n_dve)
        h = s.activation(bprod[:], bprod[:], AF.Ln)
        chain(s, act, h)
        h = s.activation(bprod[:], bprod[:], AF.Exp, scale=0.5,
                         bias=ln0125[:, 0:1])
        n_act = tinc(h, act)
        v.wait_ge(act.h, n_act)
        h = v.tensor_scalar(negb[:], bprod[:], -1.0, None, ALU.mult)
        chain(v, dve, h)
        h = v.tensor_copy(negbw[:], negb[:].broadcast_to([16, 128]))
        n_dve = tinc(h, dve)
        t.wait_ge(dve.h, n_dve)
        h = t.transpose(PS2[:, 0:16], negbw[:], id32[0:16, 0:16])
        n_pe = tinc(h, pe)
        v.wait_ge(pe.h, n_pe)
        h = v.tensor_copy(negbc[:], PS2[:, 0:16])
        n_dve_b = tinc(h, dve)
        s.wait_ge(dve.h, n_dve_b)

        with p.scope():
            SAb = [p.ps("SA0", [128, 512], F32), p.ps("SA1", [128, 512], F32)]
            SBb = [p.ps("SB0", [128, 512], F32), p.ps("SB1", [128, 512], F32)]
            PVa = p.ps("PVa", [128, 512], F32)
            PVb = p.ps("PVb", [128, 512], F32)
            khh2 = [p.sb(f"khh{i}", [128, Tfull], BF16) for i in range(2)]
            qhh2 = [p.sb(f"qhh{i}", [128, Tfull], BF16) for i in range(2)]
            vhh2 = [p.sb(f"vhh{i}", [128, NKT * 130], BF16) for i in range(2)]
            pTa = p.sb("pTa", [128, NKT * 512], BF16)
            pTb = p.sb("pTb", [128, NKT * 512], BF16)
            rsum2 = [p.sb("rsumA", [128, 512], F32),
                     p.sb("rsumB", [128, 512], F32)]
            rb_sb2 = [p.sb("rb_sbA", [64, 512], F32),
                      p.sb("rb_sbB", [64, 512], F32)]
            ctx_sb = [p.sb(f"ctx_sb{i}", [64, 512], BF16) for i in range(2)]
            for b_ in range(2):
                v.memset(vhh2[b_][:].rearrange("p (kt c) -> p kt c", c=130)
                         [:, :, 64:65], 1.0)
                h = v.memset(vhh2[b_][:].rearrange("p (kt c) -> p kt c", c=130)
                             [:, :, 129:130], 1.0)
            n_ones = tinc(h, dve)

            NHP = NH // 2
            s_done = {}
            pv_done = {}
            ctx_stq = {}
            rb_dve = [0, 0]
            sy.wait_ge(stq.h, stq.n)

            def issue_pair_loads(hp):
                par = hp % 2
                h = sy.dma_start(out=khh2[par][:],
                                 in_=kh[128 * hp:128 * hp + 128, :])
                tinc(h, ld, 16)
                h = sy.dma_start(out=qhh2[par][:],
                                 in_=qh[128 * hp:128 * hp + 128, :])
                tinc(h, ld, 16)
                h = sy.dma_start(
                    out=vhh2[par][:].rearrange("p (kt c) -> p kt c", c=130)
                    [:, :, 0:64],
                    in_=vtok[:, 128 * hp:128 * hp + 64]
                    .rearrange("(kt p) c -> p kt c", p=128))
                tinc(h, ld, 16)
                h = sy.dma_start(
                    out=vhh2[par][:].rearrange("p (kt c) -> p kt c", c=130)
                    [:, :, 65:129],
                    in_=vtok[:, 128 * hp + 64:128 * hp + 128]
                    .rearrange("(kt p) c -> p kt c", p=128))
                return tinc(h, ld, 16)

            pair_ld = {0: issue_pair_loads(0)}
            n_cpar = 0
            warA = None
            for hp in range(NHP):
                par = hp % 2
                khh = khh2[par]
                qhh = qhh2[par]
                vhh = vhh2[par]
                h0, h1 = 2 * hp, 2 * hp + 1
                t.wait_ge(ld.h, pair_ld[hp])
                s.wait_ge(ld.h, pair_ld[hp])
                if hp == 0:
                    t.wait_ge(dve.h, n_ones)
                    s.wait_ge(dve.h, n_dve_b)
                for qc in range(NQC):
                    exp_ids = []
                    for kt in range(NKT):
                        par2 = kt % 2
                        SAp, SBp = SAb[par2], SBb[par2]
                        if kt >= 2:
                            t.wait_ge(act.h, exp_ids[kt - 2][1])
                        elif warA is not None:
                            t.wait_ge(act.h, warA[kt][1])
                        h = t.matmul(SAp[:, 0:512],
                                     khh[0:64, kt * 128:(kt + 1) * 128],
                                     qhh[0:64, qc * 512:(qc + 1) * 512],
                                     start=True, stop=True,
                                     tile_position=(0, 0))
                        n_pe_a = tinc(h, pe)
                        h = t.matmul(SBp[:, 0:512],
                                     khh[64:128, kt * 128:(kt + 1) * 128],
                                     qhh[64:128, qc * 512:(qc + 1) * 512],
                                     start=True, stop=True,
                                     tile_position=(64, 0))
                        n_pe_b = tinc(h, pe)
                        if kt == NKT - 1:
                            s_done[hp] = n_pe_b
                        if kt >= 1:
                            ia, ib = exp_ids[kt - 1]
                            t.wait_ge(act.h, ia)
                            if kt == 1:
                                t.wait_ge(dve.h, rb_dve[0])
                            h = t.matmul(PVa[0:65, 0:512],
                                         vhh[:, (kt - 1) * 130:
                                             (kt - 1) * 130 + 65],
                                         pTa[:, (kt - 1) * 512:kt * 512],
                                         start=(kt == 1), stop=False)
                            tinc(h, pe)
                            t.wait_ge(act.h, ib)
                            if kt == 1:
                                t.wait_ge(dve.h, rb_dve[1])
                            h = t.matmul(PVb[0:65, 0:512],
                                         vhh[:, (kt - 1) * 130 + 65:
                                             (kt - 1) * 130 + 130],
                                         pTb[:, (kt - 1) * 512:kt * 512],
                                         start=(kt == 1), stop=False)
                            tinc(h, pe)
                        s.wait_ge(pe.h, n_pe_a)
                        h = s.activation(pTa[:, kt * 512:(kt + 1) * 512],
                                         SAp[:, 0:512], AF.Exp,
                                         bias=negbc[:, h0:h0 + 1], scale=0.125)
                        na = tinc(h, act)
                        s.wait_ge(pe.h, n_pe_b)
                        h = s.activation(pTb[:, kt * 512:(kt + 1) * 512],
                                         SBp[:, 0:512], AF.Exp,
                                         bias=negbc[:, h1:h1 + 1], scale=0.125)
                        nb = tinc(h, act)
                        exp_ids.append((na, nb))
                    warA = exp_ids[-2:]
                    ia, ib = exp_ids[-1]
                    kt = NKT - 1
                    t.wait_ge(act.h, ia)
                    h = t.matmul(PVa[0:65, 0:512],
                                 vhh[:, kt * 130:kt * 130 + 65],
                                 pTa[:, kt * 512:(kt + 1) * 512],
                                 start=False, stop=True)
                    n_pva = tinc(h, pe)
                    t.wait_ge(act.h, ib)
                    h = t.matmul(PVb[0:65, 0:512],
                                 vhh[:, kt * 130 + 65:kt * 130 + 130],
                                 pTb[:, kt * 512:(kt + 1) * 512],
                                 start=False, stop=True)
                    n_pvb = tinc(h, pe)
                    pv_done[hp] = n_pvb
                    for hi_, (bank, n_pv) in enumerate([(PVa, n_pva),
                                                        (PVb, n_pvb)]):
                        rs = rsum2[hi_]
                        s.wait_ge(pe.h, n_pv)
                        h = s.activation(rs[64:65, 0:512], bank[64:65, 0:512],
                                         AF.Ln)
                        chain(s, act, h)
                        h = s.activation(rs[64:65, 0:512], rs[64:65, 0:512],
                                         AF.Exp, scale=-1.0)
                        n_act = tinc(h, act)
                        t.wait_ge(act.h, n_act)
                        t.wait_ge(dve.h, max(rb_dve))
                        h = t.matmul(PS2[0:64, 0:512], onesf[64:65, 0:64],
                                     rs[64:65, 0:512], start=True, stop=True)
                        n_pe2 = tinc(h, pe)
                        v.wait_ge(pe.h, n_pe2)
                        rb_sb = rb_sb2[hi_]
                        h = v.tensor_copy(rb_sb[:], PS2[0:64, 0:512])
                        chain(v, dve, h)
                        cpar = n_cpar % 2
                        if cpar in ctx_stq:
                            v.wait_ge(stc[cpar].h, ctx_stq[cpar])
                        h = v.tensor_mul(ctx_sb[cpar][:], bank[0:64, 0:512],
                                         rb_sb[:])
                        rb_dve[hi_] = tinc(h, dve)
                        sy.wait_ge(dve.h, rb_dve[hi_])
                        hh_ = 2 * hp + hi_
                        h = sy.dma_start(
                            out=ctxh[64 * hh_:64 * hh_ + 64,
                                     qc * 512:(qc + 1) * 512],
                            in_=ctx_sb[cpar][:])
                        ctx_stq[cpar] = tinc(h, stc[cpar], 16)
                        n_cpar += 1
                if hp + 1 < NHP:
                    sy.wait_ge(pe.h, s_done[hp])
                    sy.wait_ge(pe.h, pv_done[hp])
                    pair_ld[hp + 1] = issue_pair_loads(hp + 1)
            emit_barrier(p, allsems)

        with p.scope():
            PSW = [p.ps(f"PW3{i}", [128, 1024], F32) for i in range(2)]
            ROT4 = [PSW[0][:, 0:512], PSW[0][:, 512:1024],
                    PSW[1][:, 0:512], PSW[1][:, 512:1024]]
            ccs2 = [qhat, p.sb("ccsB", [128, KT * Tc], BF16)]
            sx2 = [p.sem("sx20"), p.sem("sx21")]
            wsbo = p.sb("wsbo", [128, KT * D], BF16)
            h = g.dma_start(out=wsbo[:].rearrange("p (k m) -> p k m", k=KT),
                            in_=woT[:].rearrange("(k p) m -> p k m", p=128))
            n_gld = tinc(h, gld, 16)
            t.wait_ge(gld.h, n_gld)
            mm_hist = {}

            def issue_x3(c2):
                par2 = c2 % 2
                h2 = sy.dma_start(
                    out=xc2g[par2][:].rearrange("p (k tc) -> p k tc", k=KT),
                    in_=xin[:, c2 * Tc:(c2 + 1) * Tc]
                    .rearrange("(k p) tc -> p k tc", p=128))
                tinc(h2, ld, 16)
                h2 = sy.dma_start(
                    out=ccs2[par2][:].rearrange("p (k tc) -> p k tc", k=KT),
                    in_=ctxh[:, c2 * Tc:(c2 + 1) * Tc]
                    .rearrange("(k p) tc -> p k tc", p=128))
                return tinc(h2, ld, 16)

            x3n = {0: issue_x3(0)}
            for c in range(n_chunks):
                par = c % 2
                xc3 = xc2g[par]
                ccs = ccs2[par]
                if c + 1 < n_chunks:
                    if c >= 1:
                        sy.wait_ge(sx2[(c + 1) % 2].h, sx2[(c + 1) % 2].n)
                        sy.wait_ge(pe.h, mm_hist[c - 1])
                    x3n[c + 1] = issue_x3(c + 1)
                n_cl = x3n[c]
                t.wait_ge(ld.h, n_cl)
                v.wait_ge(ld.h, n_cl)
                ev_hist = []
                for m in range(KT):
                    pp = ROT4[m % 4]
                    if m >= 4:
                        t.wait_ge(dve.h, ev_hist[m - 4])
                    for k in range(KT):
                        h = t.matmul(pp[:, 0:Tc],
                                     wsbo[:, k * D + m * 128:k * D + m * 128 + 128],
                                     ccs[:, k * Tc:(k + 1) * Tc],
                                     start=(k == 0), stop=(k == KT - 1))
                    n_pe = tinc(h, pe)
                    v.wait_ge(pe.h, n_pe)
                    h = v.scalar_tensor_tensor(xc3[:, m * Tc:(m + 1) * Tc],
                                               pp[:, 0:Tc],
                                               bsb["o"][:, m:m + 1],
                                               xc3[:, m * Tc:(m + 1) * Tc],
                                               ALU.add, ALU.add)
                    ev_hist.append(tinc(h, dve))
                mm_hist[c] = n_pe
                sy.wait_ge(dve.h, ev_hist[-1])
                h = sy.dma_start(out=xout[:, c * Tc:(c + 1) * Tc]
                                 .rearrange("(k p) tc -> p k tc", p=128),
                                 in_=xc3[:].rearrange("p (k tc) -> p k tc", k=KT))
                tinc(h, sx2[par], 16)

        g.wait_ge(sx2[0].h, sx2[0].n)
        g.wait_ge(sx2[1].h, sx2[1].n)
    return nc


def _prep_attn0_weights(Wq, bq, Wk, bk, Wv, bv, Wo, bo, nqw, nkw):
    def colsplit(b):
        return np.ascontiguousarray(b.reshape(KT, 128).T)
    NH = 16
    hind = np.zeros((128, KT * NH), dtype=np.float32)
    for k in range(KT):
        hind[0:64, k * NH + 2 * k] = 1.0
        hind[64:128, k * NH + 2 * k + 1] = 1.0
    return dict(
        wqT=np.ascontiguousarray(Wq.T), wkT=np.ascontiguousarray(Wk.T),
        wvT=np.ascontiguousarray(Wv.T), woT=np.ascontiguousarray(Wo.T),
        bq=colsplit(bq), bk=colsplit(bk), bv=colsplit(bv), bo=colsplit(bo),
        nqw=colsplit(nqw), nkw=colsplit(nkw), hind=hind,
        idm=np.eye(128, dtype=np.float32),
        idm16=np.eye(128, dtype=BF))




def build_ffn1(Tfull=T, Tc=512, debug=False, fuse=None):
    assert not (debug and fuse)
    """L1 spiking/MoE ffn sublayer: xout = x + 0.5*ffn1(LN(x))."""
    E = 8
    HID = 2048
    MD = 64
    HK2 = 16
    n_chunks = Tfull // Tc
    ntt = Tc // 128

    nc = fuse["nc"] if fuse else bass.Bass("TRN2", target_bir_lowering=False)
    xin = _fdt(nc, fuse, "xin", [D, Tfull], F32, "ExternalInput")
    enc_w = _fdt(nc, fuse, "enc_w", [D, HID], F32, "ExternalInput")
    s2c_w = _fdt(nc, fuse, "s2c_w", [HID, MD], F32, "ExternalInput")
    r_w1 = _fdt(nc, fuse, "r_w1", [MD, 64], F32, "ExternalInput")
    r_w2 = _fdt(nc, fuse, "r_w2", [64, E], F32, "ExternalInput")
    e_w1 = _fdt(nc, fuse, "e_w1", [E, MD, 1024], F32, "ExternalInput")
    e_w2 = _fdt(nc, fuse, "e_w2", [E, 1024, MD], F32, "ExternalInput")
    c2s_w = _fdt(nc, fuse, "c2s_w", [MD, HID], F32, "ExternalInput")
    dec_w = _fdt(nc, fuse, "dec_w", [HID, D], F32, "ExternalInput")
    enc_b = _fdt(nc, fuse, "enc_b", [128, HK2], F32, "ExternalInput")
    s2c_b = _fdt(nc, fuse, "s2c_b", [MD, 1], F32, "ExternalInput")
    r_b1 = _fdt(nc, fuse, "r_b1", [64, 1], F32, "ExternalInput")
    r_b2 = _fdt(nc, fuse, "r_b2", [E, 1], F32, "ExternalInput")
    e_b1 = _fdt(nc, fuse, "e_b1", [128, E * KT], F32, "ExternalInput")
    e_b2 = _fdt(nc, fuse, "e_b2", [MD, E], F32, "ExternalInput")
    c2s_b = _fdt(nc, fuse, "c2s_b", [128, HK2], F32, "ExternalInput")
    dec_b = _fdt(nc, fuse, "dec_b", [128, KT], F32, "ExternalInput")
    ln_g = _fdt(nc, fuse, "ln_g", [128, KT], F32, "ExternalInput")
    ln_b = _fdt(nc, fuse, "ln_b", [128, KT], F32, "ExternalInput")
    idm = _fdt(nc, fuse, "idm", [128, 128], F32, "ExternalInput")
    ind8 = _fdt(nc, fuse, "ind8", [E, E * 128], F32, "ExternalInput")
    xout = _fdt(nc, fuse, "xout", [D, Tfull], F32, "ExternalOutput")
    dbg = {}
    if debug:
        for nm, shp, dt_ in [("d_hc", [D, 512], BF16), ("d_spk", [HID, 512], BF16),
                             ("d_cont", [MD, 512], BF16), ("d_lg", [E, 512], F32),
                             ("d_combT", [E, 512], F32), ("d_moe", [MD, 512], F32),
                             ("d_rates", [HID, 512], BF16), ("d_dec", [D, 512], F32),
                             ("d_eh0", [1024, 512], BF16)]:
            dbg[nm] = nc.dram_tensor(nm, shp, dt_, kind="ExternalOutput").ap()

    with _prog_ctx(nc, fuse) as p:
        t, s, v, g, sy = p.tensor, p.scalar, p.vector, p.gpsimd, p.sync

        xc2 = [p.sb("xcA", [128, KT * Tc], F32), p.sb("xcB", [128, KT * Tc], F32)]
        lnscr = p.sb("lnscr", [128, Tc], F32)
        lnscr2 = p.sb("lnscr2", [128, Tc], F32)
        hc = p.sb("hc", [128, KT * Tc], BF16)
        spikes = p.sb("spikes", [128, HK2 * Tc], BF16)
        cont16 = p.sb("cont16", [MD, Tc], BF16)
        rh16 = p.sb("rh16", [64, Tc], BF16)
        logits = p.sb("logits", [E, Tc], F32)
        ltok = p.sb("ltok", [128, ntt * E], F32)
        combtok = p.sb("combtok", [128, ntt * E], F32)
        combT = p.sb("combT", [E, Tc], F32)
        combb = p.sb("combb", [MD, E * Tc], BF16)
        eh16 = hc
        tmps = p.sb("tmps", [MD, Tc], F32)
        moe = p.sb("moe", [MD, Tc], F32)
        moe16 = p.sb("moe16", [MD, Tc], BF16)
        rates16 = spikes
        decsbf = p.sb("decsbf", [128, KT * Tc], F32)
        sorted8 = p.sb("sorted8", [128, E], F32)
        negs1 = p.sb("negs1", [128, 1], F32)
        mask8 = p.sb("mask8", [128, E], F32)
        eall = p.sb("eall", [128, E], F32)
        zz = p.sb("zz", [128, E], F32)
        den = p.sb("den", [128, 1], F32)
        rec = p.sb("rec", [128, 1], F32)
        id32 = p.sb("id32", [128, 128], F32)
        ind8sb = p.sb("ind8sb", [E, E * 128], F32)
        wenc = p.sb("wenc", [128, KT * HID], BF16)
        wdec = p.sb("wdec", [128, HK2 * D], BF16)
        ws2c = p.sb("ws2c", [128, HK2 * MD], BF16)
        wc2s = p.sb("wc2s", [MD, HID], BF16)
        wr1 = p.sb("wr1", [MD, 64], BF16)
        wr2 = p.sb("wr2", [64, E], BF16)
        wew1 = p.sb("wew1", [MD, E * 1024], BF16)
        wew2 = p.sb("wew2", [128, E * KT * MD], BF16)
        b_enc = p.sb("b_enc", [128, HK2], F32)
        b_s2c = p.sb("b_s2c", [MD, 1], F32)
        b_r1 = p.sb("b_r1", [64, 1], F32)
        b_r2 = p.sb("b_r2", [E, 1], F32)
        b_e1 = p.sb("b_e1", [128, E * KT], F32)
        b_e2 = p.sb("b_e2", [MD, E], F32)
        b_c2s = p.sb("b_c2s", [128, HK2], F32)
        b_dec = p.sb("b_dec", [128, KT], F32)
        g_ln = p.sb("g_ln", [128, KT], F32)
        bb_ln = p.sb("bb_ln", [128, KT], F32)
        st = {n: p.sb("st_" + n, [1, 512], F32)
              for n in ("ex", "ex2", "rstd", "brow")}
        ones_col = p.sb("ones_col", [128, 1], F32)
        ones_row = p.sb("ones_row", [1, 128], F32)
        eps1 = p.sb("eps1", [1, 1], F32)
        consts = {"ones_col": ones_col, "ones_row": ones_row, "eps1": eps1}

        PS = [p.ps(f"P{i}", [128, 512], F32) for i in range(8)]

        ld = p.sem("ld")
        xld2 = [p.sem("xldA"), p.sem("xldB")]
        stx2 = [p.sem("stxA"), p.sem("stxB")]
        gld = p.sem("gld")
        pe = p.sem("pe")
        act = p.sem("act")
        dve = p.sem("dve")
        stx = p.sem("stx")
        sems = {"pe": pe, "act": act, "dve": dve}

        v.memset(ones_col[:], 1.0)
        v.memset(ones_row[:], 1.0)
        h = v.memset(eps1[:], 1e-5)
        n_const = tinc(h, dve)

        for ap_, sbuf in [(enc_b, b_enc), (s2c_b, b_s2c), (r_b1, b_r1),
                          (r_b2, b_r2), (e_b1, b_e1), (e_b2, b_e2),
                          (c2s_b, b_c2s), (dec_b, b_dec), (ln_g, g_ln),
                          (ln_b, bb_ln), (idm, id32), (ind8, ind8sb)]:
            h = sy.dma_start(out=sbuf[:], in_=ap_[:])
            n_ld0 = tinc(h, ld, 16)
        h = g.dma_start(out=wenc[:].rearrange("p (k m) -> p k m", k=KT),
                        in_=enc_w[:].rearrange("(k p) m -> p k m", p=128))
        tinc(h, gld, 16)
        h = g.dma_start(out=wdec[:].rearrange("p (k m) -> p k m", k=HK2),
                        in_=dec_w[:].rearrange("(k p) m -> p k m", p=128))
        tinc(h, gld, 16)
        h = g.dma_start(out=ws2c[:].rearrange("p (k m) -> p k m", k=HK2),
                        in_=s2c_w[:].rearrange("(k p) m -> p k m", p=128))
        tinc(h, gld, 16)
        h = g.dma_start(out=wc2s[:], in_=c2s_w[:])
        tinc(h, gld, 16)
        h = g.dma_start(out=wr1[:], in_=r_w1[:])
        tinc(h, gld, 16)
        h = g.dma_start(out=wr2[:], in_=r_w2[:])
        tinc(h, gld, 16)
        h = g.dma_start(out=wew1[:].rearrange("p (e m) -> p e m", e=E),
                        in_=e_w1[:, :, :].rearrange("e p m -> p e m"))
        tinc(h, gld, 16)
        h = g.dma_start(out=wew2[:].rearrange("p (e k m) -> p e k m", e=E, k=KT),
                        in_=e_w2[:, :, :].rearrange("e (k p) m -> p e k m", p=128))
        n_gld = tinc(h, gld, 16)

        for eng in (t, s, v):
            eng.wait_ge(gld.h, n_gld)
            eng.wait_ge(ld.h, n_ld0)
            eng.wait_ge(dve.h, n_const)

        sqsl = lambda k: lnscr[:]
        sqsl2 = lambda k: lnscr2[:]
        hsl = lambda k: hc[:, k * Tc:(k + 1) * Tc]
        decsl = lambda k: decsbf[:, k * Tc:(k + 1) * Tc]

        def issue_xload(c):
            par = c % 2
            h = sy.dma_start(out=xc2[par][:].rearrange("p (k tc) -> p k tc", k=KT),
                             in_=xin[:, c * Tc:(c + 1) * Tc]
                             .rearrange("(k p) tc -> p k tc", p=128))
            return tinc(h, xld2[par], 16)

        xld_n = {0: issue_xload(0)}
        resid_dve = 0
        for c in range(n_chunks):
            par = c % 2
            xc = xc2[par]
            xsl = lambda k: xc[:, k * Tc:(k + 1) * Tc]
            if c + 1 < n_chunks:
                if c >= 1:
                    sy.wait_ge(stx2[(c + 1) % 2].h, stx2[(c + 1) % 2].n)
                xld_n[c + 1] = issue_xload(c + 1)
            n_x = xld_n[c]
            for eng in (s, t, v):
                eng.wait_ge(xld2[par].h, n_x)
            n_pe_ln = emit_ln_stats(p, sems, xsl, sqsl, consts,
                                    PS[0], PS[1], PS[2], PS[3], Tc, st)
            v.wait_ge(pe.h, n_pe_ln)
            for k in range(KT):
                h = v.tensor_mul(lnscr[:], xsl(k), PS[2][:, 0:Tc])
                chain(v, dve, h)
                h = v.tensor_add(hsl(k), lnscr[:], PS[3][:, 0:Tc])
                chain(v, dve, h)
            n_h = dve.n
            t.wait_ge(dve.h, n_h)

            enc_hist = []
            for m in range(HK2):
                pp = PS[m % 4]
                if m >= 4:
                    t.wait_ge(act.h, enc_hist[m - 4])
                for k in range(KT):
                    h = t.matmul(pp[:, 0:Tc],
                                 wenc[:, k * HID + m * 128:k * HID + m * 128 + 128],
                                 hsl(k), start=(k == 0), stop=(k == KT - 1))
                n_pe = tinc(h, pe)
                s.wait_ge(pe.h, n_pe)
                h = s.activation(spikes[:, m * Tc:(m + 1) * Tc], pp[:, 0:Tc],
                                 AF.Sigmoid, bias=b_enc[:, m:m + 1])
                n_act = tinc(h, act)
                enc_hist.append(n_act)
            t.wait_ge(act.h, enc_hist[-1])

            for k in range(HK2):
                h = t.matmul(PS[4][0:MD, 0:Tc],
                             ws2c[:, k * MD:(k + 1) * MD],
                             spikes[:, k * Tc:(k + 1) * Tc],
                             start=(k == 0), stop=(k == HK2 - 1))
            n_pe = tinc(h, pe)
            s.wait_ge(pe.h, n_pe)
            h = s.activation(cont16[:], PS[4][0:MD, 0:Tc], AF.Identity,
                             bias=b_s2c[:, 0:1])
            n_act = tinc(h, act)
            t.wait_ge(act.h, n_act)

            h = t.matmul(PS[5][0:64, 0:Tc], wr1[:], cont16[:],
                         start=True, stop=True)
            n_pe = tinc(h, pe)
            s.wait_ge(pe.h, n_pe)
            h = s.activation(rh16[:], PS[5][0:64, 0:Tc], AF.Relu,
                             bias=b_r1[:, 0:1])
            n_act = tinc(h, act)
            t.wait_ge(act.h, n_act)
            h = t.matmul(PS[5][0:E, 0:Tc], wr2[:], rh16[:],
                         start=True, stop=True)
            n_pe = tinc(h, pe)
            s.wait_ge(pe.h, n_pe)
            h = s.activation(logits[:], PS[5][0:E, 0:Tc], AF.Identity,
                             bias=b_r2[:, 0:1])
            n_act = tinc(h, act)
            t.wait_ge(act.h, n_act)
            tr_cp = []
            for tt in range(ntt):
                bank = PS[5] if tt % 2 == 0 else PS[4]
                if tt >= 2:
                    t.wait_ge(dve.h, tr_cp[tt - 2])
                h = t.transpose(bank[0:128, 0:E],
                                logits[:, tt * 128:(tt + 1) * 128],
                                id32[0:E, 0:E])
                n_pe = tinc(h, pe)
                v.wait_ge(pe.h, n_pe)
                h = v.tensor_copy(ltok[:, tt * E:(tt + 1) * E], bank[0:128, 0:E])
                tr_cp.append(tinc(h, dve))
            n_dve = tr_cp[-1]
            for tt in range(ntt):
                sco = ltok[:, tt * E:(tt + 1) * E]
                h = v.max(sorted8[:], sco)
                chain(v, dve, h)
                h = v.tensor_scalar(negs1[:], sorted8[:, 0:1], -1.0, None, ALU.mult)
                n_dve = tinc(h, dve)
                s.wait_ge(dve.h, n_dve)
                h = s.activation(eall[:], sco, AF.Exp, bias=negs1[:])
                n_act = tinc(h, act)
                h = v.tensor_scalar(mask8[:], sco, sorted8[:, 1:2], None, ALU.is_ge)
                chain(v, dve, h)
                v.wait_ge(act.h, n_act)
                h = v.tensor_mul(zz[:], eall[:], mask8[:])
                chain(v, dve, h)
                h = v.tensor_reduce(den[:], zz[:], AX.X, ALU.add)
                chain(v, dve, h)
                h = v.reciprocal(rec[:], den[:])
                chain(v, dve, h)
                h = v.tensor_scalar(combtok[:, tt * E:(tt + 1) * E], zz[:], rec[:],
                                    None, ALU.mult)
                n_dve = tinc(h, dve)
            t.wait_ge(dve.h, n_dve)
            ct_cp = []
            for tt in range(ntt):
                bank = PS[5] if tt % 2 == 0 else PS[4]
                if tt >= 2:
                    t.wait_ge(dve.h, ct_cp[tt - 2])
                h = t.transpose(bank[0:E, 0:128],
                                combtok[:, tt * E:(tt + 1) * E], id32[:])
                n_pe = tinc(h, pe)
                v.wait_ge(pe.h, n_pe)
                h = v.tensor_copy(combT[:, tt * 128:(tt + 1) * 128],
                                  bank[0:E, 0:128])
                ct_cp.append(tinc(h, dve))
            bc_cp = []
            for e in range(E):
                bank = PS[5] if e % 2 == 0 else PS[4]
                if e == 0:
                    t.wait_ge(dve.h, ct_cp[-1])
                elif e >= 2:
                    t.wait_ge(dve.h, bc_cp[e - 2])
                h = t.matmul(bank[0:MD, 0:Tc],
                             ind8sb[:, e * 128:e * 128 + MD], combT[:],
                             start=True, stop=True)
                n_pe = tinc(h, pe)
                v.wait_ge(pe.h, n_pe)
                h = v.tensor_copy(combb[:, e * Tc:(e + 1) * Tc], bank[0:MD, 0:Tc])
                bc_cp.append(tinc(h, dve))
            n_bcast_done = bc_cp[-1]

            h = v.memset(moe[:], 0.0)
            n_moe = tinc(h, dve)
            eh_hist = []
            eo_hist = []
            for e in range(E):
                for m in range(KT):
                    pp = PS[m % 4]
                    gi = e * KT + m
                    if gi >= 4:
                        t.wait_ge(act.h, eh_hist[gi - 4])
                    h = t.matmul(pp[:, 0:Tc],
                                 wew1[:, e * 1024 + m * 128:e * 1024 + m * 128 + 128],
                                 cont16[:], start=True, stop=True)
                    n_pe = tinc(h, pe)
                    s.wait_ge(pe.h, n_pe)
                    h = s.activation(eh16[:, m * Tc:(m + 1) * Tc], pp[:, 0:Tc],
                                     AF.Relu, bias=b_e1[:, e * KT + m:e * KT + m + 1])
                    n_act = tinc(h, act)
                    eh_hist.append(n_act)
                t.wait_ge(act.h, eh_hist[-1])
                peo = PS[4 + e % 2]
                if e == 0:
                    t.wait_ge(dve.h, n_bcast_done)
                if e >= 2:
                    t.wait_ge(dve.h, eo_hist[e - 2])
                for k in range(KT):
                    h = t.matmul(peo[0:MD, 0:Tc],
                                 wew2[:, (e * KT + k) * MD:(e * KT + k + 1) * MD],
                                 eh16[:, k * Tc:(k + 1) * Tc],
                                 start=(k == 0), stop=(k == KT - 1))
                n_pe = tinc(h, pe)
                v.wait_ge(pe.h, n_pe)
                v.wait_ge(dve.h, n_moe)
                h = v.scalar_tensor_tensor(tmps[:], peo[0:MD, 0:Tc],
                                           b_e2[:, e:e + 1],
                                           combb[:, e * Tc:(e + 1) * Tc],
                                           ALU.add, ALU.mult)
                chain(v, dve, h)
                h = v.tensor_add(moe[:], moe[:], tmps[:])
                n_moe = tinc(h, dve)
                eo_hist.append(n_moe)
            t.wait_ge(dve.h, n_moe)
            v.wait_ge(dve.h, n_moe)
            h = v.tensor_copy(moe16[:], moe[:])
            n_dve = tinc(h, dve)
            t.wait_ge(dve.h, n_dve)

            c2s_hist = []
            for m in range(HK2):
                pp = PS[m % 4]
                if m >= 4:
                    t.wait_ge(act.h, c2s_hist[m - 4])
                h = t.matmul(pp[:, 0:Tc],
                             wc2s[:, m * 128:(m + 1) * 128], moe16[:],
                             start=True, stop=True)
                n_pe = tinc(h, pe)
                s.wait_ge(pe.h, n_pe)
                h = s.activation(rates16[:, m * Tc:(m + 1) * Tc], pp[:, 0:Tc],
                                 AF.Sigmoid, bias=b_c2s[:, m:m + 1])
                n_act = tinc(h, act)
                c2s_hist.append(n_act)
            t.wait_ge(act.h, c2s_hist[-1])

            dec_hist = []
            for m in range(KT):
                pp = PS[m % 4]
                if m >= 4:
                    t.wait_ge(act.h, dec_hist[m - 4])
                for k in range(HK2):
                    h = t.matmul(pp[:, 0:Tc],
                                 wdec[:, k * D + m * 128:k * D + m * 128 + 128],
                                 rates16[:, k * Tc:(k + 1) * Tc],
                                 start=(k == 0), stop=(k == HK2 - 1))
                n_pe = tinc(h, pe)
                s.wait_ge(pe.h, n_pe)
                h = s.activation(decsl(m), pp[:, 0:Tc], AF.Sigmoid,
                                 bias=b_dec[:, m:m + 1])
                n_act = tinc(h, act)
                dec_hist.append(n_act)
            t.wait_ge(act.h, dec_hist[-1])

            if debug and c == 0:
                sy.wait_ge(dve.h, dve.n)
                sy.wait_ge(act.h, act.n)
                for nm, sb_, kk in [("d_hc", hc, KT), ("d_spk", spikes, HK2),
                                    ("d_cont", cont16, 1), ("d_lg", logits, 1),
                                    ("d_combT", combT, 1), ("d_moe", moe, 1),
                                    ("d_rates", rates16, HK2), ("d_dec", decsbf, KT),
                                    ("d_eh0", eh16, KT)]:
                    if kk == 1:
                        h = sy.dma_start(out=dbg[nm][:, 0:Tc], in_=sb_[:, 0:Tc])
                    else:
                        h = sy.dma_start(
                            out=dbg[nm][:, 0:Tc].rearrange("(k p) tc -> p k tc", p=128),
                            in_=sb_[:, 0:kk * Tc].rearrange("p (k tc) -> p k tc", k=kk))
                    tinc(h, stx, 16)
                sy.wait_ge(stx.h, stx.n)

            t.wait_ge(dve.h, resid_dve)
            s.wait_ge(dve.h, resid_dve)
            n_pe = emit_ln_stats(p, sems, decsl, sqsl2, consts,
                                 PS[4], PS[5], PS[6], PS[7], Tc, st)
            v.wait_ge(pe.h, n_pe)
            for k in range(KT):
                h = v.tensor_mul(lnscr2[:], decsl(k), PS[6][:, 0:Tc])
                chain(v, dve, h)
                h = v.tensor_add(lnscr2[:], lnscr2[:], PS[7][:, 0:Tc])
                chain(v, dve, h)
                h = v.tensor_scalar(lnscr2[:], lnscr2[:], g_ln[:, k:k + 1],
                                    bb_ln[:, k:k + 1], ALU.mult, ALU.add)
                chain(v, dve, h)
                h = v.scalar_tensor_tensor(xsl(k), lnscr2[:], 0.5, xsl(k),
                                           ALU.mult, ALU.add)
                n_dve = chain(v, dve, h)
            resid_dve = n_dve

            sy.wait_ge(dve.h, dve.n)
            h = sy.dma_start(out=xout[:, c * Tc:(c + 1) * Tc]
                             .rearrange("(k p) tc -> p k tc", p=128),
                             in_=xc[:].rearrange("p (k tc) -> p k tc", k=KT))
            tinc(h, stx2[par], 16)

        g.wait_ge(stx2[0].h, stx2[0].n)
        g.wait_ge(stx2[1].h, stx2[1].n)
    return nc


def _prep_ffn1_weights(enc_w, enc_b, s2c_w, s2c_b, r_w1, r_b1, r_w2, r_b2,
                       e_w1, e_b1, e_w2, e_b2, c2s_w, c2s_b, dec_w, dec_b,
                       ln_g, ln_b):
    E = 8
    ind8 = np.zeros((E, E * 128), dtype=np.float32)
    for e in range(E):
        ind8[e, e * 128:(e + 1) * 128] = 1.0
    return dict(
        enc_w=np.ascontiguousarray(enc_w),
        s2c_w=np.ascontiguousarray(s2c_w),
        r_w1=np.ascontiguousarray(r_w1), r_w2=np.ascontiguousarray(r_w2),
        e_w1=np.ascontiguousarray(e_w1), e_w2=np.ascontiguousarray(e_w2),
        c2s_w=np.ascontiguousarray(c2s_w), dec_w=np.ascontiguousarray(dec_w),
        enc_b=np.ascontiguousarray(enc_b.reshape(16, 128).T),
        s2c_b=np.ascontiguousarray(s2c_b.reshape(64, 1)),
        r_b1=np.ascontiguousarray(r_b1.reshape(64, 1)),
        r_b2=np.ascontiguousarray(r_b2.reshape(8, 1)),
        e_b1=np.ascontiguousarray(
            e_b1.reshape(E, KT, 128).transpose(2, 0, 1).reshape(128, E * KT)),
        e_b2=np.ascontiguousarray(e_b2.T),
        c2s_b=np.ascontiguousarray(c2s_b.reshape(16, 128).T),
        dec_b=np.ascontiguousarray(dec_b.reshape(8, 128).T),
        ln_g=np.ascontiguousarray(ln_g.reshape(8, 128).T),
        ln_b=np.ascontiguousarray(ln_b.reshape(8, 128).T),
        idm=np.eye(128, dtype=np.float32), ind8=ind8)




SCL4 = 0.3535533905932738


def _attn1_decls(nc, Tfull, colmax_only, fuse=None):
    E = {}
    E["xin"] = _fdt(nc, fuse, "xin", [D, Tfull], F32, "ExternalInput")
    for nm in ("wq", "wk", "wv"):
        if colmax_only and nm == "wv":
            continue
        E[nm] = _fdt(nc, fuse, nm, [D, D], F32, "ExternalInput")
    for nm in ("bq", "bk", "bv"):
        if colmax_only and nm == "bv":
            continue
        E[nm] = _fdt(nc, fuse, nm, [128, KT], F32, "ExternalInput")
    E["idm"] = _fdt(nc, fuse, "idm", [128, 128], F32, "ExternalInput")
    E["idm16"] = _fdt(nc, fuse, "idm16", [128, 128], BF16, "ExternalInput")
    if colmax_only:
        E["cmax_out"] = _fdt(nc, fuse, "cmax_out", [1, 1], F32,
                             "ExternalOutput")
    else:
        E["gmax8"] = _fdt(nc, fuse, "gmax8", [1, 8], F32, "ExternalInput")
        E["convB"] = _fdt(nc, fuse, "convB", [16, 3, 128, 128],
                          F32, "ExternalInput")
        E["qh"] = _fdt(nc, fuse, "qh_sc", [D, Tfull], BF16, "Internal")
        E["kh"] = _fdt(nc, fuse, "kh_sc", [D, Tfull], BF16, "Internal")
        E["vtok"] = _fdt(nc, fuse, "vtok_sc", [Tfull, D], BF16, "Internal")
        E["ctxh"] = _fdt(nc, fuse, "ctxh_sc", [D, Tfull], BF16, "Internal")
        E["xout"] = _fdt(nc, fuse, "xout", [D, Tfull], F32, "ExternalOutput")
    return E


def build_attn1(Tfull=T, Tc=512, colmax_only=False, cheap=False, fuse=None):
    NH = 16
    HD = 64
    LM = 64
    n_chunks = Tfull // Tc
    NKT = Tfull // 128
    NSC = Tfull // 512
    ntt = Tc // 128
    segw = Tfull // 64
    lm_per_chunk = Tc // segw

    nc = fuse["nc"] if fuse else bass.Bass("TRN2", target_bir_lowering=False)
    EE = _attn1_decls(nc, Tfull, colmax_only, fuse=fuse)

    with _prog_ctx(nc, fuse) as p:
        t, s, v, g, sy = p.tensor, p.scalar, p.vector, p.gpsimd, p.sync

        xc = p.sb("xc", [128, KT * Tc], F32)
        _p1_scope = p.push_scope()
        lnscr = p.sb("lnscr", [128, Tc], F32)
        hc = p.sb("hc", [128, KT * Tc], BF16)
        qhat = p.sb("qhat", [128, KT * Tc], BF16)
        wsb = {}
        p.ctx = p._scope_stack[-1]; p._depth -= 1; p._sb_left = True
        bsb = {}
        p.ctx = _p1_scope; p._depth += 1; p._sb_left = False
        for nm in (("wq", "wk") if colmax_only else ("wq", "wk", "wv")):
            wsb[nm] = p.sb("w_" + nm, [128, KT * D], BF16)
        p.ctx = p._scope_stack[-1]; p._depth -= 1; p._sb_left = True
        for nm in (("bq", "bk") if colmax_only else ("bq", "bk", "bv")):
            bsb[nm] = p.sb("b_" + nm, [128, KT], F32)
        bscl = {nm: p.sb("bs_" + nm, [128, KT], F32) for nm in ("bq", "bk")}
        id32 = p.sb("id32", [128, 128], F32)
        id16 = p.sb("id16", [128, 128], BF16)
        p.ctx = _p1_scope; p._depth += 1; p._sb_left = False
        qlsum = p.sb("qlsum", [128, KT * LM], F32)
        klsum = p.sb("klsum", [128, KT * LM], F32)
        qla = p.sb("qla", [128, KT * LM], BF16)
        kla = p.sb("kla", [128, KT * LM], BF16)
        p.ctx = p._scope_stack[-1]; p._depth -= 1; p._sb_left = True
        qla2 = p.sb("qla2", [64, NH * LM], BF16)
        kla2 = p.sb("kla2", [64, NH * LM], BF16)
        qla_dr = _fdt(nc, fuse, "qla_dr", [128, KT * LM], BF16, "Internal")
        kla_dr = _fdt(nc, fuse, "kla_dr", [128, KT * LM], BF16, "Internal")
        k2all = p.sb("k2all", [64, NH * 64], F32)
        k2e = p.sb("k2e", [64, 64], F32)
        nmax = p.sb("nmax", [64, 1], F32)
        ssum = p.sb("ssum", [64, 1], F32)
        srec = p.sb("srec", [64, 1], F32)
        cm16 = p.sb("cm16", [1, 16], F32)
        cs1 = p.sb("cs1", [1, 64], F32)
        cmax1 = p.sb("cmax1", [1, 1], F32)
        st = {n: p.sb("st_" + n, [1, 512], F32)
              for n in ("ex", "ex2", "rstd", "brow")}
        ones_col = p.sb("ones_col", [128, 1], F32)
        ones_row = p.sb("ones_row", [1, 128], F32)
        eps1 = p.sb("eps1", [1, 1], F32)
        consts = {"ones_col": ones_col, "ones_row": ones_row, "eps1": eps1}

        PS = [p.ps(f"P{i}", [128, 512], F32) for i in range(6)]
        PB = p.ps("PB", [128, 1024], BF16)
        PB2 = p.ps("PB2", [128, 1024], BF16)

        ld = p.sem("ld")
        gld = p.sem("gld")
        cvld2 = [p.sem("cvldA"), p.sem("cvldB")]
        hld2 = [p.sem("hldA"), p.sem("hldB")]
        stc2 = [p.sem("stcA2"), p.sem("stcB2")]
        pe = p.sem("pe")
        act = p.sem("act")
        dve = p.sem("dve")
        stq = p.sem("stq")
        stx = p.sem("stx")
        sems = {"pe": pe, "act": act, "dve": dve}

        v.memset(ones_col[:], 1.0)
        v.memset(ones_row[:], 1.0)
        v.memset(eps1[:], 1e-5)
        v.memset(qlsum[:], 0.0)
        h = v.memset(klsum[:], 0.0)
        n_const = tinc(h, dve)

        load_list = [(EE["idm"], id32), (EE["idm16"], id16),
                     (EE["bq"], bsb["bq"]), (EE["bk"], bsb["bk"])]
        if not colmax_only:
            load_list.append((EE["bv"], bsb["bv"]))
        for ap_, sbuf in load_list:
            h = sy.dma_start(out=sbuf[:], in_=ap_[:])
            n_ld0 = tinc(h, ld, 16)
        for nm in wsb:
            h = g.dma_start(out=wsb[nm][:].rearrange("p (k m) -> p k m", k=KT),
                            in_=EE[nm][:].rearrange("(k p) m -> p k m", p=128))
            n_gld = tinc(h, gld, 16)
        for eng in (t, s, v):
            eng.wait_ge(gld.h, n_gld)
            eng.wait_ge(dve.h, n_const)
        n_bs = None

        xsl = lambda k: xc[:, k * Tc:(k + 1) * Tc]
        sqsl = lambda k: lnscr[:]
        hsl = lambda k: hc[:, k * Tc:(k + 1) * Tc]

        if not colmax_only:
            p.ctx = _p1_scope; p._depth += 1; p._sb_left = False
            vsb = p.sb("vsb", [128, KT * Tc], BF16)
            vtsb = p.sb("vtsb", [128, ntt * D], BF16)
            p.ctx = p._scope_stack[-1]; p._depth -= 1; p._sb_left = True
        for c in (range(n_chunks) if not (colmax_only and cheap) else []):
            if c > 0:
                sy.wait_ge(stq.h, stq.n)
                sy.wait_ge(dve.h, dve.n)
            h = sy.dma_start(out=xc[:].rearrange("p (k tc) -> p k tc", k=KT),
                             in_=EE["xin"][:, c * Tc:(c + 1) * Tc]
                             .rearrange("(k p) tc -> p k tc", p=128))
            n_x = tinc(h, ld, 16)
            for eng in (s, t, v):
                eng.wait_ge(ld.h, n_x)
            if n_bs is None:
                v.tensor_scalar(bscl["bq"][:], bsb["bq"][:], SCL4, None, ALU.mult)
                h = v.tensor_scalar(bscl["bk"][:], bsb["bk"][:], SCL4, None,
                                    ALU.mult)
                n_bs = tinc(h, dve)
                s.wait_ge(dve.h, n_bs)
            n_pe_ln = emit_ln_stats(p, sems, xsl, sqsl, consts,
                                    PS[0], PS[1], PS[2], PS[3], Tc, st)
            v.wait_ge(pe.h, n_pe_ln)
            for k in range(KT):
                h = v.tensor_mul(lnscr[:], xsl(k), PS[2][:, 0:Tc])
                chain(v, dve, h)
                h = v.tensor_add(hsl(k), lnscr[:], PS[3][:, 0:Tc])
                chain(v, dve, h)
            n_h = dve.n
            t.wait_ge(dve.h, n_h)

            projs = ["wq", "wk"] if colmax_only else ["wq", "wk", "wv"]
            ev_all = []
            for nm in projs:
                qk = nm in ("wq", "wk")
                if nm == "wk":
                    s.wait_ge(stq.h, stq.n)
                    s.wait_ge(dve.h, dve.n)
                for m in range(KT):
                    pp = PS[m % 2]
                    gi = len(ev_all)
                    if gi >= 2:
                        t.wait_ge(act.h, ev_all[gi - 2])
                    for k in range(KT):
                        h = t.matmul(pp[:, 0:Tc],
                                     wsb[nm][:, k * D + m * 128:k * D + m * 128 + 128],
                                     hsl(k), start=(k == 0), stop=(k == KT - 1))
                    n_pe = tinc(h, pe)
                    s.wait_ge(pe.h, n_pe)
                    dst = qhat if qk else vsb
                    h = s.activation(dst[:, m * Tc:(m + 1) * Tc], pp[:, 0:Tc],
                                     AF.Identity,
                                     bias=(bscl["b" + nm[1]][:, m:m + 1] if qk
                                           else bsb["bv"][:, m:m + 1]),
                                     scale=(SCL4 if qk else 1.0))
                    n_act = tinc(h, act)
                    ev_all.append(n_act)
                if qk:
                    lsum = qlsum if nm == "wq" else klsum
                    v.wait_ge(act.h, n_act)
                    for m in range(KT):
                        h = v.tensor_reduce(
                            lnscr[:, 0:lm_per_chunk],
                            qhat[:, m * Tc:(m + 1) * Tc]
                            .rearrange("p (l w) -> p l w", w=segw),
                            AX.X, ALU.add)
                        chain(v, dve, h)
                        h = v.tensor_add(
                            lsum[:, m * LM + c * lm_per_chunk:
                                 m * LM + (c + 1) * lm_per_chunk],
                            lsum[:, m * LM + c * lm_per_chunk:
                                 m * LM + (c + 1) * lm_per_chunk],
                            lnscr[:, 0:lm_per_chunk])
                        n_dve = chain(v, dve, h)
                    if not colmax_only:
                        sy.wait_ge(dve.h, n_dve)
                        h = sy.dma_start(
                            out=(EE["qh"] if nm == "wq" else EE["kh"])
                            [:, c * Tc:(c + 1) * Tc]
                            .rearrange("(k p) tc -> p k tc", p=128),
                            in_=qhat[:].rearrange("p (k tc) -> p k tc", k=KT))
                        tinc(h, stq, 16)
                else:
                    ev_wv = ev_all[-KT:]
                    for m in range(KT):
                        for tt in range(ntt):
                            if tt == 0:
                                t.wait_ge(act.h, ev_wv[m])
                            rot = (m * ntt + tt) % 4
                            h = t.transpose(
                                PB[:, rot * 128:(rot + 1) * 128],
                                vsb[:, m * Tc + tt * 128:m * Tc + tt * 128 + 128],
                                id16[:])
                            n_pe = tinc(h, pe)
                            v.wait_ge(pe.h, n_pe)
                            h = v.tensor_copy(
                                vtsb[:, tt * D + m * 128:tt * D + (m + 1) * 128],
                                PB[:, rot * 128:(rot + 1) * 128])
                            n_dve = tinc(h, dve)
                            t.wait_ge(dve.h, n_dve)
                    sy.wait_ge(dve.h, n_dve)
                    h = sy.dma_start(
                        out=EE["vtok"][c * Tc:(c + 1) * Tc, :]
                        .rearrange("(tt p) d -> p tt d", p=128),
                        in_=vtsb[:].rearrange("p (tt d) -> p tt d", d=D))
                    tinc(h, stq, 16)

        if colmax_only and cheap:
            xlsum = p.sb("xlsum", [128, KT * LM], F32)
            xlsb = p.sb("xlsb", [128, KT * LM], BF16)
            lnf = p.sb("lnf", [128, Tc], F32)
            red8 = p.sb("red8", [128, lm_per_chunk], F32)
            bscl64 = {nm: p.sb("bs64_" + nm, [128, KT], F32)
                      for nm in ("bq", "bk")}
            h = v.memset(xlsum[:], 0.0)
            chain(v, dve, h)
            v.wait_ge(ld.h, n_ld0)
            h = v.tensor_scalar(bscl64["bq"][:], bsb["bq"][:], SCL4 * segw,
                                None, ALU.mult)
            chain(v, dve, h)
            h = v.tensor_scalar(bscl64["bk"][:], bsb["bk"][:], SCL4 * segw,
                                None, ALU.mult)
            n_xl0 = tinc(h, dve)
            xcBc = p.sb("xcB_cm", [128, KT * Tc], F32)
            xc2c = [xc, xcBc]
            lnr_hist = {}

            def issue_xc(c2):
                h2 = sy.dma_start(
                    out=xc2c[c2 % 2][:].rearrange("p (k tc) -> p k tc", k=KT),
                    in_=EE["xin"][:, c2 * Tc:(c2 + 1) * Tc]
                    .rearrange("(k p) tc -> p k tc", p=128))
                return tinc(h2, ld, 16)

            cn = {0: issue_xc(0)}
            for c in range(n_chunks):
                xcc = xc2c[c % 2]
                xslc = lambda k, xcc=xcc: xcc[:, k * Tc:(k + 1) * Tc]
                if c + 1 < n_chunks:
                    if c >= 1:
                        sy.wait_ge(dve.h, lnr_hist[c - 1])
                    cn[c + 1] = issue_xc(c + 1)
                for eng in (s, t, v):
                    eng.wait_ge(ld.h, cn[c])
                if c > 0:
                    s.wait_ge(dve.h, dve.n)
                n_pe_ln = emit_ln_stats(p, sems, xslc, sqsl, consts,
                                        PS[0], PS[1], PS[2], PS[3], Tc, st)
                v.wait_ge(pe.h, n_pe_ln)
                if c == 0:
                    v.wait_ge(dve.h, n_xl0)
                for k in range(KT):
                    h = v.tensor_mul(lnf[:], xslc(k), PS[2][:, 0:Tc])
                    chain(v, dve, h)
                    h = v.tensor_add(lnf[:], lnf[:], PS[3][:, 0:Tc])
                    chain(v, dve, h)
                    h = v.tensor_reduce(red8[:],
                                        lnf[:].rearrange("p (l w) -> p l w",
                                                         w=segw),
                                        AX.X, ALU.add)
                    chain(v, dve, h)
                    sl = xlsum[:, k * LM + c * lm_per_chunk:
                               k * LM + (c + 1) * lm_per_chunk]
                    h = v.tensor_add(sl, sl, red8[:])
                    chain(v, dve, h)
                lnr_hist[c] = dve.n
            h = v.tensor_copy(xlsb[:], xlsum[:])
            n_cast = tinc(h, dve)
            t.wait_ge(dve.h, n_cast)
            for nm in ("wq", "wk"):
                lsum = qlsum if nm == "wq" else klsum
                for m in range(KT):
                    pp = PS[m % 2]
                    for k in range(KT):
                        h = t.matmul(pp[:, 0:LM],
                                     wsb[nm][:, k * D + m * 128:
                                             k * D + m * 128 + 128],
                                     xlsb[:, k * LM:(k + 1) * LM],
                                     start=(k == 0), stop=(k == KT - 1))
                    n_pe = tinc(h, pe)
                    s.wait_ge(pe.h, n_pe)
                    h = s.activation(lsum[:, m * LM:(m + 1) * LM], pp[:, 0:LM],
                                     AF.Identity,
                                     bias=bscl64["b" + nm[1]][:, m:m + 1],
                                     scale=SCL4)
                    n_act = tinc(h, act)
                    t.wait_ge(act.h, n_act)
            v.wait_ge(act.h, n_act)

        h = v.tensor_scalar(qla[:], qlsum[:], 1.0 / segw, None, ALU.mult)
        chain(v, dve, h)
        h = v.tensor_scalar(kla[:], klsum[:], 1.0 / segw, None, ALU.mult)
        n_lm = tinc(h, dve)
        sy.wait_ge(dve.h, n_lm)
        sy.dma_start(out=qla_dr[:], in_=qla[:]).then_inc(ld.h, 16)
        ld.bump(16)
        h = sy.dma_start(out=kla_dr[:], in_=kla[:])
        n_lmd = tinc(h, ld, 16)
        sy.wait_ge(ld.h, n_lmd)
        sy.dma_start(out=qla2[:].rearrange("p (m b l) -> p m b l", m=KT, b=2),
                     in_=qla_dr[:].rearrange("(b p) (m l) -> p m b l", b=2, l=LM)
                     ).then_inc(ld.h, 16)
        ld.bump(16)
        h = sy.dma_start(out=kla2[:].rearrange("p (m b l) -> p m b l", m=KT, b=2),
                         in_=kla_dr[:].rearrange("(b p) (m l) -> p m b l", b=2, l=LM))
        n_lm2 = tinc(h, ld, 16)
        t.wait_ge(ld.h, n_lm2)
        s.wait_ge(ld.h, n_lm2)
        allsems1 = [ld, gld, pe, act, dve, stq, stx]
        emit_barrier(p, allsems1)
        p.ctx = _p1_scope
        p._depth = p._base_depth + 1
        p._sb_left = False
        p.pop_scope()

        def hq(buf, hh_):
            if buf is qla:
                buf = qla2
            elif buf is kla:
                buf = kla2
            return buf[0:64, hh_ * LM:(hh_ + 1) * LM]

        for hh_ in range(NH):
            h = t.matmul(PS[4][0:64, 0:64], hq(qla, hh_), hq(kla, hh_),
                         start=True, stop=True)
            n_pe = tinc(h, pe)
            v.wait_ge(pe.h, n_pe)
            h = v.tensor_reduce(nmax[:], PS[4][0:64, 0:64], AX.X, ALU.max,
                                negate=True)
            n_dve = tinc(h, dve)
            s.wait_ge(dve.h, n_dve)
            h = s.activation(k2e[:], PS[4][0:64, 0:64], AF.Exp, bias=nmax[:, 0:1],
                             accum_out=ssum[:, 0:1])
            n_act = tinc(h, act)
            t.wait_ge(act.h, n_act)
            v.wait_ge(act.h, n_act)
            h = v.reciprocal(srec[:], ssum[:])
            chain(v, dve, h)
            h = v.tensor_scalar(k2all[:, hh_ * 64:(hh_ + 1) * 64], k2e[:],
                                srec[:, 0:1], None, ALU.mult)
            n_dve = tinc(h, dve)
            t.wait_ge(dve.h, n_dve)
            h = t.matmul(PS[5][0:1, 0:64], ones_col[0:64, 0:1],
                         k2all[:, hh_ * 64:(hh_ + 1) * 64], start=True, stop=True)
            n_pe = tinc(h, pe)
            v.wait_ge(pe.h, n_pe)
            h = v.tensor_copy(cs1[:], PS[5][0:1, 0:64])
            chain(v, dve, h)
            h = v.tensor_reduce(cm16[0:1, hh_:hh_ + 1], cs1[:], AX.X, ALU.max)
            n_dve = tinc(h, dve)
            t.wait_ge(dve.h, n_dve)
        v.wait_ge(dve.h, dve.n)
        h = v.tensor_reduce(cmax1[:], cm16[:], AX.X, ALU.max)
        n_cm = tinc(h, dve)

        if colmax_only:
            sy.wait_ge(dve.h, n_cm)
            h = sy.dma_start(out=EE["cmax_out"][:], in_=cmax1[:])
            tinc(h, stx, 16)
            g.wait_ge(stx.h, stx.n)
            return nc

        gm8 = p.sb("gm8", [1, 8], F32)
        gmax = p.sb("gmax", [1, 1], F32)
        crec = p.sb("crec", [1, 1], F32)
        cb64 = p.sb("cb64", [64, 1], F32)
        sy.wait_ge(ld.h, ld.n)
        h = sy.dma_start(out=gm8[:], in_=EE["gmax8"][:])
        n_g8 = tinc(h, ld, 16)
        sy.wait_ge(ld.h, n_g8)
        v.wait_ge(ld.h, n_g8)
        h = v.tensor_reduce(gmax[:], gm8[:], AX.X, ALU.max)
        chain(v, dve, h)
        h = v.reciprocal(crec[:], gmax[:])
        n_dve = tinc(h, dve)
        t.wait_ge(dve.h, n_dve)
        h = t.matmul(PS[5][0:64, 0:1], ones_row[0:1, 0:64], crec[:],
                     start=True, stop=True)
        n_pe = tinc(h, pe)
        v.wait_ge(pe.h, n_pe)
        h = v.tensor_copy(cb64[:], PS[5][0:64, 0:1])
        n_cb = tinc(h, dve)
        t.wait_ge(dve.h, n_cb)

        _h_scope = p.push_scope()
        NHL = NH * 64
        k2T = p.sb("k2T", [64, NHL], F32)
        Vv = p.sb("Vv", [64, NHL], F32)
        VT = p.sb("VT", [64, NHL], F32)
        T1s = p.sb("T1s", [64, NHL], F32)
        T1Ts = p.sb("T1Ts", [64, NHL], F32)
        T3s = p.sb("T3s", [64, NHL], F32)
        T3Ts = p.sb("T3Ts", [64, NHL], F32)
        T5s = p.sb("T5s", [64, NHL], F32)
        T5Ts = p.sb("T5Ts", [64, NHL], F32)
        KVTs = p.sb("KVTs", [64, NHL], F32)
        Zall = p.sb("Zall", [64, NHL], F32)
        Uall = p.sb("Uall", [64, NHL], BF16)
        idrep = p.sb("idrep", [64, NHL], F32)
        h = v.tensor_copy(idrep[:].rearrange("p (h2 c) -> p h2 c", h2=NH),
                          id32[0:64, 0:64].rearrange("p (o c) -> p o c", o=1)
                          .broadcast_to([64, NH, 64]))
        n_idr = tinc(h, dve)

        def hsl64(buf, hh_):
            return buf[0:64, hh_ * 64:(hh_ + 1) * 64]

        t.wait_ge(dve.h, n_cb)
        for hh_ in range(NH):
            bank = PS[4 + (hh_ // 8) % 2]
            h = t.transpose(bank[0:64, (hh_ % 8) * 64:(hh_ % 8) * 64 + 64],
                            k2all[:, hh_ * 64:(hh_ + 1) * 64], id32[0:64, 0:64])
            n_pe = tinc(h, pe)
        v.wait_ge(pe.h, n_pe)
        v.wait_ge(dve.h, n_idr)
        h = v.tensor_copy(k2T[:, 0:512], PS[4][0:64, 0:512])
        chain(v, dve, h)
        h = v.tensor_copy(k2T[:, 512:1024], PS[5][0:64, 0:512])
        chain(v, dve, h)
        h = v.tensor_scalar(Vv[:], k2T[:], cb64[:, 0:1], None, ALU.mult)
        chain(v, dve, h)
        h = v.tensor_scalar(VT[:], k2all[:], cb64[:, 0:1], None, ALU.mult)
        n_dve = tinc(h, dve)
        t.wait_ge(dve.h, n_dve)

        for it in range(6):
            for hh_ in range(NH):
                ba = PS[4 + (hh_ // 8) % 2]
                h = t.matmul(ba[0:64, (hh_ % 8) * 64:(hh_ % 8) * 64 + 64],
                             hsl64(k2T, hh_), hsl64(Vv, hh_),
                             start=True, stop=True)
                tinc(h, pe)
            for hh_ in range(NH):
                bb = PS[2 + (hh_ // 8) % 2]
                h = t.matmul(bb[0:64, (hh_ % 8) * 64:(hh_ % 8) * 64 + 64],
                             hsl64(Vv, hh_), hsl64(k2T, hh_),
                             start=True, stop=True)
                n_pe = tinc(h, pe)
            v.wait_ge(pe.h, n_pe)
            h = v.scalar_tensor_tensor(T1s[:, 0:512], idrep[:, 0:512], 7.0,
                                       PS[4][0:64, 0:512], ALU.mult, ALU.subtract)
            chain(v, dve, h)
            h = v.scalar_tensor_tensor(T1s[:, 512:1024], idrep[:, 512:1024], 7.0,
                                       PS[5][0:64, 0:512], ALU.mult, ALU.subtract)
            chain(v, dve, h)
            h = v.scalar_tensor_tensor(T1Ts[:, 0:512], idrep[:, 0:512], 7.0,
                                       PS[2][0:64, 0:512], ALU.mult, ALU.subtract)
            chain(v, dve, h)
            h = v.scalar_tensor_tensor(T1Ts[:, 512:1024], idrep[:, 512:1024], 7.0,
                                       PS[3][0:64, 0:512], ALU.mult, ALU.subtract)
            chain(v, dve, h)
            h = v.tensor_copy(KVTs[:, 0:512], PS[2][0:64, 0:512])
            chain(v, dve, h)
            h = v.tensor_copy(KVTs[:, 512:1024], PS[3][0:64, 0:512])
            n_dve = tinc(h, dve)
            t.wait_ge(dve.h, n_dve)
            for hh_ in range(NH):
                ba = PS[4 + (hh_ // 8) % 2]
                h = t.matmul(ba[0:64, (hh_ % 8) * 64:(hh_ % 8) * 64 + 64],
                             hsl64(KVTs, hh_), hsl64(T1s, hh_),
                             start=True, stop=True)
                tinc(h, pe)
            for hh_ in range(NH):
                bb = PS[2 + (hh_ // 8) % 2]
                h = t.matmul(bb[0:64, (hh_ % 8) * 64:(hh_ % 8) * 64 + 64],
                             hsl64(T1s, hh_), hsl64(KVTs, hh_),
                             start=True, stop=True)
                n_pe = tinc(h, pe)
            v.wait_ge(pe.h, n_pe)
            h = v.scalar_tensor_tensor(T3s[:, 0:512], idrep[:, 0:512], 15.0,
                                       PS[4][0:64, 0:512], ALU.mult, ALU.subtract)
            chain(v, dve, h)
            h = v.scalar_tensor_tensor(T3s[:, 512:1024], idrep[:, 512:1024], 15.0,
                                       PS[5][0:64, 0:512], ALU.mult, ALU.subtract)
            chain(v, dve, h)
            h = v.scalar_tensor_tensor(T3Ts[:, 0:512], idrep[:, 0:512], 15.0,
                                       PS[2][0:64, 0:512], ALU.mult, ALU.subtract)
            chain(v, dve, h)
            h = v.scalar_tensor_tensor(T3Ts[:, 512:1024], idrep[:, 512:1024], 15.0,
                                       PS[3][0:64, 0:512], ALU.mult, ALU.subtract)
            n_dve = tinc(h, dve)
            t.wait_ge(dve.h, n_dve)
            for hh_ in range(NH):
                ba = PS[4 + (hh_ // 8) % 2]
                h = t.matmul(ba[0:64, (hh_ % 8) * 64:(hh_ % 8) * 64 + 64],
                             hsl64(KVTs, hh_), hsl64(T3s, hh_),
                             start=True, stop=True)
                tinc(h, pe)
            for hh_ in range(NH):
                bb = PS[2 + (hh_ // 8) % 2]
                h = t.matmul(bb[0:64, (hh_ % 8) * 64:(hh_ % 8) * 64 + 64],
                             hsl64(T3s, hh_), hsl64(KVTs, hh_),
                             start=True, stop=True)
                n_pe = tinc(h, pe)
            v.wait_ge(pe.h, n_pe)
            h = v.scalar_tensor_tensor(T5s[:, 0:512], idrep[:, 0:512], 13.0,
                                       PS[4][0:64, 0:512], ALU.mult, ALU.subtract)
            chain(v, dve, h)
            h = v.scalar_tensor_tensor(T5s[:, 512:1024], idrep[:, 512:1024], 13.0,
                                       PS[5][0:64, 0:512], ALU.mult, ALU.subtract)
            chain(v, dve, h)
            h = v.scalar_tensor_tensor(T5Ts[:, 0:512], idrep[:, 0:512], 13.0,
                                       PS[2][0:64, 0:512], ALU.mult, ALU.subtract)
            chain(v, dve, h)
            h = v.scalar_tensor_tensor(T5Ts[:, 512:1024], idrep[:, 512:1024], 13.0,
                                       PS[3][0:64, 0:512], ALU.mult, ALU.subtract)
            n_dve = tinc(h, dve)
            t.wait_ge(dve.h, n_dve)
            for hh_ in range(NH):
                ba = PS[4 + (hh_ // 8) % 2]
                h = t.matmul(ba[0:64, (hh_ % 8) * 64:(hh_ % 8) * 64 + 64],
                             hsl64(VT, hh_), hsl64(T5s, hh_),
                             start=True, stop=True)
                tinc(h, pe)
            for hh_ in range(NH):
                bb = PS[2 + (hh_ // 8) % 2]
                h = t.matmul(bb[0:64, (hh_ % 8) * 64:(hh_ % 8) * 64 + 64],
                             hsl64(T5Ts, hh_), hsl64(VT, hh_),
                             start=True, stop=True)
                n_pe = tinc(h, pe)
            v.wait_ge(pe.h, n_pe)
            h = v.tensor_scalar(Vv[:, 0:512], PS[4][0:64, 0:512], 0.25, None,
                                ALU.mult)
            chain(v, dve, h)
            h = v.tensor_scalar(Vv[:, 512:1024], PS[5][0:64, 0:512], 0.25, None,
                                ALU.mult)
            chain(v, dve, h)
            h = v.tensor_scalar(VT[:, 0:512], PS[2][0:64, 0:512], 0.25, None,
                                ALU.mult)
            chain(v, dve, h)
            h = v.tensor_scalar(VT[:, 512:1024], PS[3][0:64, 0:512], 0.25, None,
                                ALU.mult)
            n_dve = tinc(h, dve)
            t.wait_ge(dve.h, n_dve)

        khh2 = [p.sb(f"khh{i}", [64, Tfull], BF16) for i in range(2)]
        qhh2 = [p.sb(f"qhh{i}", [64, Tfull], BF16) for i in range(2)]
        vhh2 = [p.sb(f"vhh{i}", [128, NKT * HD], BF16) for i in range(2)]
        convsb2 = [p.sb(f"convsb{i}", [128, 3 * 128], BF16)
                   for i in range(2)]
        k3s = p.sb("k3s", [64, Tfull], F32)
        k3n = p.sb("k3n", [64, Tfull], BF16)
        k3T = p.sb("k3T", [128, NKT * HD], BF16)
        nmax3 = p.sb("nmax3", [64, 1], F32)
        k1e = p.sb("k1e", [128, 4 * 64], BF16)
        nmax1 = p.sb("nmax1", [128, 1], F32)
        ssum1 = p.sb("ssum1", [128, 4], F32)
        srec1 = p.sb("srec1", [128, 4], F32)
        k1T_sb = p.sb("k1T_sb", [64, 4 * 128], BF16)
        ctx16b = [p.sb(f"ctx16{i}", [64, 4 * 128], BF16) for i in range(2)]
        ctx_stq2 = {}

        def issue_h_loads(hh_):
            par = hh_ % 2
            h = sy.dma_start(out=khh2[par][:], in_=EE["kh"][64 * hh_:64 * hh_ + 64, :])
            tinc(h, hld2[par], 16)
            h = sy.dma_start(out=qhh2[par][:], in_=EE["qh"][64 * hh_:64 * hh_ + 64, :])
            tinc(h, hld2[par], 16)
            h = sy.dma_start(out=vhh2[par][:].rearrange("p (kt c2) -> p kt c2", c2=HD),
                             in_=EE["vtok"][:, 64 * hh_:64 * hh_ + 64]
                             .rearrange("(kt p) c2 -> p kt c2", p=128))
            n1 = tinc(h, hld2[par], 16)
            h = g.dma_start(out=convsb2[par][:].rearrange("p (d3 m) -> p d3 m",
                                                          d3=3),
                            in_=EE["convB"][hh_].rearrange("d3 p m -> p d3 m"))
            n2 = tinc(h, cvld2[par], 16)
            return (n1, n2)

        sy.wait_ge(stq.h, stq.n)
        h_ld = {0: issue_h_loads(0)}
        h_done = {}
        for hh_ in range(NH):
            par = hh_ % 2
            khh = khh2[par]
            qhh = qhh2[par]
            vhh = vhh2[par]
            convsb = convsb2[par]
            t.wait_ge(hld2[par].h, h_ld[hh_][0])
            t.wait_ge(cvld2[par].h, h_ld[hh_][1])
            s.wait_ge(hld2[par].h, h_ld[hh_][0])

            k3dv = []
            for sc in range(NSC):
                pp = PS[sc % 2]
                if sc >= 2:
                    t.wait_ge(dve.h, k3dv[sc - 2])
                h = t.matmul(pp[0:64, 0:512], hq(qla, hh_),
                             khh[:, sc * 512:(sc + 1) * 512],
                             start=True, stop=True)
                n_pe = tinc(h, pe)
                v.wait_ge(pe.h, n_pe)
                h = v.tensor_copy(k3s[:, sc * 512:(sc + 1) * 512], pp[0:64, 0:512])
                k3dv.append(tinc(h, dve))
            v.wait_ge(dve.h, k3dv[-1])
            h = v.tensor_reduce(nmax3[:], k3s[:], AX.X, ALU.max, negate=True)
            n_dve = tinc(h, dve)
            s.wait_ge(dve.h, n_dve)
            h = s.activation(k3n[:], k3s[:], AF.Exp, bias=nmax3[:, 0:1],
                             accum_out=ssum[:, 0:1])
            n_act = tinc(h, act)
            v.wait_ge(act.h, n_act)
            h = v.reciprocal(srec[:], ssum[:])
            chain(v, dve, h)
            t.wait_ge(act.h, n_act)
            tr_dve = []
            ngrp = (NKT + 7) // 8
            for grp in range(ngrp):
                bank = PB if grp % 2 == 0 else PB2
                gsz = min(8, NKT - grp * 8)
                if grp >= 2:
                    t.wait_ge(dve.h, tr_dve[grp - 2])
                for j in range(gsz):
                    kt = grp * 8 + j
                    h = t.transpose(bank[:, j * 64:(j + 1) * 64],
                                    k3n[:, kt * 128:(kt + 1) * 128],
                                    id16[0:64, 0:64])
                n_pe = tinc(h, pe)
                v.wait_ge(pe.h, n_pe)
                h = v.tensor_copy(k3T[:, grp * 512:grp * 512 + gsz * 64],
                                  bank[:, 0:gsz * 64])
                tr_dve.append(tinc(h, dve))
            t.wait_ge(dve.h, tr_dve[-1])
            zb = PS[2 + par]
            for kt in range(NKT):
                h = t.matmul(zb[0:64, 0:64], k3T[:, kt * HD:(kt + 1) * HD],
                             vhh[:, kt * HD:(kt + 1) * HD],
                             start=(kt == 0), stop=(kt == NKT - 1))
            n_pe = tinc(h, pe)
            v.wait_ge(pe.h, n_pe)
            h = v.tensor_scalar(hsl64(Zall, hh_), zb[0:64, 0:64],
                                srec[:, 0:1], None, ALU.mult)
            n_zdv = tinc(h, dve)
            t.wait_ge(dve.h, n_zdv)
            ub = PS[4 + par]
            h = t.matmul(ub[0:64, 0:64], hsl64(VT, hh_), hsl64(Zall, hh_),
                         start=True, stop=True)
            n_pe = tinc(h, pe)
            v.wait_ge(pe.h, n_pe)
            h = v.tensor_copy(hsl64(Uall, hh_), ub[0:64, 0:64])
            n_u = tinc(h, dve)
            t.wait_ge(dve.h, n_u)
            s.wait_ge(dve.h, n_u)

            NB4 = NKT // 4
            prev_ctx_dve = 0
            prev_s1 = []
            for qb in range(NB4):
                s1b = PS[0 + qb % 2]
                if prev_s1:
                    t.wait_ge(act.h, prev_s1[-1])
                if prev_ctx_dve:
                    t.wait_ge(dve.h, prev_ctx_dve)
                for j in range(4):
                    qt = qb * 4 + j
                    h = t.matmul(s1b[0:128, j * 64:(j + 1) * 64],
                                 qhh[:, qt * 128:(qt + 1) * 128],
                                 hq(kla, hh_), start=True, stop=True)
                n_pe = tinc(h, pe)
                v.wait_ge(pe.h, n_pe)
                h = v.tensor_reduce(nmax1[:], s1b[0:128, 0:256], AX.X, ALU.max,
                                    negate=True)
                n_dve = tinc(h, dve)
                s.wait_ge(dve.h, n_dve)
                h = s.activation(k1e[:], s1b[0:128, 0:256], AF.Exp,
                                 bias=nmax1[:, 0:1])
                n_act = tinc(h, act)
                prev_s1.append(n_act)
                v.wait_ge(act.h, n_act)
                h = v.tensor_reduce(ssum1[:], k1e[:]
                                    .rearrange("p (b c) -> p b c", b=4),
                                    AX.X, ALU.add)
                chain(v, dve, h)
                h = v.reciprocal(srec1[:], ssum1[:])
                chain(v, dve, h)
                for j in range(4):
                    h = v.tensor_scalar(k1e[:, j * 64:(j + 1) * 64],
                                        k1e[:, j * 64:(j + 1) * 64],
                                        srec1[:, j:j + 1], None, ALU.mult)
                n_dve = tinc(h, dve)
                t.wait_ge(dve.h, n_dve)
                bank = PB if qb % 2 == 0 else PB2
                for j in range(4):
                    h = t.transpose(bank[0:64, j * 128:(j + 1) * 128],
                                    k1e[:, j * 64:(j + 1) * 64], id16[:])
                n_pe = tinc(h, pe)
                v.wait_ge(pe.h, n_pe)
                h = v.tensor_copy(k1T_sb[:], bank[0:64, 0:512])
                n_dve = tinc(h, dve)
                t.wait_ge(dve.h, n_dve)
                cpar2 = qb % 2
                ctx16 = ctx16b[cpar2]
                ctx_dv = []
                for j in range(4):
                    qt = qb * 4 + j
                    cb_ = PS[qb % 2] if j % 2 == 0 else PS[2 + par]
                    if j >= 2 and ctx_dv:
                        t.wait_ge(dve.h, ctx_dv[j - 2])
                    elif prev_ctx_dve:
                        t.wait_ge(dve.h, prev_ctx_dve)
                    h = t.matmul(cb_[0:64, 0:128], hsl64(Uall, hh_),
                                 k1T_sb[:, j * 128:(j + 1) * 128],
                                 start=True, stop=False)
                    tinc(h, pe)
                    djs = [dj for dj, jj in enumerate((qt - 1, qt, qt + 1))
                           if 0 <= jj < NKT]
                    for dj in djs:
                        jj = qt - 1 + dj
                        h = t.matmul(cb_[0:64, 0:128],
                                     vhh[:, jj * HD:(jj + 1) * HD],
                                     convsb[:, dj * 128:(dj + 1) * 128],
                                     start=False, stop=(dj == djs[-1]))
                    n_pe = tinc(h, pe)
                    v.wait_ge(pe.h, n_pe)
                    if j == 0 and cpar2 in ctx_stq2:
                        v.wait_ge(stc2[cpar2].h, ctx_stq2[cpar2])
                    h = v.tensor_copy(ctx16[:, j * 128:(j + 1) * 128],
                                      cb_[0:64, 0:128])
                    ctx_dv.append(tinc(h, dve))
                prev_ctx_dve = ctx_dv[-1]
                sy.wait_ge(dve.h, prev_ctx_dve)
                h = sy.dma_start(
                    out=EE["ctxh"][64 * hh_:64 * hh_ + 64,
                                   qb * 512:(qb + 1) * 512],
                    in_=ctx16[:])
                ctx_stq2[cpar2] = tinc(h, stc2[cpar2], 16)
            h_done[hh_] = pe.n
            if hh_ + 1 < NH:
                if hh_ >= 1:
                    sy.wait_ge(pe.h, h_done[hh_ - 1])
                    g.wait_ge(pe.h, h_done[hh_ - 1])
                h_ld[hh_ + 1] = issue_h_loads(hh_ + 1)

        emit_barrier(p, [ld, gld, pe, act, dve, stq, stx,
                         cvld2[0], cvld2[1], hld2[0], hld2[1],
                         stc2[0], stc2[1]])
        p.ctx = _h_scope
        p._depth = p._base_depth + 1
        p.pop_scope()
        ccs = p.sb("ccs", [128, KT * Tc], BF16)
        sy.wait_ge(stc2[0].h, stc2[0].n)
        sy.wait_ge(stc2[1].h, stc2[1].n)
        for c in range(n_chunks):
            sy.wait_ge(stq.h, stq.n)
            if c > 0:
                sy.wait_ge(stx.h, stx.n)
            h = sy.dma_start(out=xc[:].rearrange("p (k tc) -> p k tc", k=KT),
                             in_=EE["xin"][:, c * Tc:(c + 1) * Tc]
                             .rearrange("(k p) tc -> p k tc", p=128))
            tinc(h, ld, 16)
            h = sy.dma_start(out=ccs[:].rearrange("p (k tc) -> p k tc", k=KT),
                             in_=EE["ctxh"][:, c * Tc:(c + 1) * Tc]
                             .rearrange("(k p) tc -> p k tc", p=128))
            n_cl = tinc(h, ld, 16)
            v.wait_ge(ld.h, n_cl)
            for k in range(KT):
                h = v.tensor_add(xsl(k), xsl(k), ccs[:, k * Tc:(k + 1) * Tc])
                n_dve = chain(v, dve, h)
            sy.wait_ge(dve.h, n_dve)
            h = sy.dma_start(out=EE["xout"][:, c * Tc:(c + 1) * Tc]
                             .rearrange("(k p) tc -> p k tc", p=128),
                             in_=xc[:].rearrange("p (k tc) -> p k tc", k=KT))
            tinc(h, stx, 16)

        g.wait_ge(stx.h, stx.n)
    return nc


def _prep_attn1_weights(Wq, bq, Wk, bk, Wv, bv, conv_w, Tfull=T):
    def colsplit(b):
        return np.ascontiguousarray(b.reshape(KT, 128).T)
    NH, KSZ = 16, 65
    convB = np.zeros((NH, 3, 128, 128), dtype=np.float32)
    for hh in range(NH):
        w = conv_w[hh, 0, :, 0]
        for dj in range(3):
            for pp_ in range(128):
                tap0 = (dj - 1) * 128 + pp_ + 32
                lo = max(0, tap0 - 64)
                hi = min(127, tap0)
                if lo > hi:
                    continue
                cs = np.arange(lo, hi + 1)
                convB[hh, dj, pp_, cs] = w[tap0 - cs]
    return dict(
        wq=np.ascontiguousarray(Wq), wk=np.ascontiguousarray(Wk),
        wv=np.ascontiguousarray(Wv),
        bq=colsplit(bq), bk=colsplit(bk), bv=colsplit(bv),
        convB=convB,
        idm=np.eye(128, dtype=np.float32), idm16=np.eye(128, dtype=BF))


_PROG_CACHE = {}


def build_launch1(Tfull=T):
    nc = bass.Bass("TRN2", target_bir_lowering=False)
    x0 = nc.dram_tensor("xin", [D, Tfull], F32, kind="ExternalInput").ap()
    xi1 = nc.dram_tensor("x1_i", [D, Tfull], F32, kind="Internal").ap()
    xi2 = nc.dram_tensor("x2_i", [D, Tfull], F32, kind="Internal").ap()
    xi3 = nc.dram_tensor("x3_i", [D, Tfull], F32, kind="Internal").ap()
    x4 = nc.dram_tensor("x4out", [D, Tfull], F32, kind="ExternalOutput").ap()
    cm = nc.dram_tensor("cmax_out", [1, 1], F32, kind="ExternalOutput").ap()
    wf0, wa0, wf1, wcm = {}, {}, {}, {}
    with ExitStack() as ctx:
        p = Prog(nc, ctx)
        FU = lambda pref, wc, io: dict(nc=nc, p=p, pref=pref, wcache=wc, io=io)
        Tc0 = min(1024, Tfull)
        build_ffn0(Tfull, Tc0, fuse=FU("f0_", wf0, {"xin": x0, "xout": xi1}))
        build_attn0(Tfull, 512, fuse=FU("a0_", wa0, {"xin": xi1, "xout": xi2}))
        build_ffn0(Tfull, Tc0, fuse=FU("f0_", wf0, {"xin": xi2, "xout": xi3}))
        build_ffn1(Tfull, 512, fuse=FU("f1_", wf1, {"xin": xi3, "xout": x4}))
        build_attn1(Tfull, colmax_only=True, cheap=True,
                    fuse=FU("cm_", wcm, {"xin": x4, "cmax_out": cm}))
        p.finalize()
    return nc


def build_launch2(Tfull=T):
    nc = bass.Bass("TRN2", target_bir_lowering=False)
    x4 = nc.dram_tensor("xin", [D, Tfull], F32, kind="ExternalInput").ap()
    xi5 = nc.dram_tensor("x5_i", [D, Tfull], F32, kind="Internal").ap()
    xout = nc.dram_tensor("xout", [D, Tfull], F32, kind="ExternalOutput").ap()
    wa1, wf1 = {}, {}
    with ExitStack() as ctx:
        p = Prog(nc, ctx)
        FU = lambda pref, wc, io: dict(nc=nc, p=p, pref=pref, wcache=wc, io=io)
        build_attn1(Tfull, colmax_only=False,
                    fuse=FU("a1_", wa1, {"xin": x4, "xout": xi5}))
        build_ffn1(Tfull, 512, fuse=FU("f1_", wf1, {"xin": xi5, "xout": xout}))
        p.finalize()
    return nc


def _progs():
    if not _PROG_CACHE:
        _PROG_CACHE["L1"] = build_launch1(Tfull=T)
        _PROG_CACHE["L2"] = build_launch2(Tfull=T)
    return _PROG_CACHE


def kernel(**inputs):
    ii = inputs
    cores = list(range(NCORES))
    progs = _progs()

    w12p, w3p, gwT, ind8 = _prep_ffn0_weights(
        np.asarray(ii["L0_ffn_w12"]), np.asarray(ii["L0_ffn_w3"]),
        np.asarray(ii["L0_ffn_gate_w"]))
    a0 = _prep_attn0_weights(
        np.asarray(ii["L0_attn_Wq"]), np.asarray(ii["L0_attn_bq"]),
        np.asarray(ii["L0_attn_Wk"]), np.asarray(ii["L0_attn_bk"]),
        np.asarray(ii["L0_attn_Wv"]), np.asarray(ii["L0_attn_bv"]),
        np.asarray(ii["L0_attn_Wo"]), np.asarray(ii["L0_attn_bo"]),
        np.asarray(ii["L0_attn_norm_q_w"]), np.asarray(ii["L0_attn_norm_k_w"]))
    f1 = _prep_ffn1_weights(
        np.asarray(ii["L1_ffn_enc_w"]), np.asarray(ii["L1_ffn_enc_b"]),
        np.asarray(ii["L1_ffn_s2c_w"]), np.asarray(ii["L1_ffn_s2c_b"]),
        np.asarray(ii["L1_ffn_r_w1"]), np.asarray(ii["L1_ffn_r_b1"]),
        np.asarray(ii["L1_ffn_r_w2"]), np.asarray(ii["L1_ffn_r_b2"]),
        np.asarray(ii["L1_ffn_e_w1"]), np.asarray(ii["L1_ffn_e_b1"]),
        np.asarray(ii["L1_ffn_e_w2"]), np.asarray(ii["L1_ffn_e_b2"]),
        np.asarray(ii["L1_ffn_c2s_w"]), np.asarray(ii["L1_ffn_c2s_b"]),
        np.asarray(ii["L1_ffn_dec_w"]), np.asarray(ii["L1_ffn_dec_b"]),
        np.asarray(ii["L1_ffn_ln_g"]), np.asarray(ii["L1_ffn_ln_b"]))
    a1 = _prep_attn1_weights(
        np.asarray(ii["L1_attn_Wq"]), np.asarray(ii["L1_attn_bq"]),
        np.asarray(ii["L1_attn_Wk"]), np.asarray(ii["L1_attn_bk"]),
        np.asarray(ii["L1_attn_Wv"]), np.asarray(ii["L1_attn_bv"]),
        np.asarray(ii["L1_attn_conv_w"]), Tfull=T)

    x = np.asarray(ii["L0_attn_x"], dtype=np.float32)
    xs = [np.ascontiguousarray(x[b].T) for b in cores]

    pfx = lambda pref, dd: {pref + k: v for k, v in dd.items()}
    f0w = dict(w12p=w12p, w3p=w3p, gwT=gwT, ind8=ind8, idm=a0["idm"])
    a1cm = {k: a1[k] for k in ("wq", "wk", "bq", "bk", "idm", "idm16")}

    l1maps = [dict(xin=xs[b], **pfx("f0_", f0w), **pfx("a0_", a0),
                   **pfx("f1_", f1), **pfx("cm_", a1cm)) for b in cores]
    res = run_bass_kernel_spmd(progs["L1"], l1maps, core_ids=cores)
    x4s = [np.asarray(r["x4out"]) for r in res.results]
    cms = [np.asarray(r["cmax_out"], np.float32) for r in res.results]
    gmax8 = np.ascontiguousarray(
        np.stack([c.reshape(1) for c in cms], axis=1).astype(np.float32))

    l2maps = [dict(xin=x4s[b], a1_gmax8=gmax8, **pfx("a1_", a1),
                   **pfx("f1_", f1)) for b in cores]
    res = run_bass_kernel_spmd(progs["L2"], l2maps, core_ids=cores)
    out = np.stack([np.ascontiguousarray(np.asarray(r["xout"]).T)
                    for r in res.results])
    return out.astype(np.float32)


# version 11
# speedup vs baseline: 1.1236x; 1.1236x over previous
import numpy as np
import ml_dtypes
from contextlib import ExitStack

import concourse.bass as bass
import concourse.mybir as mybir
from concourse.bass_utils import run_bass_kernel_spmd

F32 = mybir.dt.float32
BF16 = mybir.dt.bfloat16
AF = mybir.ActivationFunctionType
ALU = mybir.AluOpType
AX = mybir.AxisListType
BF = ml_dtypes.bfloat16

D = 1024
KT = 8
T = 4096
NCORES = 8


class _H:
    def __init__(self, rec):
        self._rec = rec

    def then_inc(self, sem, v=1):
        self._rec[3].append((sem, v))
        return self


class EQ:
    def __init__(self, name):
        self._name = name
        self._ops = []

    def __getattr__(self, attr):
        if attr.startswith("_"):
            raise AttributeError(attr)
        def call(*a, **kw):
            rec = [attr, a, kw, []]
            self._ops.append(rec)
            return _H(rec)
        return call

    def _replay(self, eng):
        for name, a, kw, incs in self._ops:
            ins = getattr(eng, name)(*a, **kw)
            for sem, v in incs:
                ins = ins.then_inc(sem, v)


class Sem:
    def __init__(self, handle):
        self.h = handle
        self.n = 0

    def bump(self, k=1):
        self.n += k
        return self.n


def tinc(handle, sem, k=1):
    handle.then_inc(sem.h, k)
    return sem.bump(k)


def chain(q, sem, handle):
    n = tinc(handle, sem)
    q.wait_ge(sem.h, n)
    return n


class Prog:
    def __init__(self, nc, ctx):
        self.nc = nc
        self.ctx = ctx
        self._root_ctx = ctx
        self._semmap = {}
        self._names = {}
        self.prefix = ""
        self._base_depth = 0
        self.sync = EQ("sync")
        self.tensor = EQ("tensor")
        self.scalar = EQ("scalar")
        self.vector = EQ("vector")
        self.gpsimd = EQ("gpsimd")

    def _uname(self, name):
        name = self.prefix + name
        cnt = self._names.get(name, 0)
        self._names[name] = cnt + 1
        return name if cnt == 0 else f"{name}__{cnt}"

    def sb(self, name, shape, dt=F32):
        if getattr(self, "_sb_left", False):
            side = "left"
        else:
            side = "right" if getattr(self, "_depth", 0) > 0 else "left"
        return self.ctx.enter_context(
            self.nc.sbuf_tensor(self._uname(name), shape, dt, side=side))

    def ps(self, name, shape, dt=F32):
        side = "right" if getattr(self, "_depth", 0) > 0 else "left"
        return self.ctx.enter_context(
            self.nc.psum_tensor(self._uname(name), shape, dt, side=side))

    def sem(self, name):
        name = self.prefix + name
        if name not in self._semmap:
            self._semmap[name] = Sem(
                self._root_ctx.enter_context(self.nc.semaphore(name)))
        return self._semmap[name]

    def push_scope(self):
        self._scope_stack = getattr(self, "_scope_stack", [])
        self._scope_stack.append(self.ctx)
        self._depth = getattr(self, "_depth", 0) + 1
        inner = ExitStack()
        self.ctx = inner
        return inner

    def pop_scope(self):
        self.ctx.close()
        self.ctx = self._scope_stack.pop()
        self._depth -= 1

    from contextlib import contextmanager as _cm

    @_cm
    def scope(self):
        old = self.ctx
        inner = ExitStack()
        self.ctx = inner
        self._depth = getattr(self, "_depth", 0) + 1
        try:
            with inner:
                yield
        finally:
            self.ctx = old
            self._depth -= 1

    def finalize(self):
        block = self.ctx.enter_context(self.nc.Block())
        for attr, q in [("sync", self.sync), ("tensor", self.tensor),
                        ("scalar", self.scalar), ("vector", self.vector),
                        ("gpsimd", self.gpsimd)]:
            if q._ops:
                getattr(block, attr)(lambda eng, q=q: q._replay(eng))


def emit_ln_stats(p, sems, x_k, sq_k, consts, psA, psB, psC, psD, n_cols, st):
    t, s, v = p.tensor, p.scalar, p.vector
    pe, act, dve = sems["pe"], sems["act"], sems["dve"]

    def _same_ap(a, b):
        return a.tensor.name == b.tensor.name and a.offset == b.offset
    interleave = all(_same_ap(sq_k(k), sq_k(0)) for k in range(1, KT))
    if not interleave:
        for k in range(KT):
            h = s.activation(sq_k(k), x_k(k), AF.Square)
        n_act = tinc(h, act)
    for k in range(KT):
        t.matmul(psA[0:1, 0:n_cols], consts["ones_col"][:], x_k(k),
                 start=(k == 0), stop=(k == KT - 1))
    if interleave:
        n_pe0 = pe.n
        for k in range(KT):
            if k > 0:
                s.wait_ge(pe.h, n_pe0)
            h = s.activation(sq_k(0), x_k(k), AF.Square)
            n_act = tinc(h, act)
            t.wait_ge(act.h, n_act)
            h = t.matmul(psB[0:1, 0:n_cols], consts["ones_col"][:], sq_k(0),
                         start=(k == 0), stop=(k == KT - 1))
            n_pe0 = tinc(h, pe)
        n_pe = n_pe0
    else:
        t.wait_ge(act.h, n_act)
        for k in range(KT):
            h = t.matmul(psB[0:1, 0:n_cols], consts["ones_col"][:], sq_k(k),
                         start=(k == 0), stop=(k == KT - 1))
        n_pe = tinc(h, pe)

    ex = st["ex"][0:1, 0:n_cols]
    ex2 = st["ex2"][0:1, 0:n_cols]
    rstd = st["rstd"][0:1, 0:n_cols]
    brow = st["brow"][0:1, 0:n_cols]
    v.wait_ge(pe.h, n_pe)
    v.tensor_scalar(ex, psA[0:1, 0:n_cols], 1.0 / D, None, ALU.mult)
    h = v.tensor_scalar(ex2, psB[0:1, 0:n_cols], 1.0 / D, None, ALU.mult)
    chain(v, dve, h)
    h = v.tensor_mul(brow, ex, ex)
    chain(v, dve, h)
    h = v.tensor_sub(ex2, ex2, brow)
    n_dve = tinc(h, dve)
    s.wait_ge(dve.h, n_dve)
    h = s.activation(rstd, ex2, AF.Ln, bias=consts["eps1"][0:1, 0:1])
    chain(s, act, h)
    h = s.activation(rstd, rstd, AF.Exp, scale=-0.5)
    n_act = tinc(h, act)
    v.wait_ge(act.h, n_act)
    h = v.scalar_tensor_tensor(brow, ex, -1.0, rstd, ALU.mult, ALU.mult)
    n_dve = tinc(h, dve)

    t.wait_ge(dve.h, n_dve)
    t.matmul(psC[:, 0:n_cols], consts["ones_row"][:], rstd, start=True, stop=True)
    h = t.matmul(psD[:, 0:n_cols], consts["ones_row"][:], brow, start=True, stop=True)
    return tinc(h, pe)


def emit_ln(p, sems, x_k, sq_k, h_k, consts, psA, psB, psC, psD, n_cols, st):
    t, s, v = p.tensor, p.scalar, p.vector
    pe, act, dve = sems["pe"], sems["act"], sems["dve"]
    n_pe = emit_ln_stats(p, sems, x_k, sq_k, consts, psA, psB, psC, psD,
                         n_cols, st)
    v.wait_ge(pe.h, n_pe)
    for k in range(KT):
        h = v.tensor_mul(sq_k(k), x_k(k), psC[:, 0:n_cols])
    chain(v, dve, h)
    for k in range(KT):
        h = v.tensor_add(h_k(k), sq_k(k), psD[:, 0:n_cols])
    return tinc(h, dve)


def build_ffn0(Tfull=T, Tc=1024, fuse=None):
    E = 8
    NBLK = 16
    NDM = 8
    HK = 16
    n_chunks = Tfull // Tc
    NS = Tc // 512

    nc = fuse["nc"] if fuse else bass.Bass("TRN2", target_bir_lowering=False)
    xin = _fdt(nc, fuse, "xin", [D, Tfull], F32, "ExternalInput")
    w12p = _fdt(nc, fuse, "w12p", [E, NBLK, D, 256], F32, "ExternalInput")
    w3p = _fdt(nc, fuse, "w3p", [E, NDM, 2 * D, 128], F32, "ExternalInput")
    gwT = _fdt(nc, fuse, "gwT", [D, E], F32, "ExternalInput")
    idm = _fdt(nc, fuse, "idm", [128, 128], F32, "ExternalInput")
    ind8 = _fdt(nc, fuse, "ind8", [E, E * 128], F32, "ExternalInput")
    xout = _fdt(nc, fuse, "xout", [D, Tfull], F32, "ExternalOutput")

    with _prog_ctx(nc, fuse) as p:
        t, s, v, g, sy = p.tensor, p.scalar, p.vector, p.gpsimd, p.sync

        xc = p.sb("xc", [128, KT * Tc], F32)
        sq = p.sb("sq", [128, KT * Tc], F32)
        hc = p.sb("hc", [128, KT * Tc], BF16)
        hid = p.sb("hid", [128, NBLK * NS * 512], BF16)
        silu_t = p.sb("silu_t", [128, 2 * 512], BF16)
        t2_t = p.sb("t2_t", [128, 2 * 512], BF16)
        pbc = p.sb("pbc", [128, E * Tc], BF16)
        w12sb = p.sb("w12sb", [128, 2 * KT * 256], BF16)
        w3sb = p.sb("w3sb", [128, 2 * HK * 128], BF16)
        gwsb = p.sb("gwsb", [128, KT * E], BF16)
        id32 = p.sb("id32", [128, 128], F32)
        ind8sb = p.sb("ind8sb", [E, E * 128], F32)
        ntt = Tc // 128
        sc_tok = p.sb("sc_tok", [128, ntt * E], F32)
        sorted8 = p.sb("sorted8", [128, E], F32)
        negs1 = p.sb("negs1", [128, 1], F32)
        mask8 = p.sb("mask8", [128, E], F32)
        eall = p.sb("eall", [128, E], F32)
        zz = p.sb("zz", [128, E], F32)
        den = p.sb("den", [128, 1], F32)
        rec = p.sb("rec", [128, 1], F32)
        ptok = p.sb("ptok", [128, ntt * E], F32)
        pT = p.sb("pT", [E, Tc], F32)
        st = {n: p.sb("st_" + n, [1, 512], F32)
              for n in ("ex", "ex2", "rstd", "brow")}
        ones_col = p.sb("ones_col", [128, 1], F32)
        ones_row = p.sb("ones_row", [1, 128], F32)
        eps1 = p.sb("eps1", [1, 1], F32)
        consts = {"ones_col": ones_col, "ones_row": ones_row, "eps1": eps1}

        PS = [p.ps(f"P{i}", [128, 512], F32) for i in range(8)]

        ld = p.sem("ld")
        wldp = {("w12", 0): p.sem("wld120"), ("w12", 1): p.sem("wld121"),
                ("w3", 0): p.sem("wld30"), ("w3", 1): p.sem("wld31")}
        wld_base = {kk: ss.n for kk, ss in wldp.items()}
        pe = p.sem("pe")
        act = p.sem("act")
        dve = p.sem("dve")
        stx = p.sem("stx")
        sems = {"pe": pe, "act": act, "dve": dve}

        v.memset(ones_col[:], 1.0)
        v.memset(ones_row[:], 1.0)
        h = v.memset(eps1[:], 1e-5)
        n_const = tinc(h, dve)

        sy.dma_start(out=id32[:], in_=idm[:]).then_inc(ld.h, 16)
        ld.bump(16)
        h = sy.dma_start(out=ind8sb[:], in_=ind8[:])
        n_ld0 = tinc(h, ld, 16)
        gld = p.sem("gld")
        h = g.dma_start(out=gwsb[:].rearrange("p (k e) -> p k e", k=KT),
                        in_=gwT[:].rearrange("(k p) e -> p k e", p=128))
        n_gld = tinc(h, gld, 16)

        w12v = lambda par: w12sb[:, par * KT * 256:(par + 1) * KT * 256]
        w3v = lambda par: w3sb[:, par * HK * 128:(par + 1) * HK * 128]
        dma_seq = []
        for e in range(E):
            dma_seq += [("w12", e, m) for m in range(NBLK)]
            dma_seq += [("w3", e, dm) for dm in range(NDM)]
        gseq = [(kind, c, e, blk) for c in range(n_chunks)
                for (kind, e, blk) in dma_seq]
        kind_hist = {"w12": [], "w3": []}
        for i, (kind, c, e, blk) in enumerate(gseq):
            kind_hist[kind].append(i)
        block_pe = []

        xsl = lambda k, n: xc[:, k * Tc + n * 512: k * Tc + n * 512 + 512]
        sqsl = lambda k, n: sq[:, k * Tc + n * 512: k * Tc + n * 512 + 512]
        hsl = lambda k, n: hc[:, k * Tc + n * 512: k * Tc + n * 512 + 512]

        gblk = 0
        per_kind = {"w12": 0, "w3": 0}
        for eng in (t, s, v):
            eng.wait_ge(dve.h, n_const)
        t.wait_ge(gld.h, n_gld)

        for c in range(n_chunks):
            if c > 0:
                sy.wait_ge(stx.h, stx.n)
            h = sy.dma_start(out=xc[:].rearrange("p (k tc) -> p k tc", k=KT),
                             in_=xin[:, c * Tc:(c + 1) * Tc]
                             .rearrange("(k p) tc -> p k tc", p=128))
            n_x = tinc(h, ld, 16)
            for eng in (s, t, v):
                eng.wait_ge(ld.h, n_x)

            for n in range(NS):
                n_h = emit_ln(p, sems, lambda k, n=n: xsl(k, n),
                              lambda k, n=n: sqsl(k, n),
                              lambda k, n=n: hsl(k, n), consts,
                              PS[0], PS[1], PS[2], PS[3], 512, st)

            t.wait_ge(dve.h, n_h)
            for tt in range(ntt):
                for k in range(KT):
                    h = t.matmul(PS[4][:, 0:E],
                                 hc[:, k * Tc + tt * 128: k * Tc + tt * 128 + 128],
                                 gwsb[:, k * E:(k + 1) * E],
                                 start=(k == 0), stop=(k == KT - 1))
                n_pe = tinc(h, pe)
                v.wait_ge(pe.h, n_pe)
                h = v.tensor_copy(sc_tok[:, tt * E:(tt + 1) * E], PS[4][:, 0:E])
                n_dve = tinc(h, dve)
                t.wait_ge(dve.h, n_dve)
            for tt in range(ntt):
                sco = sc_tok[:, tt * E:(tt + 1) * E]
                h = v.max(sorted8[:], sco)
                chain(v, dve, h)
                h = v.tensor_scalar(negs1[:], sorted8[:, 0:1], -1.0, None, ALU.mult)
                n_dve = tinc(h, dve)
                s.wait_ge(dve.h, n_dve)
                h = s.activation(eall[:], sco, AF.Exp, bias=negs1[:])
                n_act = tinc(h, act)
                h = v.tensor_scalar(mask8[:], sco, sorted8[:, 1:2], None, ALU.is_ge)
                chain(v, dve, h)
                v.wait_ge(act.h, n_act)
                h = v.tensor_mul(zz[:], eall[:], mask8[:])
                chain(v, dve, h)
                h = v.tensor_reduce(den[:], zz[:], AX.X, ALU.add)
                chain(v, dve, h)
                h = v.reciprocal(rec[:], den[:])
                chain(v, dve, h)
                h = v.tensor_scalar(ptok[:, tt * E:(tt + 1) * E], zz[:], rec[:],
                                    None, ALU.mult)
                n_dve = tinc(h, dve)
            t.wait_ge(dve.h, n_dve)
            for tt in range(ntt):
                h = t.transpose(PS[5][0:E, 0:128],
                                ptok[:, tt * E:(tt + 1) * E], id32[:])
                n_pe = tinc(h, pe)
                v.wait_ge(pe.h, n_pe)
                h = v.tensor_copy(pT[:, tt * 128:(tt + 1) * 128], PS[5][0:E, 0:128])
                n_dve = tinc(h, dve)
                t.wait_ge(dve.h, n_dve)
            for e in range(E):
                for n in range(NS):
                    h = t.matmul(PS[6][:, 0:512],
                                 ind8sb[:, e * 128:(e + 1) * 128],
                                 pT[0:E, n * 512:(n + 1) * 512],
                                 start=True, stop=True)
                    n_pe = tinc(h, pe)
                    s.wait_ge(pe.h, n_pe)
                    h = s.activation(pbc[:, e * Tc + n * 512: e * Tc + n * 512 + 512],
                                     PS[6][:, 0:512], AF.Copy)
                    n_actb = tinc(h, act)
                    t.wait_ge(act.h, n_actb)

            hid_hist = {}
            py_hist = {}
            for e in range(E):
                for m in range(NBLK):
                    kcnt = per_kind["w12"]
                    par = kcnt % 2
                    t.wait_ge(wldp[("w12", par)].h,
                              wld_base[("w12", par)] + 16 * (kcnt // 2 + 1))
                    for n in range(NS):
                        mp = (m * NS + n) % 2
                        pha = PS[0 + mp * 2]
                        phb = PS[1 + mp * 2]
                        prev = hid_hist.get(mp)
                        if prev is not None:
                            t.wait_ge(act.h, prev[0])
                            t.wait_ge(dve.h, prev[1])
                        for k in range(KT):
                            t.matmul(pha[:, 0:512],
                                     w12v(par)[:, k * 256:k * 256 + 128],
                                     hsl(k, n), start=(k == 0), stop=(k == KT - 1))
                        for k in range(KT):
                            h = t.matmul(phb[:, 0:512],
                                         w12v(par)[:, k * 256 + 128:k * 256 + 256],
                                         hsl(k, n), start=(k == 0), stop=(k == KT - 1))
                        last_mm = h
                        n_pe = tinc(h, pe)
                        sl = silu_t[:, mp * 512:mp * 512 + 512]
                        t2 = t2_t[:, mp * 512:mp * 512 + 512]
                        s.wait_ge(pe.h, n_pe)
                        if prev is not None:
                            s.wait_ge(dve.h, prev[1])
                        h = s.activation(sl, pha[:, 0:512], AF.Silu)
                        n_act = tinc(h, act)
                        v.wait_ge(pe.h, n_pe)
                        h = v.tensor_mul(t2, phb[:, 0:512],
                                         pbc[:, e * Tc + n * 512:e * Tc + n * 512 + 512])
                        chain(v, dve, h)
                        v.wait_ge(act.h, n_act)
                        h = v.tensor_mul(hid[:, (m * NS + n) * 512:(m * NS + n) * 512 + 512],
                                         sl, t2)
                        n_dve = tinc(h, dve)
                        hid_hist[mp] = (n_act, n_dve)
                    gblk += 1
                    block_pe.append(pe.n)
                    per_kind["w12"] += 1
                t.wait_ge(dve.h, max(d for (_, d) in hid_hist.values()))
                for dm in range(NDM):
                    kcnt = per_kind["w3"]
                    par = kcnt % 2
                    t.wait_ge(wldp[("w3", par)].h,
                              wld_base[("w3", par)] + 16 * (kcnt // 2 + 1))
                    for n in range(NS):
                        dp = (dm * NS + n) % 2
                        py = PS[4 + dp]
                        prev = py_hist.get(dp)
                        if prev is not None:
                            t.wait_ge(dve.h, prev)
                        for hk in range(HK):
                            h = t.matmul(py[:, 0:512],
                                         w3v(par)[:, hk * 128:hk * 128 + 128],
                                         hid[:, (hk * NS + n) * 512:(hk * NS + n) * 512 + 512],
                                         start=(hk == 0), stop=(hk == HK - 1))
                        last_mm3 = h
                        n_pe = tinc(h, pe)
                        v.wait_ge(pe.h, n_pe)
                        h = v.scalar_tensor_tensor(xsl(dm, n), py[:, 0:512], 0.5,
                                                   xsl(dm, n), ALU.mult, ALU.add)
                        n_dve = tinc(h, dve)
                        py_hist[dp] = n_dve
                    gblk += 1
                    block_pe.append(pe.n)
                    per_kind["w3"] += 1

            t.wait_ge(dve.h, dve.n)
            t.wait_ge(act.h, act.n)
            sy.wait_ge(dve.h, dve.n)
            h = sy.dma_start(out=xout[:, c * Tc:(c + 1) * Tc]
                             .rearrange("(k p) tc -> p k tc", p=128),
                             in_=xc[:].rearrange("p (k tc) -> p k tc", k=KT))
            tinc(h, stx, 16)

        per_kind = {"w12": 0, "w3": 0}
        for i, (kind, c, e, blk) in enumerate(gseq):
            kcnt = per_kind[kind]
            par = kcnt % 2
            if kcnt >= 2:
                g.wait_ge(pe.h, block_pe[kind_hist[kind][kcnt - 2]])
            if kind == "w12":
                hh = g.dma_start(
                    out=w12v(par).rearrange("p (k c2) -> p k c2", k=KT),
                    in_=w12p[e, blk].rearrange("(k p) c2 -> p k c2", p=128))
            else:
                hh = g.dma_start(
                    out=w3v(par).rearrange("p (k c2) -> p k c2", k=HK),
                    in_=w3p[e, blk].rearrange("(k p) c2 -> p k c2", p=128))
            tinc(hh, wldp[(kind, par)], 16)
            per_kind[kind] += 1
        assert len(block_pe) == len(gseq)
        g.wait_ge(stx.h, stx.n)
    return nc


def _prep_ffn0_weights(w12, w3, gate_w):
    E = 8
    w12p = np.empty((E, 16, D, 256), dtype=np.float32)
    for e in range(E):
        w12T = np.ascontiguousarray(w12[e].T)
        for m in range(16):
            w12p[e, m, :, 0:128] = w12T[:, m * 128:(m + 1) * 128]
            w12p[e, m, :, 128:256] = w12T[:, (m + 16) * 128:(m + 17) * 128]
    w3p = np.empty((E, 8, 2 * D, 128), dtype=np.float32)
    for e in range(E):
        w3T = np.ascontiguousarray(w3[e].T)
        for dm in range(8):
            w3p[e, dm] = w3T[:, dm * 128:(dm + 1) * 128]
    gwT = np.ascontiguousarray(gate_w.T)
    ind8 = np.zeros((E, E * 128), dtype=np.float32)
    for e in range(E):
        ind8[e, e * 128:(e + 1) * 128] = 1.0
    return w12p, w3p, gwT, ind8




def emit_barrier(p, all_sems):
    for q in (p.tensor, p.scalar, p.vector, p.gpsimd, p.sync):
        for s_ in all_sems:
            if s_.n > 0:
                q.wait_ge(s_.h, s_.n)


def _fdt(nc, fuse, name, shape, dt_, kind):
    if fuse is None:
        return nc.dram_tensor(name, shape, dt_, kind=kind).ap()
    io = fuse.get("io", {})
    if name in io:
        return io[name]
    cache = fuse["wcache"]
    if name not in cache:
        cache[name] = nc.dram_tensor(fuse["pref"] + name, shape, dt_,
                                     kind=kind).ap()
    return cache[name]


from contextlib import contextmanager as _cm2


@_cm2
def _prog_ctx(nc, fuse):
    if fuse is None:
        with ExitStack() as ctx:
            p = Prog(nc, ctx)
            yield p
            p.finalize()
    else:
        p = fuse["p"]
        old_pref, old_base = p.prefix, p._base_depth
        p.prefix = fuse["pref"]
        with p.scope():
            p._base_depth = p._depth
            yield p
            emit_barrier(p, list(p._semmap.values()))
        p.prefix, p._base_depth = old_pref, old_base


def build_attn0(Tfull=T, Tc=512, fuse=None):
    NH = 16
    HD = 64
    n_chunks = Tfull // Tc
    NKT = Tfull // 128
    NQC = Tfull // 512
    ntt = Tc // 128

    nc = fuse["nc"] if fuse else bass.Bass("TRN2", target_bir_lowering=False)
    xin = _fdt(nc, fuse, "xin", [D, Tfull], F32, "ExternalInput")
    wqT = _fdt(nc, fuse, "wqT", [D, D], F32, "ExternalInput")
    wkT = _fdt(nc, fuse, "wkT", [D, D], F32, "ExternalInput")
    wvT = _fdt(nc, fuse, "wvT", [D, D], F32, "ExternalInput")
    woT = _fdt(nc, fuse, "woT", [D, D], F32, "ExternalInput")
    bq = _fdt(nc, fuse, "bq", [128, KT], F32, "ExternalInput")
    bk = _fdt(nc, fuse, "bk", [128, KT], F32, "ExternalInput")
    bv = _fdt(nc, fuse, "bv", [128, KT], F32, "ExternalInput")
    bo = _fdt(nc, fuse, "bo", [128, KT], F32, "ExternalInput")
    nqw = _fdt(nc, fuse, "nqw", [128, KT], F32, "ExternalInput")
    nkw = _fdt(nc, fuse, "nkw", [128, KT], F32, "ExternalInput")
    idm = _fdt(nc, fuse, "idm", [128, 128], F32, "ExternalInput")
    idm16 = _fdt(nc, fuse, "idm16", [128, 128], BF16, "ExternalInput")
    hind = _fdt(nc, fuse, "hind", [128, KT * NH], F32, "ExternalInput")
    qh = _fdt(nc, fuse, "qh_sc", [D, Tfull], BF16, "Internal")
    kh = _fdt(nc, fuse, "kh_sc", [D, Tfull], BF16, "Internal")
    vtok = _fdt(nc, fuse, "vtok_sc", [Tfull, D], BF16, "Internal")
    ctxh = _fdt(nc, fuse, "ctxh_sc", [D, Tfull], BF16, "Internal")
    xout = _fdt(nc, fuse, "xout", [D, Tfull], F32, "ExternalOutput")

    with _prog_ctx(nc, fuse) as p:
        t, s, v, g, sy = p.tensor, p.scalar, p.vector, p.gpsimd, p.sync

        xc2g = [p.sb("xcA", [128, KT * Tc], F32),
                p.sb("xcB", [128, KT * Tc], F32)]
        xc = xc2g[0]
        qhat = p.sb("qhat", [128, KT * Tc], BF16)
        bsb = {nm: p.sb("b_" + nm, [128, KT], F32)
               for nm in ("q", "k", "v", "o")}
        nwsb = {nm: p.sb("nw_" + nm, [128, KT], F32) for nm in ("q", "k")}
        id32 = p.sb("id32", [128, 128], F32)
        id16 = p.sb("id16", [128, 128], BF16)
        hindsb = p.sb("hindsb", [128, KT * NH], F32)
        vmax = {nm: p.sb("vmax_" + nm, [16, 1], F32) for nm in ("q", "k")}
        cmax = p.sb("cmax", [16, 1], F32)
        bprod = p.sb("bprod", [16, 1], F32)
        negb = p.sb("negb", [16, 1], F32)
        negbw = p.sb("negbw", [16, 128], F32)
        negbc = p.sb("negbc", [128, 16], F32)
        ln0125 = p.sb("ln0125", [16, 1], F32)
        st = {n: p.sb("st_" + n, [1, 512], F32)
              for n in ("ex", "ex2", "rstd", "brow")}
        rstdq = p.sb("rstdq", [1, 512], F32)
        ones_col = p.sb("ones_col", [128, 1], F32)
        ones_row = p.sb("ones_row", [1, 128], F32)
        onesf = p.sb("onesf", [128, 64], F32)
        eps1 = p.sb("eps1", [1, 1], F32)
        eps6 = p.sb("eps6", [1, 1], F32)
        consts = {"ones_col": ones_col, "ones_row": ones_row, "eps1": eps1}

        PS2 = p.ps("PS2", [128, 512], F32)

        ld = p.sem("ld")
        gld = p.sem("gld")
        pe = p.sem("pe")
        act = p.sem("act")
        dve = p.sem("dve")
        stq = p.sem("stq")
        stx = p.sem("stx")
        stc = [p.sem("stc0"), p.sem("stc1")]
        sems = {"pe": pe, "act": act, "dve": dve}
        allsems = [ld, gld, pe, act, dve, stq, stx, stc[0], stc[1]]

        v.memset(ones_col[:], 1.0)
        v.memset(ones_row[:], 1.0)
        v.memset(onesf[:], 1.0)
        v.memset(eps1[:], 1e-5)
        v.memset(eps6[:], 1e-6)
        v.memset(vmax["q"][:], -1e30)
        v.memset(vmax["k"][:], -1e30)
        h = v.memset(ln0125[:], -2.0794415416798357)
        n_const = tinc(h, dve)

        for ap_, sbuf in [(idm, id32), (idm16, id16), (hind, hindsb),
                          (bq, bsb["q"]), (bk, bsb["k"]), (bv, bsb["v"]),
                          (bo, bsb["o"]), (nqw, nwsb["q"]), (nkw, nwsb["k"])]:
            h = sy.dma_start(out=sbuf[:], in_=ap_[:])
            n_ld0 = tinc(h, ld, 16)

        xsl = lambda k: xc[:, k * Tc:(k + 1) * Tc]

        with p.scope():
            PSW = [p.ps(f"PW{i}", [128, 1024], F32) for i in range(2)]
            ROT4 = [PSW[0][:, 0:512], PSW[0][:, 512:1024],
                    PSW[1][:, 0:512], PSW[1][:, 512:1024]]
            PB = p.ps("PB", [128, 1024], BF16)
            PB2 = p.ps("PB2", [128, 1024], BF16)
            sq = p.sb("sq", [128, KT * Tc], F32)
            qraw = sq
            hc = p.sb("hc", [128, KT * Tc], BF16)
            vsb = p.sb("vsb", [128, KT * Tc], BF16)
            vtsb = p.sb("vtsb", [128, ntt * D], BF16)
            qsq = p.sb("qsq", [128, 2 * Tc], F32)
            wsb = {nm: p.sb("w_" + nm, [128, KT * D], BF16)
                   for nm in ("q", "k", "v")}
            for nm, wt in [("q", wqT), ("k", wkT), ("v", wvT)]:
                h = g.dma_start(out=wsb[nm][:].rearrange("p (k m) -> p k m", k=KT),
                                in_=wt[:].rearrange("(k p) m -> p k m", p=128))
                n_gld = tinc(h, gld, 16)
            for eng in (t, s, v):
                eng.wait_ge(gld.h, n_gld)
                eng.wait_ge(dve.h, n_const)

            sqsl = lambda k: sq[:, k * Tc:(k + 1) * Tc]
            hsl = lambda k: hc[:, k * Tc:(k + 1) * Tc]

            nh_hist = {}

            def issue_x0(c2):
                h2 = sy.dma_start(
                    out=xc2g[c2 % 2][:].rearrange("p (k tc) -> p k tc", k=KT),
                    in_=xin[:, c2 * Tc:(c2 + 1) * Tc]
                    .rearrange("(k p) tc -> p k tc", p=128))
                return tinc(h2, ld, 16)

            xn = {0: issue_x0(0)}
            for c in range(n_chunks):
                xc = xc2g[c % 2]
                xsl = lambda k, xc=xc: xc[:, k * Tc:(k + 1) * Tc]
                if c + 1 < n_chunks:
                    if c >= 1:
                        sy.wait_ge(dve.h, nh_hist[c - 1])
                    xn[c + 1] = issue_x0(c + 1)
                n_x = xn[c]
                for eng in (s, t, v):
                    eng.wait_ge(ld.h, n_x)
                if c > 0:
                    s.wait_ge(dve.h, dve.n)
                n_h = emit_ln(p, sems, xsl, sqsl, hsl, consts,
                              ROT4[0], ROT4[1], ROT4[2], ROT4[3], Tc, st)
                nh_hist[c] = n_h
                t.wait_ge(dve.h, n_h)

                for nm in ("q", "k", "v"):
                    qk = nm != "v"
                    ev_hist = []
                    for m in range(KT):
                        pp = ROT4[m % 4]
                        if m >= 4:
                            t.wait_ge(act.h, ev_hist[m - 4])
                        for k in range(KT):
                            h = t.matmul(pp[:, 0:Tc],
                                         wsb[nm][:, k * D + m * 128:k * D + m * 128 + 128],
                                         hsl(k), start=(k == 0), stop=(k == KT - 1))
                        n_pe = tinc(h, pe)
                        s.wait_ge(pe.h, n_pe)
                        dst = qraw if qk else vsb
                        h = s.activation(dst[:, m * Tc:(m + 1) * Tc], pp[:, 0:Tc],
                                         AF.Identity, bias=bsb[nm][:, m:m + 1])
                        n_act = tinc(h, act)
                        ev_hist.append(n_act)

                    if qk:
                        sq_pe = []
                        for m in range(KT):
                            qs = qsq[:, (m % 2) * Tc:(m % 2) * Tc + Tc]
                            v.wait_ge(act.h, ev_hist[m])
                            if m >= 2:
                                v.wait_ge(pe.h, sq_pe[m - 2])
                            h = v.tensor_mul(qs, qraw[:, m * Tc:(m + 1) * Tc],
                                             qraw[:, m * Tc:(m + 1) * Tc])
                            n_dve = tinc(h, dve)
                            t.wait_ge(dve.h, n_dve)
                            h = t.matmul(PS2[0:1, 0:Tc], ones_col[:], qs,
                                         start=(m == 0), stop=(m == KT - 1))
                            sq_pe.append(tinc(h, pe))
                        s.wait_ge(pe.h, sq_pe[-1])
                        h = s.activation(rstdq[0:1, 0:Tc], PS2[0:1, 0:Tc], AF.Ln,
                                         bias=eps6[0:1, 0:1], scale=1.0 / D)
                        chain(s, act, h)
                        h = s.activation(rstdq[0:1, 0:Tc], rstdq[0:1, 0:Tc],
                                         AF.Exp, scale=-0.5)
                        n_act = tinc(h, act)
                        t.wait_ge(act.h, n_act)
                        h = t.matmul(PS2[:, 0:Tc], ones_row[:], rstdq[0:1, 0:Tc],
                                     start=True, stop=True)
                        n_pe = tinc(h, pe)
                        v.wait_ge(pe.h, n_pe)
                        v.wait_ge(stq.h, stq.n)
                        for m in range(KT):
                            h = v.scalar_tensor_tensor(
                                qhat[:, m * Tc:(m + 1) * Tc],
                                qraw[:, m * Tc:(m + 1) * Tc],
                                nwsb[nm][:, m:m + 1], PS2[:, 0:Tc],
                                ALU.mult, ALU.mult)
                        n_dve = tinc(h, dve)
                        t.wait_ge(dve.h, n_dve)
                        v.wait_ge(dve.h, n_dve)
                        sq_pe = []
                        for m in range(KT):
                            qs = qsq[:, (m % 2) * Tc:(m % 2) * Tc + Tc]
                            if m >= 2:
                                v.wait_ge(pe.h, sq_pe[m - 2])
                            h = v.tensor_mul(qs, qhat[:, m * Tc:(m + 1) * Tc],
                                             qhat[:, m * Tc:(m + 1) * Tc])
                            n_dve = tinc(h, dve)
                            t.wait_ge(dve.h, n_dve)
                            h = t.matmul(PS2[0:NH, 0:Tc],
                                         hindsb[:, m * NH:(m + 1) * NH], qs,
                                         start=(m == 0), stop=(m == KT - 1))
                            sq_pe.append(tinc(h, pe))
                        v.wait_ge(pe.h, sq_pe[-1])
                        h = v.tensor_reduce(cmax[:], PS2[0:NH, 0:Tc], AX.X,
                                            ALU.max)
                        chain(v, dve, h)
                        h = v.tensor_max(vmax[nm][:], vmax[nm][:], cmax[:])
                        n_dve = tinc(h, dve)
                        t.wait_ge(dve.h, n_dve)
                        sy.wait_ge(dve.h, n_dve)
                        h = sy.dma_start(
                            out=(qh if nm == "q" else kh)[:, c * Tc:(c + 1) * Tc]
                            .rearrange("(k p) tc -> p k tc", p=128),
                            in_=qhat[:].rearrange("p (k tc) -> p k tc", k=KT))
                        tinc(h, stq, 16)
                    else:
                        tr_hist = []
                        for m in range(KT):
                            for tt in range(ntt):
                                i = m * ntt + tt
                                bank = PB if i % 2 == 0 else PB2
                                rot = (i // 2) % 4
                                if i >= 2:
                                    t.wait_ge(dve.h, tr_hist[i - 2])
                                t.wait_ge(act.h, ev_hist[m])
                                h = t.transpose(
                                    bank[:, rot * 128:(rot + 1) * 128],
                                    vsb[:, m * Tc + tt * 128:m * Tc + tt * 128 + 128],
                                    id16[:])
                                n_pe = tinc(h, pe)
                                v.wait_ge(pe.h, n_pe)
                                h = v.tensor_copy(
                                    vtsb[:, tt * D + m * 128:tt * D + (m + 1) * 128],
                                    bank[:, rot * 128:(rot + 1) * 128])
                                tr_hist.append(tinc(h, dve))
                        sy.wait_ge(dve.h, tr_hist[-1])
                        h = sy.dma_start(
                            out=vtok[c * Tc:(c + 1) * Tc, :]
                            .rearrange("(tt p) d -> p tt d", p=128),
                            in_=vtsb[:].rearrange("p (tt d) -> p tt d", d=D))
                        tinc(h, stq, 16)
            emit_barrier(p, allsems)

        h = v.tensor_mul(bprod[:], vmax["q"][:], vmax["k"][:])
        chain(v, dve, h)
        n_dve = dve.n
        s.wait_ge(dve.h, n_dve)
        h = s.activation(bprod[:], bprod[:], AF.Ln)
        chain(s, act, h)
        h = s.activation(bprod[:], bprod[:], AF.Exp, scale=0.5,
                         bias=ln0125[:, 0:1])
        n_act = tinc(h, act)
        v.wait_ge(act.h, n_act)
        h = v.tensor_scalar(negb[:], bprod[:], -1.0, None, ALU.mult)
        chain(v, dve, h)
        h = v.tensor_copy(negbw[:], negb[:].broadcast_to([16, 128]))
        n_dve = tinc(h, dve)
        t.wait_ge(dve.h, n_dve)
        h = t.transpose(PS2[:, 0:16], negbw[:], id32[0:16, 0:16])
        n_pe = tinc(h, pe)
        v.wait_ge(pe.h, n_pe)
        h = v.tensor_copy(negbc[:], PS2[:, 0:16])
        n_dve_b = tinc(h, dve)
        s.wait_ge(dve.h, n_dve_b)

        with p.scope():
            SAb = [p.ps("SA0", [128, 512], F32), p.ps("SA1", [128, 512], F32)]
            SBb = [p.ps("SB0", [128, 512], F32), p.ps("SB1", [128, 512], F32)]
            PVa = p.ps("PVa", [128, 512], F32)
            PVb = p.ps("PVb", [128, 512], F32)
            khh2 = [p.sb(f"khh{i}", [128, Tfull], BF16) for i in range(2)]
            qhh2 = [p.sb(f"qhh{i}", [128, Tfull], BF16) for i in range(2)]
            vhh2 = [p.sb(f"vhh{i}", [128, NKT * 130], BF16) for i in range(2)]
            pTa = p.sb("pTa", [128, NKT * 512], BF16)
            pTb = p.sb("pTb", [128, NKT * 512], BF16)
            rsum2 = [p.sb("rsumA", [128, 512], F32),
                     p.sb("rsumB", [128, 512], F32)]
            rb_sb2 = [p.sb("rb_sbA", [64, 512], F32),
                      p.sb("rb_sbB", [64, 512], F32)]
            ctx_sb = [p.sb(f"ctx_sb{i}", [64, 512], BF16) for i in range(2)]
            for b_ in range(2):
                v.memset(vhh2[b_][:].rearrange("p (kt c) -> p kt c", c=130)
                         [:, :, 64:65], 1.0)
                h = v.memset(vhh2[b_][:].rearrange("p (kt c) -> p kt c", c=130)
                             [:, :, 129:130], 1.0)
            n_ones = tinc(h, dve)

            NHP = NH // 2
            s_done = {}
            pv_done = {}
            ctx_stq = {}
            rb_dve = [0, 0]
            sy.wait_ge(stq.h, stq.n)

            def issue_pair_loads(hp):
                par = hp % 2
                h = sy.dma_start(out=khh2[par][:],
                                 in_=kh[128 * hp:128 * hp + 128, :])
                tinc(h, ld, 16)
                h = sy.dma_start(out=qhh2[par][:],
                                 in_=qh[128 * hp:128 * hp + 128, :])
                tinc(h, ld, 16)
                h = sy.dma_start(
                    out=vhh2[par][:].rearrange("p (kt c) -> p kt c", c=130)
                    [:, :, 0:64],
                    in_=vtok[:, 128 * hp:128 * hp + 64]
                    .rearrange("(kt p) c -> p kt c", p=128))
                tinc(h, ld, 16)
                h = sy.dma_start(
                    out=vhh2[par][:].rearrange("p (kt c) -> p kt c", c=130)
                    [:, :, 65:129],
                    in_=vtok[:, 128 * hp + 64:128 * hp + 128]
                    .rearrange("(kt p) c -> p kt c", p=128))
                return tinc(h, ld, 16)

            pair_ld = {0: issue_pair_loads(0)}
            n_cpar = 0
            warA = None
            for hp in range(NHP):
                par = hp % 2
                khh = khh2[par]
                qhh = qhh2[par]
                vhh = vhh2[par]
                h0, h1 = 2 * hp, 2 * hp + 1
                t.wait_ge(ld.h, pair_ld[hp])
                s.wait_ge(ld.h, pair_ld[hp])
                if hp == 0:
                    t.wait_ge(dve.h, n_ones)
                    s.wait_ge(dve.h, n_dve_b)
                for qc in range(NQC):
                    exp_ids = []
                    for kt in range(NKT):
                        par2 = kt % 2
                        SAp, SBp = SAb[par2], SBb[par2]
                        if kt >= 2:
                            t.wait_ge(act.h, exp_ids[kt - 2][1])
                        elif warA is not None:
                            t.wait_ge(act.h, warA[kt][1])
                        h = t.matmul(SAp[:, 0:512],
                                     khh[0:64, kt * 128:(kt + 1) * 128],
                                     qhh[0:64, qc * 512:(qc + 1) * 512],
                                     start=True, stop=True,
                                     tile_position=(0, 0))
                        n_pe_a = tinc(h, pe)
                        h = t.matmul(SBp[:, 0:512],
                                     khh[64:128, kt * 128:(kt + 1) * 128],
                                     qhh[64:128, qc * 512:(qc + 1) * 512],
                                     start=True, stop=True,
                                     tile_position=(64, 0))
                        n_pe_b = tinc(h, pe)
                        if kt == NKT - 1:
                            s_done[hp] = n_pe_b
                        if kt >= 1:
                            ia, ib = exp_ids[kt - 1]
                            t.wait_ge(act.h, ia)
                            if kt == 1:
                                t.wait_ge(dve.h, rb_dve[0])
                            h = t.matmul(PVa[0:65, 0:512],
                                         vhh[:, (kt - 1) * 130:
                                             (kt - 1) * 130 + 65],
                                         pTa[:, (kt - 1) * 512:kt * 512],
                                         start=(kt == 1), stop=False)
                            tinc(h, pe)
                            t.wait_ge(act.h, ib)
                            if kt == 1:
                                t.wait_ge(dve.h, rb_dve[1])
                            h = t.matmul(PVb[0:65, 0:512],
                                         vhh[:, (kt - 1) * 130 + 65:
                                             (kt - 1) * 130 + 130],
                                         pTb[:, (kt - 1) * 512:kt * 512],
                                         start=(kt == 1), stop=False)
                            tinc(h, pe)
                        s.wait_ge(pe.h, n_pe_a)
                        h = s.activation(pTa[:, kt * 512:(kt + 1) * 512],
                                         SAp[:, 0:512], AF.Exp,
                                         bias=negbc[:, h0:h0 + 1], scale=0.125)
                        na = tinc(h, act)
                        s.wait_ge(pe.h, n_pe_b)
                        h = s.activation(pTb[:, kt * 512:(kt + 1) * 512],
                                         SBp[:, 0:512], AF.Exp,
                                         bias=negbc[:, h1:h1 + 1], scale=0.125)
                        nb = tinc(h, act)
                        exp_ids.append((na, nb))
                    warA = exp_ids[-2:]
                    ia, ib = exp_ids[-1]
                    kt = NKT - 1
                    t.wait_ge(act.h, ia)
                    h = t.matmul(PVa[0:65, 0:512],
                                 vhh[:, kt * 130:kt * 130 + 65],
                                 pTa[:, kt * 512:(kt + 1) * 512],
                                 start=False, stop=True)
                    n_pva = tinc(h, pe)
                    t.wait_ge(act.h, ib)
                    h = t.matmul(PVb[0:65, 0:512],
                                 vhh[:, kt * 130 + 65:kt * 130 + 130],
                                 pTb[:, kt * 512:(kt + 1) * 512],
                                 start=False, stop=True)
                    n_pvb = tinc(h, pe)
                    pv_done[hp] = n_pvb
                    for hi_, (bank, n_pv) in enumerate([(PVa, n_pva),
                                                        (PVb, n_pvb)]):
                        rs = rsum2[hi_]
                        v.wait_ge(pe.h, n_pv)
                        h = v.reciprocal(rs[64:65, 0:512], bank[64:65, 0:512])
                        n_rcp = tinc(h, dve)
                        t.wait_ge(dve.h, n_rcp)
                        t.wait_ge(dve.h, max(rb_dve))
                        h = t.matmul(PS2[0:64, 0:512], onesf[64:65, 0:64],
                                     rs[64:65, 0:512], start=True, stop=True)
                        n_pe2 = tinc(h, pe)
                        v.wait_ge(pe.h, n_pe2)
                        rb_sb = rb_sb2[hi_]
                        h = v.tensor_copy(rb_sb[:], PS2[0:64, 0:512])
                        chain(v, dve, h)
                        cpar = n_cpar % 2
                        if cpar in ctx_stq:
                            v.wait_ge(stc[cpar].h, ctx_stq[cpar])
                        h = v.tensor_mul(ctx_sb[cpar][:], bank[0:64, 0:512],
                                         rb_sb[:])
                        rb_dve[hi_] = tinc(h, dve)
                        sy.wait_ge(dve.h, rb_dve[hi_])
                        hh_ = 2 * hp + hi_
                        h = sy.dma_start(
                            out=ctxh[64 * hh_:64 * hh_ + 64,
                                     qc * 512:(qc + 1) * 512],
                            in_=ctx_sb[cpar][:])
                        ctx_stq[cpar] = tinc(h, stc[cpar], 16)
                        n_cpar += 1
                if hp + 1 < NHP:
                    sy.wait_ge(pe.h, s_done[hp])
                    sy.wait_ge(pe.h, pv_done[hp])
                    pair_ld[hp + 1] = issue_pair_loads(hp + 1)
            emit_barrier(p, allsems)

        with p.scope():
            PSW = [p.ps(f"PW3{i}", [128, 1024], F32) for i in range(2)]
            ROT4 = [PSW[0][:, 0:512], PSW[0][:, 512:1024],
                    PSW[1][:, 0:512], PSW[1][:, 512:1024]]
            ccs2 = [qhat, p.sb("ccsB", [128, KT * Tc], BF16)]
            sx2 = [p.sem("sx20"), p.sem("sx21")]
            wsbo = p.sb("wsbo", [128, KT * D], BF16)
            h = g.dma_start(out=wsbo[:].rearrange("p (k m) -> p k m", k=KT),
                            in_=woT[:].rearrange("(k p) m -> p k m", p=128))
            n_gld = tinc(h, gld, 16)
            t.wait_ge(gld.h, n_gld)
            mm_hist = {}

            def issue_x3(c2):
                par2 = c2 % 2
                h2 = sy.dma_start(
                    out=xc2g[par2][:].rearrange("p (k tc) -> p k tc", k=KT),
                    in_=xin[:, c2 * Tc:(c2 + 1) * Tc]
                    .rearrange("(k p) tc -> p k tc", p=128))
                tinc(h2, ld, 16)
                h2 = sy.dma_start(
                    out=ccs2[par2][:].rearrange("p (k tc) -> p k tc", k=KT),
                    in_=ctxh[:, c2 * Tc:(c2 + 1) * Tc]
                    .rearrange("(k p) tc -> p k tc", p=128))
                return tinc(h2, ld, 16)

            x3n = {0: issue_x3(0)}
            for c in range(n_chunks):
                par = c % 2
                xc3 = xc2g[par]
                ccs = ccs2[par]
                if c + 1 < n_chunks:
                    if c >= 1:
                        sy.wait_ge(sx2[(c + 1) % 2].h, sx2[(c + 1) % 2].n)
                        sy.wait_ge(pe.h, mm_hist[c - 1])
                    x3n[c + 1] = issue_x3(c + 1)
                n_cl = x3n[c]
                t.wait_ge(ld.h, n_cl)
                v.wait_ge(ld.h, n_cl)
                ev_hist = []
                for m in range(KT):
                    pp = ROT4[m % 4]
                    if m >= 4:
                        t.wait_ge(dve.h, ev_hist[m - 4])
                    for k in range(KT):
                        h = t.matmul(pp[:, 0:Tc],
                                     wsbo[:, k * D + m * 128:k * D + m * 128 + 128],
                                     ccs[:, k * Tc:(k + 1) * Tc],
                                     start=(k == 0), stop=(k == KT - 1))
                    n_pe = tinc(h, pe)
                    v.wait_ge(pe.h, n_pe)
                    h = v.scalar_tensor_tensor(xc3[:, m * Tc:(m + 1) * Tc],
                                               pp[:, 0:Tc],
                                               bsb["o"][:, m:m + 1],
                                               xc3[:, m * Tc:(m + 1) * Tc],
                                               ALU.add, ALU.add)
                    ev_hist.append(tinc(h, dve))
                mm_hist[c] = n_pe
                sy.wait_ge(dve.h, ev_hist[-1])
                h = sy.dma_start(out=xout[:, c * Tc:(c + 1) * Tc]
                                 .rearrange("(k p) tc -> p k tc", p=128),
                                 in_=xc3[:].rearrange("p (k tc) -> p k tc", k=KT))
                tinc(h, sx2[par], 16)

        g.wait_ge(sx2[0].h, sx2[0].n)
        g.wait_ge(sx2[1].h, sx2[1].n)
    return nc


def _prep_attn0_weights(Wq, bq, Wk, bk, Wv, bv, Wo, bo, nqw, nkw):
    def colsplit(b):
        return np.ascontiguousarray(b.reshape(KT, 128).T)
    NH = 16
    hind = np.zeros((128, KT * NH), dtype=np.float32)
    for k in range(KT):
        hind[0:64, k * NH + 2 * k] = 1.0
        hind[64:128, k * NH + 2 * k + 1] = 1.0
    return dict(
        wqT=np.ascontiguousarray(Wq.T), wkT=np.ascontiguousarray(Wk.T),
        wvT=np.ascontiguousarray(Wv.T), woT=np.ascontiguousarray(Wo.T),
        bq=colsplit(bq), bk=colsplit(bk), bv=colsplit(bv), bo=colsplit(bo),
        nqw=colsplit(nqw), nkw=colsplit(nkw), hind=hind,
        idm=np.eye(128, dtype=np.float32),
        idm16=np.eye(128, dtype=BF))




def build_ffn1(Tfull=T, Tc=512, debug=False, fuse=None):
    assert not (debug and fuse)
    """L1 spiking/MoE ffn sublayer: xout = x + 0.5*ffn1(LN(x))."""
    E = 8
    HID = 2048
    MD = 64
    HK2 = 16
    n_chunks = Tfull // Tc
    ntt = Tc // 128

    nc = fuse["nc"] if fuse else bass.Bass("TRN2", target_bir_lowering=False)
    xin = _fdt(nc, fuse, "xin", [D, Tfull], F32, "ExternalInput")
    enc_w = _fdt(nc, fuse, "enc_w", [D, HID], F32, "ExternalInput")
    s2c_w = _fdt(nc, fuse, "s2c_w", [HID, MD], F32, "ExternalInput")
    r_w1 = _fdt(nc, fuse, "r_w1", [MD, 64], F32, "ExternalInput")
    r_w2 = _fdt(nc, fuse, "r_w2", [64, E], F32, "ExternalInput")
    e_w1 = _fdt(nc, fuse, "e_w1", [E, MD, 1024], F32, "ExternalInput")
    e_w2 = _fdt(nc, fuse, "e_w2", [E, 1024, MD], F32, "ExternalInput")
    c2s_w = _fdt(nc, fuse, "c2s_w", [MD, HID], F32, "ExternalInput")
    dec_w = _fdt(nc, fuse, "dec_w", [HID, D], F32, "ExternalInput")
    enc_b = _fdt(nc, fuse, "enc_b", [128, HK2], F32, "ExternalInput")
    s2c_b = _fdt(nc, fuse, "s2c_b", [MD, 1], F32, "ExternalInput")
    r_b1 = _fdt(nc, fuse, "r_b1", [64, 1], F32, "ExternalInput")
    r_b2 = _fdt(nc, fuse, "r_b2", [E, 1], F32, "ExternalInput")
    e_b1 = _fdt(nc, fuse, "e_b1", [128, E * KT], F32, "ExternalInput")
    e_b2 = _fdt(nc, fuse, "e_b2", [MD, E], F32, "ExternalInput")
    c2s_b = _fdt(nc, fuse, "c2s_b", [128, HK2], F32, "ExternalInput")
    dec_b = _fdt(nc, fuse, "dec_b", [128, KT], F32, "ExternalInput")
    ln_g = _fdt(nc, fuse, "ln_g", [128, KT], F32, "ExternalInput")
    ln_b = _fdt(nc, fuse, "ln_b", [128, KT], F32, "ExternalInput")
    idm = _fdt(nc, fuse, "idm", [128, 128], F32, "ExternalInput")
    ind8 = _fdt(nc, fuse, "ind8", [E, E * 128], F32, "ExternalInput")
    xout = _fdt(nc, fuse, "xout", [D, Tfull], F32, "ExternalOutput")
    dbg = {}
    if debug:
        for nm, shp, dt_ in [("d_hc", [D, 512], BF16), ("d_spk", [HID, 512], BF16),
                             ("d_cont", [MD, 512], BF16), ("d_lg", [E, 512], F32),
                             ("d_combT", [E, 512], F32), ("d_moe", [MD, 512], F32),
                             ("d_rates", [HID, 512], BF16), ("d_dec", [D, 512], F32),
                             ("d_eh0", [1024, 512], BF16)]:
            dbg[nm] = nc.dram_tensor(nm, shp, dt_, kind="ExternalOutput").ap()

    with _prog_ctx(nc, fuse) as p:
        t, s, v, g, sy = p.tensor, p.scalar, p.vector, p.gpsimd, p.sync

        xc2 = [p.sb("xcA", [128, KT * Tc], F32), p.sb("xcB", [128, KT * Tc], F32)]
        lnscr = p.sb("lnscr", [128, Tc], F32)
        lnscr2 = p.sb("lnscr2", [128, Tc], F32)
        hc = p.sb("hc", [128, KT * Tc], BF16)
        spikes = p.sb("spikes", [128, HK2 * Tc], BF16)
        cont16 = p.sb("cont16", [MD, Tc], BF16)
        rh16 = p.sb("rh16", [64, Tc], BF16)
        logits = p.sb("logits", [E, Tc], F32)
        ltok = p.sb("ltok", [128, ntt * E], F32)
        combtok = p.sb("combtok", [128, ntt * E], F32)
        combT = p.sb("combT", [E, Tc], F32)
        combb = p.sb("combb", [MD, E * Tc], BF16)
        eh16 = hc
        tmps = p.sb("tmps", [MD, Tc], F32)
        moe = p.sb("moe", [MD, Tc], F32)
        moe16 = p.sb("moe16", [MD, Tc], BF16)
        rates16 = spikes
        decsbf = p.sb("decsbf", [128, KT * Tc], F32)
        sorted8 = p.sb("sorted8", [128, E], F32)
        negs1 = p.sb("negs1", [128, 1], F32)
        mask8 = p.sb("mask8", [128, E], F32)
        eall = p.sb("eall", [128, E], F32)
        zz = p.sb("zz", [128, E], F32)
        den = p.sb("den", [128, 1], F32)
        rec = p.sb("rec", [128, 1], F32)
        id32 = p.sb("id32", [128, 128], F32)
        ind8sb = p.sb("ind8sb", [E, E * 128], F32)
        wenc = p.sb("wenc", [128, KT * HID], BF16)
        wdec = p.sb("wdec", [128, HK2 * D], BF16)
        ws2c = p.sb("ws2c", [128, HK2 * MD], BF16)
        wc2s = p.sb("wc2s", [MD, HID], BF16)
        wr1 = p.sb("wr1", [MD, 64], BF16)
        wr2 = p.sb("wr2", [64, E], BF16)
        wew1 = p.sb("wew1", [MD, E * 1024], BF16)
        wew2 = p.sb("wew2", [128, E * KT * MD], BF16)
        b_enc = p.sb("b_enc", [128, HK2], F32)
        b_s2c = p.sb("b_s2c", [MD, 1], F32)
        b_r1 = p.sb("b_r1", [64, 1], F32)
        b_r2 = p.sb("b_r2", [E, 1], F32)
        b_e1 = p.sb("b_e1", [128, E * KT], F32)
        b_e2 = p.sb("b_e2", [MD, E], F32)
        b_c2s = p.sb("b_c2s", [128, HK2], F32)
        b_dec = p.sb("b_dec", [128, KT], F32)
        g_ln = p.sb("g_ln", [128, KT], F32)
        bb_ln = p.sb("bb_ln", [128, KT], F32)
        st = {n: p.sb("st_" + n, [1, 512], F32)
              for n in ("ex", "ex2", "rstd", "brow")}
        ones_col = p.sb("ones_col", [128, 1], F32)
        ones_row = p.sb("ones_row", [1, 128], F32)
        eps1 = p.sb("eps1", [1, 1], F32)
        consts = {"ones_col": ones_col, "ones_row": ones_row, "eps1": eps1}

        PS = [p.ps(f"P{i}", [128, 512], F32) for i in range(8)]

        ld = p.sem("ld")
        xld2 = [p.sem("xldA"), p.sem("xldB")]
        stx2 = [p.sem("stxA"), p.sem("stxB")]
        gld = p.sem("gld")
        pe = p.sem("pe")
        act = p.sem("act")
        dve = p.sem("dve")
        stx = p.sem("stx")
        sems = {"pe": pe, "act": act, "dve": dve}

        v.memset(ones_col[:], 1.0)
        v.memset(ones_row[:], 1.0)
        h = v.memset(eps1[:], 1e-5)
        n_const = tinc(h, dve)

        for ap_, sbuf in [(enc_b, b_enc), (s2c_b, b_s2c), (r_b1, b_r1),
                          (r_b2, b_r2), (e_b1, b_e1), (e_b2, b_e2),
                          (c2s_b, b_c2s), (dec_b, b_dec), (ln_g, g_ln),
                          (ln_b, bb_ln), (idm, id32), (ind8, ind8sb)]:
            h = sy.dma_start(out=sbuf[:], in_=ap_[:])
            n_ld0 = tinc(h, ld, 16)
        h = g.dma_start(out=wenc[:].rearrange("p (k m) -> p k m", k=KT),
                        in_=enc_w[:].rearrange("(k p) m -> p k m", p=128))
        tinc(h, gld, 16)
        h = g.dma_start(out=wdec[:].rearrange("p (k m) -> p k m", k=HK2),
                        in_=dec_w[:].rearrange("(k p) m -> p k m", p=128))
        tinc(h, gld, 16)
        h = g.dma_start(out=ws2c[:].rearrange("p (k m) -> p k m", k=HK2),
                        in_=s2c_w[:].rearrange("(k p) m -> p k m", p=128))
        tinc(h, gld, 16)
        h = g.dma_start(out=wc2s[:], in_=c2s_w[:])
        tinc(h, gld, 16)
        h = g.dma_start(out=wr1[:], in_=r_w1[:])
        tinc(h, gld, 16)
        h = g.dma_start(out=wr2[:], in_=r_w2[:])
        tinc(h, gld, 16)
        h = g.dma_start(out=wew1[:].rearrange("p (e m) -> p e m", e=E),
                        in_=e_w1[:, :, :].rearrange("e p m -> p e m"))
        tinc(h, gld, 16)
        h = g.dma_start(out=wew2[:].rearrange("p (e k m) -> p e k m", e=E, k=KT),
                        in_=e_w2[:, :, :].rearrange("e (k p) m -> p e k m", p=128))
        n_gld = tinc(h, gld, 16)

        for eng in (t, s, v):
            eng.wait_ge(gld.h, n_gld)
            eng.wait_ge(ld.h, n_ld0)
            eng.wait_ge(dve.h, n_const)

        sqsl = lambda k: lnscr[:]
        sqsl2 = lambda k: lnscr2[:]
        hsl = lambda k: hc[:, k * Tc:(k + 1) * Tc]
        decsl = lambda k: decsbf[:, k * Tc:(k + 1) * Tc]

        def issue_xload(c):
            par = c % 2
            h = sy.dma_start(out=xc2[par][:].rearrange("p (k tc) -> p k tc", k=KT),
                             in_=xin[:, c * Tc:(c + 1) * Tc]
                             .rearrange("(k p) tc -> p k tc", p=128))
            return tinc(h, xld2[par], 16)

        xld_n = {0: issue_xload(0)}
        resid_dve = 0
        for c in range(n_chunks):
            par = c % 2
            xc = xc2[par]
            xsl = lambda k: xc[:, k * Tc:(k + 1) * Tc]
            if c + 1 < n_chunks:
                if c >= 1:
                    sy.wait_ge(stx2[(c + 1) % 2].h, stx2[(c + 1) % 2].n)
                xld_n[c + 1] = issue_xload(c + 1)
            n_x = xld_n[c]
            for eng in (s, t, v):
                eng.wait_ge(xld2[par].h, n_x)
            n_pe_ln = emit_ln_stats(p, sems, xsl, sqsl, consts,
                                    PS[0], PS[1], PS[2], PS[3], Tc, st)
            v.wait_ge(pe.h, n_pe_ln)
            for k in range(KT):
                h = v.tensor_mul(lnscr[:], xsl(k), PS[2][:, 0:Tc])
                chain(v, dve, h)
                h = v.tensor_add(hsl(k), lnscr[:], PS[3][:, 0:Tc])
                chain(v, dve, h)
            n_h = dve.n
            t.wait_ge(dve.h, n_h)

            enc_hist = []
            for m in range(HK2):
                pp = PS[m % 4]
                if m >= 4:
                    t.wait_ge(act.h, enc_hist[m - 4])
                for k in range(KT):
                    h = t.matmul(pp[:, 0:Tc],
                                 wenc[:, k * HID + m * 128:k * HID + m * 128 + 128],
                                 hsl(k), start=(k == 0), stop=(k == KT - 1))
                n_pe = tinc(h, pe)
                s.wait_ge(pe.h, n_pe)
                h = s.activation(spikes[:, m * Tc:(m + 1) * Tc], pp[:, 0:Tc],
                                 AF.Sigmoid, bias=b_enc[:, m:m + 1])
                n_act = tinc(h, act)
                enc_hist.append(n_act)
            t.wait_ge(act.h, enc_hist[-1])

            for k in range(HK2):
                h = t.matmul(PS[4][0:MD, 0:Tc],
                             ws2c[:, k * MD:(k + 1) * MD],
                             spikes[:, k * Tc:(k + 1) * Tc],
                             start=(k == 0), stop=(k == HK2 - 1))
            n_pe = tinc(h, pe)
            s.wait_ge(pe.h, n_pe)
            h = s.activation(cont16[:], PS[4][0:MD, 0:Tc], AF.Identity,
                             bias=b_s2c[:, 0:1])
            n_act = tinc(h, act)
            t.wait_ge(act.h, n_act)

            h = t.matmul(PS[5][0:64, 0:Tc], wr1[:], cont16[:],
                         start=True, stop=True)
            n_pe = tinc(h, pe)
            s.wait_ge(pe.h, n_pe)
            h = s.activation(rh16[:], PS[5][0:64, 0:Tc], AF.Relu,
                             bias=b_r1[:, 0:1])
            n_act = tinc(h, act)
            t.wait_ge(act.h, n_act)
            h = t.matmul(PS[5][0:E, 0:Tc], wr2[:], rh16[:],
                         start=True, stop=True)
            n_pe = tinc(h, pe)
            s.wait_ge(pe.h, n_pe)
            h = s.activation(logits[:], PS[5][0:E, 0:Tc], AF.Identity,
                             bias=b_r2[:, 0:1])
            n_act = tinc(h, act)
            t.wait_ge(act.h, n_act)
            tr_cp = []
            for tt in range(ntt):
                bank = PS[5] if tt % 2 == 0 else PS[4]
                if tt >= 2:
                    t.wait_ge(dve.h, tr_cp[tt - 2])
                h = t.transpose(bank[0:128, 0:E],
                                logits[:, tt * 128:(tt + 1) * 128],
                                id32[0:E, 0:E])
                n_pe = tinc(h, pe)
                v.wait_ge(pe.h, n_pe)
                h = v.tensor_copy(ltok[:, tt * E:(tt + 1) * E], bank[0:128, 0:E])
                tr_cp.append(tinc(h, dve))
            n_dve = tr_cp[-1]
            for tt in range(ntt):
                sco = ltok[:, tt * E:(tt + 1) * E]
                h = v.max(sorted8[:], sco)
                chain(v, dve, h)
                h = v.tensor_scalar(negs1[:], sorted8[:, 0:1], -1.0, None, ALU.mult)
                n_dve = tinc(h, dve)
                s.wait_ge(dve.h, n_dve)
                h = s.activation(eall[:], sco, AF.Exp, bias=negs1[:])
                n_act = tinc(h, act)
                h = v.tensor_scalar(mask8[:], sco, sorted8[:, 1:2], None, ALU.is_ge)
                chain(v, dve, h)
                v.wait_ge(act.h, n_act)
                h = v.tensor_mul(zz[:], eall[:], mask8[:])
                chain(v, dve, h)
                h = v.tensor_reduce(den[:], zz[:], AX.X, ALU.add)
                chain(v, dve, h)
                h = v.reciprocal(rec[:], den[:])
                chain(v, dve, h)
                h = v.tensor_scalar(combtok[:, tt * E:(tt + 1) * E], zz[:], rec[:],
                                    None, ALU.mult)
                n_dve = tinc(h, dve)
            t.wait_ge(dve.h, n_dve)
            ct_cp = []
            for tt in range(ntt):
                bank = PS[5] if tt % 2 == 0 else PS[4]
                if tt >= 2:
                    t.wait_ge(dve.h, ct_cp[tt - 2])
                h = t.transpose(bank[0:E, 0:128],
                                combtok[:, tt * E:(tt + 1) * E], id32[:])
                n_pe = tinc(h, pe)
                v.wait_ge(pe.h, n_pe)
                h = v.tensor_copy(combT[:, tt * 128:(tt + 1) * 128],
                                  bank[0:E, 0:128])
                ct_cp.append(tinc(h, dve))
            bc_cp = []
            for e in range(E):
                bank = PS[5] if e % 2 == 0 else PS[4]
                if e == 0:
                    t.wait_ge(dve.h, ct_cp[-1])
                elif e >= 2:
                    t.wait_ge(dve.h, bc_cp[e - 2])
                h = t.matmul(bank[0:MD, 0:Tc],
                             ind8sb[:, e * 128:e * 128 + MD], combT[:],
                             start=True, stop=True)
                n_pe = tinc(h, pe)
                v.wait_ge(pe.h, n_pe)
                h = v.tensor_copy(combb[:, e * Tc:(e + 1) * Tc], bank[0:MD, 0:Tc])
                bc_cp.append(tinc(h, dve))
            n_bcast_done = bc_cp[-1]

            h = v.memset(moe[:], 0.0)
            n_moe = tinc(h, dve)
            eh_hist = []
            eo_hist = []
            for e in range(E):
                for m in range(KT):
                    pp = PS[m % 4]
                    gi = e * KT + m
                    if gi >= 4:
                        t.wait_ge(act.h, eh_hist[gi - 4])
                    h = t.matmul(pp[:, 0:Tc],
                                 wew1[:, e * 1024 + m * 128:e * 1024 + m * 128 + 128],
                                 cont16[:], start=True, stop=True)
                    n_pe = tinc(h, pe)
                    s.wait_ge(pe.h, n_pe)
                    h = s.activation(eh16[:, m * Tc:(m + 1) * Tc], pp[:, 0:Tc],
                                     AF.Relu, bias=b_e1[:, e * KT + m:e * KT + m + 1])
                    n_act = tinc(h, act)
                    eh_hist.append(n_act)
                t.wait_ge(act.h, eh_hist[-1])
                peo = PS[4 + e % 2]
                if e == 0:
                    t.wait_ge(dve.h, n_bcast_done)
                if e >= 2:
                    t.wait_ge(dve.h, eo_hist[e - 2])
                for k in range(KT):
                    h = t.matmul(peo[0:MD, 0:Tc],
                                 wew2[:, (e * KT + k) * MD:(e * KT + k + 1) * MD],
                                 eh16[:, k * Tc:(k + 1) * Tc],
                                 start=(k == 0), stop=(k == KT - 1))
                n_pe = tinc(h, pe)
                v.wait_ge(pe.h, n_pe)
                v.wait_ge(dve.h, n_moe)
                h = v.scalar_tensor_tensor(tmps[:], peo[0:MD, 0:Tc],
                                           b_e2[:, e:e + 1],
                                           combb[:, e * Tc:(e + 1) * Tc],
                                           ALU.add, ALU.mult)
                chain(v, dve, h)
                h = v.tensor_add(moe[:], moe[:], tmps[:])
                n_moe = tinc(h, dve)
                eo_hist.append(n_moe)
            t.wait_ge(dve.h, n_moe)
            v.wait_ge(dve.h, n_moe)
            h = v.tensor_copy(moe16[:], moe[:])
            n_dve = tinc(h, dve)
            t.wait_ge(dve.h, n_dve)

            c2s_hist = []
            for m in range(HK2):
                pp = PS[m % 4]
                if m >= 4:
                    t.wait_ge(act.h, c2s_hist[m - 4])
                h = t.matmul(pp[:, 0:Tc],
                             wc2s[:, m * 128:(m + 1) * 128], moe16[:],
                             start=True, stop=True)
                n_pe = tinc(h, pe)
                s.wait_ge(pe.h, n_pe)
                h = s.activation(rates16[:, m * Tc:(m + 1) * Tc], pp[:, 0:Tc],
                                 AF.Sigmoid, bias=b_c2s[:, m:m + 1])
                n_act = tinc(h, act)
                c2s_hist.append(n_act)
            t.wait_ge(act.h, c2s_hist[-1])

            dec_hist = []
            for m in range(KT):
                pp = PS[m % 4]
                if m >= 4:
                    t.wait_ge(act.h, dec_hist[m - 4])
                for k in range(HK2):
                    h = t.matmul(pp[:, 0:Tc],
                                 wdec[:, k * D + m * 128:k * D + m * 128 + 128],
                                 rates16[:, k * Tc:(k + 1) * Tc],
                                 start=(k == 0), stop=(k == HK2 - 1))
                n_pe = tinc(h, pe)
                s.wait_ge(pe.h, n_pe)
                h = s.activation(decsl(m), pp[:, 0:Tc], AF.Sigmoid,
                                 bias=b_dec[:, m:m + 1])
                n_act = tinc(h, act)
                dec_hist.append(n_act)
            t.wait_ge(act.h, dec_hist[-1])

            if debug and c == 0:
                sy.wait_ge(dve.h, dve.n)
                sy.wait_ge(act.h, act.n)
                for nm, sb_, kk in [("d_hc", hc, KT), ("d_spk", spikes, HK2),
                                    ("d_cont", cont16, 1), ("d_lg", logits, 1),
                                    ("d_combT", combT, 1), ("d_moe", moe, 1),
                                    ("d_rates", rates16, HK2), ("d_dec", decsbf, KT),
                                    ("d_eh0", eh16, KT)]:
                    if kk == 1:
                        h = sy.dma_start(out=dbg[nm][:, 0:Tc], in_=sb_[:, 0:Tc])
                    else:
                        h = sy.dma_start(
                            out=dbg[nm][:, 0:Tc].rearrange("(k p) tc -> p k tc", p=128),
                            in_=sb_[:, 0:kk * Tc].rearrange("p (k tc) -> p k tc", k=kk))
                    tinc(h, stx, 16)
                sy.wait_ge(stx.h, stx.n)

            t.wait_ge(dve.h, resid_dve)
            s.wait_ge(dve.h, resid_dve)
            n_pe = emit_ln_stats(p, sems, decsl, sqsl2, consts,
                                 PS[4], PS[5], PS[6], PS[7], Tc, st)
            v.wait_ge(pe.h, n_pe)
            for k in range(KT):
                h = v.tensor_mul(lnscr2[:], decsl(k), PS[6][:, 0:Tc])
                chain(v, dve, h)
                h = v.tensor_add(lnscr2[:], lnscr2[:], PS[7][:, 0:Tc])
                chain(v, dve, h)
                h = v.tensor_scalar(lnscr2[:], lnscr2[:], g_ln[:, k:k + 1],
                                    bb_ln[:, k:k + 1], ALU.mult, ALU.add)
                chain(v, dve, h)
                h = v.scalar_tensor_tensor(xsl(k), lnscr2[:], 0.5, xsl(k),
                                           ALU.mult, ALU.add)
                n_dve = chain(v, dve, h)
            resid_dve = n_dve

            sy.wait_ge(dve.h, dve.n)
            h = sy.dma_start(out=xout[:, c * Tc:(c + 1) * Tc]
                             .rearrange("(k p) tc -> p k tc", p=128),
                             in_=xc[:].rearrange("p (k tc) -> p k tc", k=KT))
            tinc(h, stx2[par], 16)

        g.wait_ge(stx2[0].h, stx2[0].n)
        g.wait_ge(stx2[1].h, stx2[1].n)
    return nc


def _prep_ffn1_weights(enc_w, enc_b, s2c_w, s2c_b, r_w1, r_b1, r_w2, r_b2,
                       e_w1, e_b1, e_w2, e_b2, c2s_w, c2s_b, dec_w, dec_b,
                       ln_g, ln_b):
    E = 8
    ind8 = np.zeros((E, E * 128), dtype=np.float32)
    for e in range(E):
        ind8[e, e * 128:(e + 1) * 128] = 1.0
    return dict(
        enc_w=np.ascontiguousarray(enc_w),
        s2c_w=np.ascontiguousarray(s2c_w),
        r_w1=np.ascontiguousarray(r_w1), r_w2=np.ascontiguousarray(r_w2),
        e_w1=np.ascontiguousarray(e_w1), e_w2=np.ascontiguousarray(e_w2),
        c2s_w=np.ascontiguousarray(c2s_w), dec_w=np.ascontiguousarray(dec_w),
        enc_b=np.ascontiguousarray(enc_b.reshape(16, 128).T),
        s2c_b=np.ascontiguousarray(s2c_b.reshape(64, 1)),
        r_b1=np.ascontiguousarray(r_b1.reshape(64, 1)),
        r_b2=np.ascontiguousarray(r_b2.reshape(8, 1)),
        e_b1=np.ascontiguousarray(
            e_b1.reshape(E, KT, 128).transpose(2, 0, 1).reshape(128, E * KT)),
        e_b2=np.ascontiguousarray(e_b2.T),
        c2s_b=np.ascontiguousarray(c2s_b.reshape(16, 128).T),
        dec_b=np.ascontiguousarray(dec_b.reshape(8, 128).T),
        ln_g=np.ascontiguousarray(ln_g.reshape(8, 128).T),
        ln_b=np.ascontiguousarray(ln_b.reshape(8, 128).T),
        idm=np.eye(128, dtype=np.float32), ind8=ind8)




SCL4 = 0.3535533905932738


def _attn1_decls(nc, Tfull, colmax_only, fuse=None):
    E = {}
    E["xin"] = _fdt(nc, fuse, "xin", [D, Tfull], F32, "ExternalInput")
    for nm in ("wq", "wk", "wv"):
        if colmax_only and nm == "wv":
            continue
        E[nm] = _fdt(nc, fuse, nm, [D, D], F32, "ExternalInput")
    for nm in ("bq", "bk", "bv"):
        if colmax_only and nm == "bv":
            continue
        E[nm] = _fdt(nc, fuse, nm, [128, KT], F32, "ExternalInput")
    E["idm"] = _fdt(nc, fuse, "idm", [128, 128], F32, "ExternalInput")
    E["idm16"] = _fdt(nc, fuse, "idm16", [128, 128], BF16, "ExternalInput")
    if colmax_only:
        E["cmax_out"] = _fdt(nc, fuse, "cmax_out", [1, 1], F32,
                             "ExternalOutput")
    else:
        E["gmax8"] = _fdt(nc, fuse, "gmax8", [1, 8], F32, "ExternalInput")
        E["convB"] = _fdt(nc, fuse, "convB", [16, 3, 128, 128],
                          F32, "ExternalInput")
        E["qh"] = _fdt(nc, fuse, "qh_sc", [D, Tfull], BF16, "Internal")
        E["kh"] = _fdt(nc, fuse, "kh_sc", [D, Tfull], BF16, "Internal")
        E["vtok"] = _fdt(nc, fuse, "vtok_sc", [Tfull, D], BF16, "Internal")
        E["ctxh"] = _fdt(nc, fuse, "ctxh_sc", [D, Tfull], BF16, "Internal")
        E["xout"] = _fdt(nc, fuse, "xout", [D, Tfull], F32, "ExternalOutput")
    return E


def build_attn1(Tfull=T, Tc=512, colmax_only=False, cheap=False, fuse=None):
    NH = 16
    HD = 64
    LM = 64
    n_chunks = Tfull // Tc
    NKT = Tfull // 128
    NSC = Tfull // 512
    ntt = Tc // 128
    segw = Tfull // 64
    lm_per_chunk = Tc // segw

    nc = fuse["nc"] if fuse else bass.Bass("TRN2", target_bir_lowering=False)
    EE = _attn1_decls(nc, Tfull, colmax_only, fuse=fuse)

    with _prog_ctx(nc, fuse) as p:
        t, s, v, g, sy = p.tensor, p.scalar, p.vector, p.gpsimd, p.sync

        xc2a = [p.sb("xc", [128, KT * Tc], F32),
                p.sb("xcB", [128, KT * Tc], F32)]
        xc = xc2a[0]
        _p1_scope = p.push_scope()
        lnscr = p.sb("lnscr", [128, Tc], F32)
        hc = p.sb("hc", [128, KT * Tc], BF16)
        qhat = p.sb("qhat", [128, KT * Tc], BF16)
        wsb = {}
        p.ctx = p._scope_stack[-1]; p._depth -= 1; p._sb_left = True
        bsb = {}
        p.ctx = _p1_scope; p._depth += 1; p._sb_left = False
        for nm in (("wq", "wk") if colmax_only else ("wq", "wk", "wv")):
            wsb[nm] = p.sb("w_" + nm, [128, KT * D], BF16)
        p.ctx = p._scope_stack[-1]; p._depth -= 1; p._sb_left = True
        for nm in (("bq", "bk") if colmax_only else ("bq", "bk", "bv")):
            bsb[nm] = p.sb("b_" + nm, [128, KT], F32)
        bscl = {nm: p.sb("bs_" + nm, [128, KT], F32) for nm in ("bq", "bk")}
        id32 = p.sb("id32", [128, 128], F32)
        id16 = p.sb("id16", [128, 128], BF16)
        p.ctx = _p1_scope; p._depth += 1; p._sb_left = False
        qlsum = p.sb("qlsum", [128, KT * LM], F32)
        klsum = p.sb("klsum", [128, KT * LM], F32)
        qla = p.sb("qla", [128, KT * LM], BF16)
        kla = p.sb("kla", [128, KT * LM], BF16)
        p.ctx = p._scope_stack[-1]; p._depth -= 1; p._sb_left = True
        qla2 = p.sb("qla2", [64, NH * LM], BF16)
        kla2 = p.sb("kla2", [64, NH * LM], BF16)
        qla_dr = _fdt(nc, fuse, "qla_dr", [128, KT * LM], BF16, "Internal")
        kla_dr = _fdt(nc, fuse, "kla_dr", [128, KT * LM], BF16, "Internal")
        k2all = p.sb("k2all", [64, NH * 64], F32)
        k2e = p.sb("k2e", [64, 64], F32)
        nmax = p.sb("nmax", [64, 1], F32)
        ssum = p.sb("ssum", [64, 1], F32)
        srec = p.sb("srec", [64, 1], F32)
        cm16 = p.sb("cm16", [1, 16], F32)
        cs1 = p.sb("cs1", [1, 64], F32)
        cmax1 = p.sb("cmax1", [1, 1], F32)
        st = {n: p.sb("st_" + n, [1, 512], F32)
              for n in ("ex", "ex2", "rstd", "brow")}
        ones_col = p.sb("ones_col", [128, 1], F32)
        ones_row = p.sb("ones_row", [1, 128], F32)
        eps1 = p.sb("eps1", [1, 1], F32)
        consts = {"ones_col": ones_col, "ones_row": ones_row, "eps1": eps1}

        PS = [p.ps(f"P{i}", [128, 512], F32) for i in range(6)]
        PB = p.ps("PB", [128, 1024], BF16)
        PB2 = p.ps("PB2", [128, 1024], BF16)

        ld = p.sem("ld")
        gld = p.sem("gld")
        cvld2 = [p.sem("cvldA"), p.sem("cvldB")]
        hld2 = [p.sem("hldA"), p.sem("hldB")]
        stc2 = [p.sem("stcA2"), p.sem("stcB2")]
        pe = p.sem("pe")
        act = p.sem("act")
        dve = p.sem("dve")
        stq = p.sem("stq")
        stx = p.sem("stx")
        sems = {"pe": pe, "act": act, "dve": dve}

        v.memset(ones_col[:], 1.0)
        v.memset(ones_row[:], 1.0)
        v.memset(eps1[:], 1e-5)
        v.memset(qlsum[:], 0.0)
        h = v.memset(klsum[:], 0.0)
        n_const = tinc(h, dve)

        load_list = [(EE["idm"], id32), (EE["idm16"], id16),
                     (EE["bq"], bsb["bq"]), (EE["bk"], bsb["bk"])]
        if not colmax_only:
            load_list.append((EE["bv"], bsb["bv"]))
        for ap_, sbuf in load_list:
            h = sy.dma_start(out=sbuf[:], in_=ap_[:])
            n_ld0 = tinc(h, ld, 16)
        for nm in wsb:
            h = g.dma_start(out=wsb[nm][:].rearrange("p (k m) -> p k m", k=KT),
                            in_=EE[nm][:].rearrange("(k p) m -> p k m", p=128))
            n_gld = tinc(h, gld, 16)
        for eng in (t, s, v):
            eng.wait_ge(gld.h, n_gld)
            eng.wait_ge(dve.h, n_const)
        n_bs = None

        xsl = lambda k: xc[:, k * Tc:(k + 1) * Tc]
        sqsl = lambda k: lnscr[:]
        hsl = lambda k: hc[:, k * Tc:(k + 1) * Tc]

        if not colmax_only:
            p.ctx = _p1_scope; p._depth += 1; p._sb_left = False
            vsb = p.sb("vsb", [128, KT * Tc], BF16)
            vtsb = p.sb("vtsb", [128, ntt * D], BF16)
            p.ctx = p._scope_stack[-1]; p._depth -= 1; p._sb_left = True
        nh_hist1 = {}

        def issue_x1(c2):
            h2 = sy.dma_start(
                out=xc2a[c2 % 2][:].rearrange("p (k tc) -> p k tc", k=KT),
                in_=EE["xin"][:, c2 * Tc:(c2 + 1) * Tc]
                .rearrange("(k p) tc -> p k tc", p=128))
            return tinc(h2, ld, 16)

        if not (colmax_only and cheap):
            x1n = {0: issue_x1(0)}
        for c in (range(n_chunks) if not (colmax_only and cheap) else []):
            xc = xc2a[c % 2]
            xsl = lambda k, xc=xc: xc[:, k * Tc:(k + 1) * Tc]
            if c + 1 < n_chunks:
                if c >= 1:
                    sy.wait_ge(dve.h, nh_hist1[c - 1])
                x1n[c + 1] = issue_x1(c + 1)
            n_x = x1n[c]
            for eng in (s, t, v):
                eng.wait_ge(ld.h, n_x)
            if c > 0:
                s.wait_ge(dve.h, dve.n)
            if n_bs is None:
                v.tensor_scalar(bscl["bq"][:], bsb["bq"][:], SCL4, None, ALU.mult)
                h = v.tensor_scalar(bscl["bk"][:], bsb["bk"][:], SCL4, None,
                                    ALU.mult)
                n_bs = tinc(h, dve)
                s.wait_ge(dve.h, n_bs)
            n_pe_ln = emit_ln_stats(p, sems, xsl, sqsl, consts,
                                    PS[0], PS[1], PS[2], PS[3], Tc, st)
            v.wait_ge(pe.h, n_pe_ln)
            for k in range(KT):
                h = v.tensor_mul(lnscr[:], xsl(k), PS[2][:, 0:Tc])
                chain(v, dve, h)
                h = v.tensor_add(hsl(k), lnscr[:], PS[3][:, 0:Tc])
                chain(v, dve, h)
            n_h = dve.n
            nh_hist1[c] = n_h
            t.wait_ge(dve.h, n_h)

            projs = ["wq", "wk"] if colmax_only else ["wq", "wk", "wv"]
            ev_all = []
            for nm in projs:
                qk = nm in ("wq", "wk")
                if nm == "wk":
                    s.wait_ge(stq.h, stq.n)
                    s.wait_ge(dve.h, dve.n)
                for m in range(KT):
                    pp = PS[m % 2]
                    gi = len(ev_all)
                    if gi >= 2:
                        t.wait_ge(act.h, ev_all[gi - 2])
                    for k in range(KT):
                        h = t.matmul(pp[:, 0:Tc],
                                     wsb[nm][:, k * D + m * 128:k * D + m * 128 + 128],
                                     hsl(k), start=(k == 0), stop=(k == KT - 1))
                    n_pe = tinc(h, pe)
                    s.wait_ge(pe.h, n_pe)
                    dst = qhat if qk else vsb
                    h = s.activation(dst[:, m * Tc:(m + 1) * Tc], pp[:, 0:Tc],
                                     AF.Identity,
                                     bias=(bscl["b" + nm[1]][:, m:m + 1] if qk
                                           else bsb["bv"][:, m:m + 1]),
                                     scale=(SCL4 if qk else 1.0))
                    n_act = tinc(h, act)
                    ev_all.append(n_act)
                if qk:
                    lsum = qlsum if nm == "wq" else klsum
                    v.wait_ge(act.h, n_act)
                    for m in range(KT):
                        h = v.tensor_reduce(
                            lnscr[:, 0:lm_per_chunk],
                            qhat[:, m * Tc:(m + 1) * Tc]
                            .rearrange("p (l w) -> p l w", w=segw),
                            AX.X, ALU.add)
                        chain(v, dve, h)
                        h = v.tensor_add(
                            lsum[:, m * LM + c * lm_per_chunk:
                                 m * LM + (c + 1) * lm_per_chunk],
                            lsum[:, m * LM + c * lm_per_chunk:
                                 m * LM + (c + 1) * lm_per_chunk],
                            lnscr[:, 0:lm_per_chunk])
                        n_dve = chain(v, dve, h)
                    if not colmax_only:
                        sy.wait_ge(dve.h, n_dve)
                        h = sy.dma_start(
                            out=(EE["qh"] if nm == "wq" else EE["kh"])
                            [:, c * Tc:(c + 1) * Tc]
                            .rearrange("(k p) tc -> p k tc", p=128),
                            in_=qhat[:].rearrange("p (k tc) -> p k tc", k=KT))
                        tinc(h, stq, 16)
                else:
                    ev_wv = ev_all[-KT:]
                    for m in range(KT):
                        for tt in range(ntt):
                            if tt == 0:
                                t.wait_ge(act.h, ev_wv[m])
                            rot = (m * ntt + tt) % 4
                            h = t.transpose(
                                PB[:, rot * 128:(rot + 1) * 128],
                                vsb[:, m * Tc + tt * 128:m * Tc + tt * 128 + 128],
                                id16[:])
                            n_pe = tinc(h, pe)
                            v.wait_ge(pe.h, n_pe)
                            h = v.tensor_copy(
                                vtsb[:, tt * D + m * 128:tt * D + (m + 1) * 128],
                                PB[:, rot * 128:(rot + 1) * 128])
                            n_dve = tinc(h, dve)
                            t.wait_ge(dve.h, n_dve)
                    sy.wait_ge(dve.h, n_dve)
                    h = sy.dma_start(
                        out=EE["vtok"][c * Tc:(c + 1) * Tc, :]
                        .rearrange("(tt p) d -> p tt d", p=128),
                        in_=vtsb[:].rearrange("p (tt d) -> p tt d", d=D))
                    tinc(h, stq, 16)

        if colmax_only and cheap:
            xlsum = p.sb("xlsum", [128, KT * LM], F32)
            xlsb = p.sb("xlsb", [128, KT * LM], BF16)
            lnf = p.sb("lnf", [128, Tc], F32)
            red8 = p.sb("red8", [128, lm_per_chunk], F32)
            bscl64 = {nm: p.sb("bs64_" + nm, [128, KT], F32)
                      for nm in ("bq", "bk")}
            h = v.memset(xlsum[:], 0.0)
            chain(v, dve, h)
            v.wait_ge(ld.h, n_ld0)
            h = v.tensor_scalar(bscl64["bq"][:], bsb["bq"][:], SCL4 * segw,
                                None, ALU.mult)
            chain(v, dve, h)
            h = v.tensor_scalar(bscl64["bk"][:], bsb["bk"][:], SCL4 * segw,
                                None, ALU.mult)
            n_xl0 = tinc(h, dve)
            xcBc = p.sb("xcB_cm", [128, KT * Tc], F32)
            xc2c = [xc, xcBc]
            lnr_hist = {}

            def issue_xc(c2):
                h2 = sy.dma_start(
                    out=xc2c[c2 % 2][:].rearrange("p (k tc) -> p k tc", k=KT),
                    in_=EE["xin"][:, c2 * Tc:(c2 + 1) * Tc]
                    .rearrange("(k p) tc -> p k tc", p=128))
                return tinc(h2, ld, 16)

            cn = {0: issue_xc(0)}
            for c in range(n_chunks):
                xcc = xc2c[c % 2]
                xslc = lambda k, xcc=xcc: xcc[:, k * Tc:(k + 1) * Tc]
                if c + 1 < n_chunks:
                    if c >= 1:
                        sy.wait_ge(dve.h, lnr_hist[c - 1])
                    cn[c + 1] = issue_xc(c + 1)
                for eng in (s, t, v):
                    eng.wait_ge(ld.h, cn[c])
                if c > 0:
                    s.wait_ge(dve.h, dve.n)
                n_pe_ln = emit_ln_stats(p, sems, xslc, sqsl, consts,
                                        PS[0], PS[1], PS[2], PS[3], Tc, st)
                v.wait_ge(pe.h, n_pe_ln)
                if c == 0:
                    v.wait_ge(dve.h, n_xl0)
                for k in range(KT):
                    h = v.tensor_mul(lnf[:], xslc(k), PS[2][:, 0:Tc])
                    chain(v, dve, h)
                    h = v.tensor_add(lnf[:], lnf[:], PS[3][:, 0:Tc])
                    chain(v, dve, h)
                    h = v.tensor_reduce(red8[:],
                                        lnf[:].rearrange("p (l w) -> p l w",
                                                         w=segw),
                                        AX.X, ALU.add)
                    chain(v, dve, h)
                    sl = xlsum[:, k * LM + c * lm_per_chunk:
                               k * LM + (c + 1) * lm_per_chunk]
                    h = v.tensor_add(sl, sl, red8[:])
                    chain(v, dve, h)
                lnr_hist[c] = dve.n
            h = v.tensor_copy(xlsb[:], xlsum[:])
            n_cast = tinc(h, dve)
            t.wait_ge(dve.h, n_cast)
            for nm in ("wq", "wk"):
                lsum = qlsum if nm == "wq" else klsum
                for m in range(KT):
                    pp = PS[m % 2]
                    for k in range(KT):
                        h = t.matmul(pp[:, 0:LM],
                                     wsb[nm][:, k * D + m * 128:
                                             k * D + m * 128 + 128],
                                     xlsb[:, k * LM:(k + 1) * LM],
                                     start=(k == 0), stop=(k == KT - 1))
                    n_pe = tinc(h, pe)
                    s.wait_ge(pe.h, n_pe)
                    h = s.activation(lsum[:, m * LM:(m + 1) * LM], pp[:, 0:LM],
                                     AF.Identity,
                                     bias=bscl64["b" + nm[1]][:, m:m + 1],
                                     scale=SCL4)
                    n_act = tinc(h, act)
                    t.wait_ge(act.h, n_act)
            v.wait_ge(act.h, n_act)

        h = v.tensor_scalar(qla[:], qlsum[:], 1.0 / segw, None, ALU.mult)
        chain(v, dve, h)
        h = v.tensor_scalar(kla[:], klsum[:], 1.0 / segw, None, ALU.mult)
        n_lm = tinc(h, dve)
        sy.wait_ge(dve.h, n_lm)
        sy.dma_start(out=qla_dr[:], in_=qla[:]).then_inc(ld.h, 16)
        ld.bump(16)
        h = sy.dma_start(out=kla_dr[:], in_=kla[:])
        n_lmd = tinc(h, ld, 16)
        sy.wait_ge(ld.h, n_lmd)
        sy.dma_start(out=qla2[:].rearrange("p (m b l) -> p m b l", m=KT, b=2),
                     in_=qla_dr[:].rearrange("(b p) (m l) -> p m b l", b=2, l=LM)
                     ).then_inc(ld.h, 16)
        ld.bump(16)
        h = sy.dma_start(out=kla2[:].rearrange("p (m b l) -> p m b l", m=KT, b=2),
                         in_=kla_dr[:].rearrange("(b p) (m l) -> p m b l", b=2, l=LM))
        n_lm2 = tinc(h, ld, 16)
        t.wait_ge(ld.h, n_lm2)
        s.wait_ge(ld.h, n_lm2)
        allsems1 = [ld, gld, pe, act, dve, stq, stx]
        emit_barrier(p, allsems1)
        p.ctx = _p1_scope
        p._depth = p._base_depth + 1
        p._sb_left = False
        p.pop_scope()

        def hq(buf, hh_):
            if buf is qla:
                buf = qla2
            elif buf is kla:
                buf = kla2
            return buf[0:64, hh_ * LM:(hh_ + 1) * LM]

        for hh_ in range(NH):
            h = t.matmul(PS[4][0:64, 0:64], hq(qla, hh_), hq(kla, hh_),
                         start=True, stop=True)
            n_pe = tinc(h, pe)
            v.wait_ge(pe.h, n_pe)
            h = v.tensor_reduce(nmax[:], PS[4][0:64, 0:64], AX.X, ALU.max,
                                negate=True)
            n_dve = tinc(h, dve)
            s.wait_ge(dve.h, n_dve)
            h = s.activation(k2e[:], PS[4][0:64, 0:64], AF.Exp, bias=nmax[:, 0:1],
                             accum_out=ssum[:, 0:1])
            n_act = tinc(h, act)
            t.wait_ge(act.h, n_act)
            v.wait_ge(act.h, n_act)
            h = v.reciprocal(srec[:], ssum[:])
            chain(v, dve, h)
            h = v.tensor_scalar(k2all[:, hh_ * 64:(hh_ + 1) * 64], k2e[:],
                                srec[:, 0:1], None, ALU.mult)
            n_dve = tinc(h, dve)
            t.wait_ge(dve.h, n_dve)
            h = t.matmul(PS[5][0:1, 0:64], ones_col[0:64, 0:1],
                         k2all[:, hh_ * 64:(hh_ + 1) * 64], start=True, stop=True)
            n_pe = tinc(h, pe)
            v.wait_ge(pe.h, n_pe)
            h = v.tensor_copy(cs1[:], PS[5][0:1, 0:64])
            chain(v, dve, h)
            h = v.tensor_reduce(cm16[0:1, hh_:hh_ + 1], cs1[:], AX.X, ALU.max)
            n_dve = tinc(h, dve)
            t.wait_ge(dve.h, n_dve)
        v.wait_ge(dve.h, dve.n)
        h = v.tensor_reduce(cmax1[:], cm16[:], AX.X, ALU.max)
        n_cm = tinc(h, dve)

        if colmax_only:
            sy.wait_ge(dve.h, n_cm)
            h = sy.dma_start(out=EE["cmax_out"][:], in_=cmax1[:])
            tinc(h, stx, 16)
            g.wait_ge(stx.h, stx.n)
            return nc

        gm8 = p.sb("gm8", [1, 8], F32)
        gmax = p.sb("gmax", [1, 1], F32)
        crec = p.sb("crec", [1, 1], F32)
        cb64 = p.sb("cb64", [64, 1], F32)
        sy.wait_ge(ld.h, ld.n)
        h = sy.dma_start(out=gm8[:], in_=EE["gmax8"][:])
        n_g8 = tinc(h, ld, 16)
        sy.wait_ge(ld.h, n_g8)
        v.wait_ge(ld.h, n_g8)
        h = v.tensor_reduce(gmax[:], gm8[:], AX.X, ALU.max)
        chain(v, dve, h)
        h = v.reciprocal(crec[:], gmax[:])
        n_dve = tinc(h, dve)
        t.wait_ge(dve.h, n_dve)
        h = t.matmul(PS[5][0:64, 0:1], ones_row[0:1, 0:64], crec[:],
                     start=True, stop=True)
        n_pe = tinc(h, pe)
        v.wait_ge(pe.h, n_pe)
        h = v.tensor_copy(cb64[:], PS[5][0:64, 0:1])
        n_cb = tinc(h, dve)
        t.wait_ge(dve.h, n_cb)

        _h_scope = p.push_scope()
        NHL = NH * 64
        k2T = p.sb("k2T", [64, NHL], F32)
        Vv = p.sb("Vv", [64, NHL], F32)
        VT = p.sb("VT", [64, NHL], F32)
        T1s = p.sb("T1s", [64, NHL], F32)
        T1Ts = p.sb("T1Ts", [64, NHL], F32)
        T3s = p.sb("T3s", [64, NHL], F32)
        T3Ts = p.sb("T3Ts", [64, NHL], F32)
        T5s = p.sb("T5s", [64, NHL], F32)
        T5Ts = p.sb("T5Ts", [64, NHL], F32)
        KVTs = p.sb("KVTs", [64, NHL], F32)
        Zall = p.sb("Zall", [64, NHL], F32)
        Uall = p.sb("Uall", [64, NHL], BF16)
        idrep = p.sb("idrep", [64, NHL], F32)
        h = v.tensor_copy(idrep[:].rearrange("p (h2 c) -> p h2 c", h2=NH),
                          id32[0:64, 0:64].rearrange("p (o c) -> p o c", o=1)
                          .broadcast_to([64, NH, 64]))
        n_idr = tinc(h, dve)

        def hsl64(buf, hh_):
            return buf[0:64, hh_ * 64:(hh_ + 1) * 64]

        t.wait_ge(dve.h, n_cb)
        for hh_ in range(NH):
            bank = PS[4 + (hh_ // 8) % 2]
            h = t.transpose(bank[0:64, (hh_ % 8) * 64:(hh_ % 8) * 64 + 64],
                            k2all[:, hh_ * 64:(hh_ + 1) * 64], id32[0:64, 0:64])
            n_pe = tinc(h, pe)
        v.wait_ge(pe.h, n_pe)
        v.wait_ge(dve.h, n_idr)
        h = v.tensor_copy(k2T[:, 0:512], PS[4][0:64, 0:512])
        chain(v, dve, h)
        h = v.tensor_copy(k2T[:, 512:1024], PS[5][0:64, 0:512])
        chain(v, dve, h)
        h = v.tensor_scalar(Vv[:], k2T[:], cb64[:, 0:1], None, ALU.mult)
        chain(v, dve, h)
        h = v.tensor_scalar(VT[:], k2all[:], cb64[:, 0:1], None, ALU.mult)
        n_dve = tinc(h, dve)
        t.wait_ge(dve.h, n_dve)

        for it in range(6):
            for hh_ in range(NH):
                ba = PS[4 + (hh_ // 8) % 2]
                h = t.matmul(ba[0:64, (hh_ % 8) * 64:(hh_ % 8) * 64 + 64],
                             hsl64(k2T, hh_), hsl64(Vv, hh_),
                             start=True, stop=True)
                tinc(h, pe)
            for hh_ in range(NH):
                bb = PS[2 + (hh_ // 8) % 2]
                h = t.matmul(bb[0:64, (hh_ % 8) * 64:(hh_ % 8) * 64 + 64],
                             hsl64(Vv, hh_), hsl64(k2T, hh_),
                             start=True, stop=True)
                n_pe = tinc(h, pe)
            v.wait_ge(pe.h, n_pe)
            h = v.scalar_tensor_tensor(T1s[:, 0:512], idrep[:, 0:512], 7.0,
                                       PS[4][0:64, 0:512], ALU.mult, ALU.subtract)
            chain(v, dve, h)
            h = v.scalar_tensor_tensor(T1s[:, 512:1024], idrep[:, 512:1024], 7.0,
                                       PS[5][0:64, 0:512], ALU.mult, ALU.subtract)
            chain(v, dve, h)
            h = v.scalar_tensor_tensor(T1Ts[:, 0:512], idrep[:, 0:512], 7.0,
                                       PS[2][0:64, 0:512], ALU.mult, ALU.subtract)
            chain(v, dve, h)
            h = v.scalar_tensor_tensor(T1Ts[:, 512:1024], idrep[:, 512:1024], 7.0,
                                       PS[3][0:64, 0:512], ALU.mult, ALU.subtract)
            chain(v, dve, h)
            h = v.tensor_copy(KVTs[:, 0:512], PS[2][0:64, 0:512])
            chain(v, dve, h)
            h = v.tensor_copy(KVTs[:, 512:1024], PS[3][0:64, 0:512])
            n_dve = tinc(h, dve)
            t.wait_ge(dve.h, n_dve)
            for hh_ in range(NH):
                ba = PS[4 + (hh_ // 8) % 2]
                h = t.matmul(ba[0:64, (hh_ % 8) * 64:(hh_ % 8) * 64 + 64],
                             hsl64(KVTs, hh_), hsl64(T1s, hh_),
                             start=True, stop=True)
                tinc(h, pe)
            for hh_ in range(NH):
                bb = PS[2 + (hh_ // 8) % 2]
                h = t.matmul(bb[0:64, (hh_ % 8) * 64:(hh_ % 8) * 64 + 64],
                             hsl64(T1s, hh_), hsl64(KVTs, hh_),
                             start=True, stop=True)
                n_pe = tinc(h, pe)
            v.wait_ge(pe.h, n_pe)
            h = v.scalar_tensor_tensor(T3s[:, 0:512], idrep[:, 0:512], 15.0,
                                       PS[4][0:64, 0:512], ALU.mult, ALU.subtract)
            chain(v, dve, h)
            h = v.scalar_tensor_tensor(T3s[:, 512:1024], idrep[:, 512:1024], 15.0,
                                       PS[5][0:64, 0:512], ALU.mult, ALU.subtract)
            chain(v, dve, h)
            h = v.scalar_tensor_tensor(T3Ts[:, 0:512], idrep[:, 0:512], 15.0,
                                       PS[2][0:64, 0:512], ALU.mult, ALU.subtract)
            chain(v, dve, h)
            h = v.scalar_tensor_tensor(T3Ts[:, 512:1024], idrep[:, 512:1024], 15.0,
                                       PS[3][0:64, 0:512], ALU.mult, ALU.subtract)
            n_dve = tinc(h, dve)
            t.wait_ge(dve.h, n_dve)
            for hh_ in range(NH):
                ba = PS[4 + (hh_ // 8) % 2]
                h = t.matmul(ba[0:64, (hh_ % 8) * 64:(hh_ % 8) * 64 + 64],
                             hsl64(KVTs, hh_), hsl64(T3s, hh_),
                             start=True, stop=True)
                tinc(h, pe)
            for hh_ in range(NH):
                bb = PS[2 + (hh_ // 8) % 2]
                h = t.matmul(bb[0:64, (hh_ % 8) * 64:(hh_ % 8) * 64 + 64],
                             hsl64(T3s, hh_), hsl64(KVTs, hh_),
                             start=True, stop=True)
                n_pe = tinc(h, pe)
            v.wait_ge(pe.h, n_pe)
            h = v.scalar_tensor_tensor(T5s[:, 0:512], idrep[:, 0:512], 13.0,
                                       PS[4][0:64, 0:512], ALU.mult, ALU.subtract)
            chain(v, dve, h)
            h = v.scalar_tensor_tensor(T5s[:, 512:1024], idrep[:, 512:1024], 13.0,
                                       PS[5][0:64, 0:512], ALU.mult, ALU.subtract)
            chain(v, dve, h)
            h = v.scalar_tensor_tensor(T5Ts[:, 0:512], idrep[:, 0:512], 13.0,
                                       PS[2][0:64, 0:512], ALU.mult, ALU.subtract)
            chain(v, dve, h)
            h = v.scalar_tensor_tensor(T5Ts[:, 512:1024], idrep[:, 512:1024], 13.0,
                                       PS[3][0:64, 0:512], ALU.mult, ALU.subtract)
            n_dve = tinc(h, dve)
            t.wait_ge(dve.h, n_dve)
            for hh_ in range(NH):
                ba = PS[4 + (hh_ // 8) % 2]
                h = t.matmul(ba[0:64, (hh_ % 8) * 64:(hh_ % 8) * 64 + 64],
                             hsl64(VT, hh_), hsl64(T5s, hh_),
                             start=True, stop=True)
                tinc(h, pe)
            for hh_ in range(NH):
                bb = PS[2 + (hh_ // 8) % 2]
                h = t.matmul(bb[0:64, (hh_ % 8) * 64:(hh_ % 8) * 64 + 64],
                             hsl64(T5Ts, hh_), hsl64(VT, hh_),
                             start=True, stop=True)
                n_pe = tinc(h, pe)
            v.wait_ge(pe.h, n_pe)
            h = v.tensor_scalar(Vv[:, 0:512], PS[4][0:64, 0:512], 0.25, None,
                                ALU.mult)
            chain(v, dve, h)
            h = v.tensor_scalar(Vv[:, 512:1024], PS[5][0:64, 0:512], 0.25, None,
                                ALU.mult)
            chain(v, dve, h)
            h = v.tensor_scalar(VT[:, 0:512], PS[2][0:64, 0:512], 0.25, None,
                                ALU.mult)
            chain(v, dve, h)
            h = v.tensor_scalar(VT[:, 512:1024], PS[3][0:64, 0:512], 0.25, None,
                                ALU.mult)
            n_dve = tinc(h, dve)
            t.wait_ge(dve.h, n_dve)

        khh2 = [p.sb(f"khh{i}", [64, Tfull], BF16) for i in range(2)]
        qhh2 = [p.sb(f"qhh{i}", [64, Tfull], BF16) for i in range(2)]
        vhh2 = [p.sb(f"vhh{i}", [128, NKT * HD], BF16) for i in range(2)]
        convsb2 = [p.sb(f"convsb{i}", [128, 3 * 128], BF16)
                   for i in range(2)]
        k3s = p.sb("k3s", [64, Tfull], F32)
        k3n = p.sb("k3n", [64, Tfull], BF16)
        k3T = p.sb("k3T", [128, NKT * HD], BF16)
        nmax3 = p.sb("nmax3", [64, 1], F32)
        k1e = p.sb("k1e", [128, 4 * 64], BF16)
        nmax1 = p.sb("nmax1", [128, 1], F32)
        ssum1 = p.sb("ssum1", [128, 4], F32)
        srec1 = p.sb("srec1", [128, 4], F32)
        k1T_sb = p.sb("k1T_sb", [64, 4 * 128], BF16)
        ctx16b = [p.sb(f"ctx16{i}", [64, 4 * 128], BF16) for i in range(2)]
        ctx_stq2 = {}

        def issue_h_loads(hh_):
            par = hh_ % 2
            h = sy.dma_start(out=khh2[par][:], in_=EE["kh"][64 * hh_:64 * hh_ + 64, :])
            tinc(h, hld2[par], 16)
            h = sy.dma_start(out=qhh2[par][:], in_=EE["qh"][64 * hh_:64 * hh_ + 64, :])
            tinc(h, hld2[par], 16)
            h = sy.dma_start(out=vhh2[par][:].rearrange("p (kt c2) -> p kt c2", c2=HD),
                             in_=EE["vtok"][:, 64 * hh_:64 * hh_ + 64]
                             .rearrange("(kt p) c2 -> p kt c2", p=128))
            n1 = tinc(h, hld2[par], 16)
            h = g.dma_start(out=convsb2[par][:].rearrange("p (d3 m) -> p d3 m",
                                                          d3=3),
                            in_=EE["convB"][hh_].rearrange("d3 p m -> p d3 m"))
            n2 = tinc(h, cvld2[par], 16)
            return (n1, n2)

        sy.wait_ge(stq.h, stq.n)
        h_ld = {0: issue_h_loads(0)}
        h_done = {}
        for hh_ in range(NH):
            par = hh_ % 2
            khh = khh2[par]
            qhh = qhh2[par]
            vhh = vhh2[par]
            convsb = convsb2[par]
            t.wait_ge(hld2[par].h, h_ld[hh_][0])
            t.wait_ge(cvld2[par].h, h_ld[hh_][1])
            s.wait_ge(hld2[par].h, h_ld[hh_][0])

            k3dv = []
            for sc in range(NSC):
                pp = PS[sc % 2]
                if sc >= 2:
                    t.wait_ge(dve.h, k3dv[sc - 2])
                h = t.matmul(pp[0:64, 0:512], hq(qla, hh_),
                             khh[:, sc * 512:(sc + 1) * 512],
                             start=True, stop=True)
                n_pe = tinc(h, pe)
                v.wait_ge(pe.h, n_pe)
                h = v.tensor_copy(k3s[:, sc * 512:(sc + 1) * 512], pp[0:64, 0:512])
                k3dv.append(tinc(h, dve))
            v.wait_ge(dve.h, k3dv[-1])
            h = v.tensor_reduce(nmax3[:], k3s[:], AX.X, ALU.max, negate=True)
            n_dve = tinc(h, dve)
            s.wait_ge(dve.h, n_dve)
            h = s.activation(k3n[:], k3s[:], AF.Exp, bias=nmax3[:, 0:1],
                             accum_out=ssum[:, 0:1])
            n_act = tinc(h, act)
            v.wait_ge(act.h, n_act)
            h = v.reciprocal(srec[:], ssum[:])
            chain(v, dve, h)
            t.wait_ge(act.h, n_act)
            tr_dve = []
            ngrp = (NKT + 7) // 8
            for grp in range(ngrp):
                bank = PB if grp % 2 == 0 else PB2
                gsz = min(8, NKT - grp * 8)
                if grp >= 2:
                    t.wait_ge(dve.h, tr_dve[grp - 2])
                for j in range(gsz):
                    kt = grp * 8 + j
                    h = t.transpose(bank[:, j * 64:(j + 1) * 64],
                                    k3n[:, kt * 128:(kt + 1) * 128],
                                    id16[0:64, 0:64])
                n_pe = tinc(h, pe)
                v.wait_ge(pe.h, n_pe)
                h = v.tensor_copy(k3T[:, grp * 512:grp * 512 + gsz * 64],
                                  bank[:, 0:gsz * 64])
                tr_dve.append(tinc(h, dve))
            t.wait_ge(dve.h, tr_dve[-1])
            zb = PS[2 + par]
            for kt in range(NKT):
                h = t.matmul(zb[0:64, 0:64], k3T[:, kt * HD:(kt + 1) * HD],
                             vhh[:, kt * HD:(kt + 1) * HD],
                             start=(kt == 0), stop=(kt == NKT - 1))
            n_pe = tinc(h, pe)
            v.wait_ge(pe.h, n_pe)
            h = v.tensor_scalar(hsl64(Zall, hh_), zb[0:64, 0:64],
                                srec[:, 0:1], None, ALU.mult)
            n_zdv = tinc(h, dve)
            t.wait_ge(dve.h, n_zdv)
            ub = PS[4 + par]
            h = t.matmul(ub[0:64, 0:64], hsl64(VT, hh_), hsl64(Zall, hh_),
                         start=True, stop=True)
            n_pe = tinc(h, pe)
            v.wait_ge(pe.h, n_pe)
            h = v.tensor_copy(hsl64(Uall, hh_), ub[0:64, 0:64])
            n_u = tinc(h, dve)
            t.wait_ge(dve.h, n_u)
            s.wait_ge(dve.h, n_u)

            NB4 = NKT // 4
            prev_ctx_dve = 0
            prev_s1 = []
            for qb in range(NB4):
                s1b = PS[0 + qb % 2]
                if prev_s1:
                    t.wait_ge(act.h, prev_s1[-1])
                if prev_ctx_dve:
                    t.wait_ge(dve.h, prev_ctx_dve)
                for j in range(4):
                    qt = qb * 4 + j
                    h = t.matmul(s1b[0:128, j * 64:(j + 1) * 64],
                                 qhh[:, qt * 128:(qt + 1) * 128],
                                 hq(kla, hh_), start=True, stop=True)
                n_pe = tinc(h, pe)
                v.wait_ge(pe.h, n_pe)
                h = v.tensor_reduce(nmax1[:], s1b[0:128, 0:256], AX.X, ALU.max,
                                    negate=True)
                n_dve = tinc(h, dve)
                s.wait_ge(dve.h, n_dve)
                h = s.activation(k1e[:], s1b[0:128, 0:256], AF.Exp,
                                 bias=nmax1[:, 0:1])
                n_act = tinc(h, act)
                prev_s1.append(n_act)
                v.wait_ge(act.h, n_act)
                h = v.tensor_reduce(ssum1[:], k1e[:]
                                    .rearrange("p (b c) -> p b c", b=4),
                                    AX.X, ALU.add)
                chain(v, dve, h)
                h = v.reciprocal(srec1[:], ssum1[:])
                chain(v, dve, h)
                for j in range(4):
                    h = v.tensor_scalar(k1e[:, j * 64:(j + 1) * 64],
                                        k1e[:, j * 64:(j + 1) * 64],
                                        srec1[:, j:j + 1], None, ALU.mult)
                n_dve = tinc(h, dve)
                t.wait_ge(dve.h, n_dve)
                bank = PB if qb % 2 == 0 else PB2
                for j in range(4):
                    h = t.transpose(bank[0:64, j * 128:(j + 1) * 128],
                                    k1e[:, j * 64:(j + 1) * 64], id16[:])
                n_pe = tinc(h, pe)
                v.wait_ge(pe.h, n_pe)
                h = v.tensor_copy(k1T_sb[:], bank[0:64, 0:512])
                n_dve = tinc(h, dve)
                t.wait_ge(dve.h, n_dve)
                cpar2 = qb % 2
                ctx16 = ctx16b[cpar2]
                ctx_dv = []
                for j in range(4):
                    qt = qb * 4 + j
                    cb_ = PS[qb % 2] if j % 2 == 0 else PS[2 + par]
                    if j >= 2 and ctx_dv:
                        t.wait_ge(dve.h, ctx_dv[j - 2])
                    elif prev_ctx_dve:
                        t.wait_ge(dve.h, prev_ctx_dve)
                    h = t.matmul(cb_[0:64, 0:128], hsl64(Uall, hh_),
                                 k1T_sb[:, j * 128:(j + 1) * 128],
                                 start=True, stop=False)
                    tinc(h, pe)
                    djs = [dj for dj, jj in enumerate((qt - 1, qt, qt + 1))
                           if 0 <= jj < NKT]
                    for dj in djs:
                        jj = qt - 1 + dj
                        h = t.matmul(cb_[0:64, 0:128],
                                     vhh[:, jj * HD:(jj + 1) * HD],
                                     convsb[:, dj * 128:(dj + 1) * 128],
                                     start=False, stop=(dj == djs[-1]))
                    n_pe = tinc(h, pe)
                    v.wait_ge(pe.h, n_pe)
                    if j == 0 and cpar2 in ctx_stq2:
                        v.wait_ge(stc2[cpar2].h, ctx_stq2[cpar2])
                    h = v.tensor_copy(ctx16[:, j * 128:(j + 1) * 128],
                                      cb_[0:64, 0:128])
                    ctx_dv.append(tinc(h, dve))
                prev_ctx_dve = ctx_dv[-1]
                sy.wait_ge(dve.h, prev_ctx_dve)
                h = sy.dma_start(
                    out=EE["ctxh"][64 * hh_:64 * hh_ + 64,
                                   qb * 512:(qb + 1) * 512],
                    in_=ctx16[:])
                ctx_stq2[cpar2] = tinc(h, stc2[cpar2], 16)
            h_done[hh_] = pe.n
            if hh_ + 1 < NH:
                if hh_ >= 1:
                    sy.wait_ge(pe.h, h_done[hh_ - 1])
                    g.wait_ge(pe.h, h_done[hh_ - 1])
                h_ld[hh_ + 1] = issue_h_loads(hh_ + 1)

        emit_barrier(p, [ld, gld, pe, act, dve, stq, stx,
                         cvld2[0], cvld2[1], hld2[0], hld2[1],
                         stc2[0], stc2[1]])
        p.ctx = _h_scope
        p._depth = p._base_depth + 1
        p.pop_scope()
        ccs2p = [p.sb("ccs", [128, KT * Tc], BF16),
                 p.sb("ccsB", [128, KT * Tc], BF16)]
        sx2a = [p.sem("sx3a"), p.sem("sx3b")]
        sy.wait_ge(stc2[0].h, stc2[0].n)
        sy.wait_ge(stc2[1].h, stc2[1].n)
        sy.wait_ge(stq.h, stq.n)

        def issue_x3a(c2):
            par2 = c2 % 2
            h2 = sy.dma_start(
                out=xc2a[par2][:].rearrange("p (k tc) -> p k tc", k=KT),
                in_=EE["xin"][:, c2 * Tc:(c2 + 1) * Tc]
                .rearrange("(k p) tc -> p k tc", p=128))
            tinc(h2, ld, 16)
            h2 = sy.dma_start(
                out=ccs2p[par2][:].rearrange("p (k tc) -> p k tc", k=KT),
                in_=EE["ctxh"][:, c2 * Tc:(c2 + 1) * Tc]
                .rearrange("(k p) tc -> p k tc", p=128))
            return tinc(h2, ld, 16)

        x3an = {0: issue_x3a(0)}
        for c in range(n_chunks):
            par = c % 2
            xc3 = xc2a[par]
            ccs = ccs2p[par]
            if c + 1 < n_chunks:
                if c >= 1:
                    sy.wait_ge(sx2a[(c + 1) % 2].h, sx2a[(c + 1) % 2].n)
                x3an[c + 1] = issue_x3a(c + 1)
            v.wait_ge(ld.h, x3an[c])
            for k in range(KT):
                h = v.tensor_add(xc3[:, k * Tc:(k + 1) * Tc],
                                 xc3[:, k * Tc:(k + 1) * Tc],
                                 ccs[:, k * Tc:(k + 1) * Tc])
                n_dve = chain(v, dve, h)
            sy.wait_ge(dve.h, n_dve)
            h = sy.dma_start(out=EE["xout"][:, c * Tc:(c + 1) * Tc]
                             .rearrange("(k p) tc -> p k tc", p=128),
                             in_=xc3[:].rearrange("p (k tc) -> p k tc", k=KT))
            tinc(h, sx2a[par], 16)

        g.wait_ge(sx2a[0].h, sx2a[0].n)
        g.wait_ge(sx2a[1].h, sx2a[1].n)
    return nc


def _prep_attn1_weights(Wq, bq, Wk, bk, Wv, bv, conv_w, Tfull=T):
    def colsplit(b):
        return np.ascontiguousarray(b.reshape(KT, 128).T)
    NH, KSZ = 16, 65
    convB = np.zeros((NH, 3, 128, 128), dtype=np.float32)
    for hh in range(NH):
        w = conv_w[hh, 0, :, 0]
        for dj in range(3):
            for pp_ in range(128):
                tap0 = (dj - 1) * 128 + pp_ + 32
                lo = max(0, tap0 - 64)
                hi = min(127, tap0)
                if lo > hi:
                    continue
                cs = np.arange(lo, hi + 1)
                convB[hh, dj, pp_, cs] = w[tap0 - cs]
    return dict(
        wq=np.ascontiguousarray(Wq), wk=np.ascontiguousarray(Wk),
        wv=np.ascontiguousarray(Wv),
        bq=colsplit(bq), bk=colsplit(bk), bv=colsplit(bv),
        convB=convB,
        idm=np.eye(128, dtype=np.float32), idm16=np.eye(128, dtype=BF))


_PROG_CACHE = {}


def build_launch1(Tfull=T):
    nc = bass.Bass("TRN2", target_bir_lowering=False)
    x0 = nc.dram_tensor("xin", [D, Tfull], F32, kind="ExternalInput").ap()
    xi1 = nc.dram_tensor("x1_i", [D, Tfull], F32, kind="Internal").ap()
    xi2 = nc.dram_tensor("x2_i", [D, Tfull], F32, kind="Internal").ap()
    xi3 = nc.dram_tensor("x3_i", [D, Tfull], F32, kind="Internal").ap()
    x4 = nc.dram_tensor("x4out", [D, Tfull], F32, kind="ExternalOutput").ap()
    cm = nc.dram_tensor("cmax_out", [1, 1], F32, kind="ExternalOutput").ap()
    wf0, wa0, wf1, wcm = {}, {}, {}, {}
    with ExitStack() as ctx:
        p = Prog(nc, ctx)
        FU = lambda pref, wc, io: dict(nc=nc, p=p, pref=pref, wcache=wc, io=io)
        Tc0 = min(1024, Tfull)
        build_ffn0(Tfull, Tc0, fuse=FU("f0_", wf0, {"xin": x0, "xout": xi1}))
        build_attn0(Tfull, 512, fuse=FU("a0_", wa0, {"xin": xi1, "xout": xi2}))
        build_ffn0(Tfull, Tc0, fuse=FU("f0_", wf0, {"xin": xi2, "xout": xi3}))
        build_ffn1(Tfull, 512, fuse=FU("f1_", wf1, {"xin": xi3, "xout": x4}))
        build_attn1(Tfull, colmax_only=True, cheap=True,
                    fuse=FU("cm_", wcm, {"xin": x4, "cmax_out": cm}))
        p.finalize()
    return nc


def build_launch2(Tfull=T):
    nc = bass.Bass("TRN2", target_bir_lowering=False)
    x4 = nc.dram_tensor("xin", [D, Tfull], F32, kind="ExternalInput").ap()
    xi5 = nc.dram_tensor("x5_i", [D, Tfull], F32, kind="Internal").ap()
    xout = nc.dram_tensor("xout", [D, Tfull], F32, kind="ExternalOutput").ap()
    wa1, wf1 = {}, {}
    with ExitStack() as ctx:
        p = Prog(nc, ctx)
        FU = lambda pref, wc, io: dict(nc=nc, p=p, pref=pref, wcache=wc, io=io)
        build_attn1(Tfull, colmax_only=False,
                    fuse=FU("a1_", wa1, {"xin": x4, "xout": xi5}))
        build_ffn1(Tfull, 512, fuse=FU("f1_", wf1, {"xin": xi5, "xout": xout}))
        p.finalize()
    return nc


def _progs():
    if not _PROG_CACHE:
        _PROG_CACHE["L1"] = build_launch1(Tfull=T)
        _PROG_CACHE["L2"] = build_launch2(Tfull=T)
    return _PROG_CACHE


def kernel(**inputs):
    ii = inputs
    cores = list(range(NCORES))
    progs = _progs()

    w12p, w3p, gwT, ind8 = _prep_ffn0_weights(
        np.asarray(ii["L0_ffn_w12"]), np.asarray(ii["L0_ffn_w3"]),
        np.asarray(ii["L0_ffn_gate_w"]))
    a0 = _prep_attn0_weights(
        np.asarray(ii["L0_attn_Wq"]), np.asarray(ii["L0_attn_bq"]),
        np.asarray(ii["L0_attn_Wk"]), np.asarray(ii["L0_attn_bk"]),
        np.asarray(ii["L0_attn_Wv"]), np.asarray(ii["L0_attn_bv"]),
        np.asarray(ii["L0_attn_Wo"]), np.asarray(ii["L0_attn_bo"]),
        np.asarray(ii["L0_attn_norm_q_w"]), np.asarray(ii["L0_attn_norm_k_w"]))
    f1 = _prep_ffn1_weights(
        np.asarray(ii["L1_ffn_enc_w"]), np.asarray(ii["L1_ffn_enc_b"]),
        np.asarray(ii["L1_ffn_s2c_w"]), np.asarray(ii["L1_ffn_s2c_b"]),
        np.asarray(ii["L1_ffn_r_w1"]), np.asarray(ii["L1_ffn_r_b1"]),
        np.asarray(ii["L1_ffn_r_w2"]), np.asarray(ii["L1_ffn_r_b2"]),
        np.asarray(ii["L1_ffn_e_w1"]), np.asarray(ii["L1_ffn_e_b1"]),
        np.asarray(ii["L1_ffn_e_w2"]), np.asarray(ii["L1_ffn_e_b2"]),
        np.asarray(ii["L1_ffn_c2s_w"]), np.asarray(ii["L1_ffn_c2s_b"]),
        np.asarray(ii["L1_ffn_dec_w"]), np.asarray(ii["L1_ffn_dec_b"]),
        np.asarray(ii["L1_ffn_ln_g"]), np.asarray(ii["L1_ffn_ln_b"]))
    a1 = _prep_attn1_weights(
        np.asarray(ii["L1_attn_Wq"]), np.asarray(ii["L1_attn_bq"]),
        np.asarray(ii["L1_attn_Wk"]), np.asarray(ii["L1_attn_bk"]),
        np.asarray(ii["L1_attn_Wv"]), np.asarray(ii["L1_attn_bv"]),
        np.asarray(ii["L1_attn_conv_w"]), Tfull=T)

    x = np.asarray(ii["L0_attn_x"], dtype=np.float32)
    xs = [np.ascontiguousarray(x[b].T) for b in cores]

    pfx = lambda pref, dd: {pref + k: v for k, v in dd.items()}
    f0w = dict(w12p=w12p, w3p=w3p, gwT=gwT, ind8=ind8, idm=a0["idm"])
    a1cm = {k: a1[k] for k in ("wq", "wk", "bq", "bk", "idm", "idm16")}

    l1maps = [dict(xin=xs[b], **pfx("f0_", f0w), **pfx("a0_", a0),
                   **pfx("f1_", f1), **pfx("cm_", a1cm)) for b in cores]
    res = run_bass_kernel_spmd(progs["L1"], l1maps, core_ids=cores)
    x4s = [np.asarray(r["x4out"]) for r in res.results]
    cms = [np.asarray(r["cmax_out"], np.float32) for r in res.results]
    gmax8 = np.ascontiguousarray(
        np.stack([c.reshape(1) for c in cms], axis=1).astype(np.float32))

    l2maps = [dict(xin=x4s[b], a1_gmax8=gmax8, **pfx("a1_", a1),
                   **pfx("f1_", f1)) for b in cores]
    res = run_bass_kernel_spmd(progs["L2"], l2maps, core_ids=cores)
    out = np.stack([np.ascontiguousarray(np.asarray(r["xout"]).T)
                    for r in res.results])
    return out.astype(np.float32)
